# version 02
# baseline (speedup 1.0000x reference)
import numpy as np
import ml_dtypes
import concourse.bass as bass
import concourse.mybir as mybir
from concourse.bass_utils import run_bass_kernel_spmd

F32 = mybir.dt.float32
BF16 = mybir.dt.bfloat16
AF = mybir.ActivationFunctionType
ALU = mybir.AluOpType
AX = mybir.AxisListType

T = 4096
D = 1024
NH = 16
DH = 64
NEG = -1.0e30
SB_LIMIT = 212800

ENG_COMPUTE_SEM = {"pe": "c_pe", "act": "c_act", "dve": "c_dve", "pool": "c_pool"}
ENG_QUEUE_SEM = {"sp": "q_sp", "pool": "q_pool", "act": "q_act"}
ALL_SEMS = ("c_pe", "c_act", "c_dve", "c_pool", "q_sp", "q_pool", "q_act")


class _Op:
    __slots__ = ("eng", "fn", "reads", "writes", "dma", "idx", "needs_inc",
                 "sem", "cnt", "waits", "vis", "tag", "deps", "bar_waits")

    def __init__(self, eng, fn, reads, writes, dma, idx, vis, tag):
        self.eng = eng
        self.fn = fn
        self.reads = tuple(reads)
        self.writes = tuple(writes)
        self.dma = dma
        self.idx = idx
        self.needs_inc = dma
        self.sem = None
        self.cnt = None
        self.waits = None
        self.vis = vis
        self.tag = tag
        self.deps = ()
        self.bar_waits = ()


PARANOID = [False]


class Prog:

    def __init__(self):
        self.ops = []
        self.barriers = []
        self._writer = {}
        self._readers = {}

    def op(self, eng, fn, reads=(), writes=(), dma=False, vis=True, tag=""):
        o = _Op(eng, fn, reads, writes, dma, len(self.ops), vis, tag)
        deps = set()
        for k in o.reads:
            w = self._writer.get(k)
            if w is not None:
                deps.add(w)
        for k in o.writes:
            w = self._writer.get(k)
            if w is not None:
                deps.add(w)
            for r in self._readers.get(k, ()):
                deps.add(r)
        deps.discard(o.idx)
        if PARANOID[0] and self.ops:
            deps.add(len(self.ops) - 1)
        real = []
        for d in deps:
            dop = self.ops[d]
            if ((not dop.dma) and dop.eng == eng == "pe" and not o.dma
                    and not PARANOID[0]):
                continue
            real.append(d)
        o.deps = tuple(real)
        for d in real:
            self.ops[d].needs_inc = True
        for k in o.reads:
            self._readers.setdefault(k, []).append(o.idx)
        for k in o.writes:
            self._writer[k] = o.idx
            self._readers[k] = []
        self.ops.append(o)
        return o

    def barrier(self, keep_prefix="W:"):
        self.barriers.append(len(self.ops))
        self._writer = {k: v for k, v in self._writer.items()
                        if k.startswith(keep_prefix)}
        self._readers = {k: v for k, v in self._readers.items()
                         if k.startswith(keep_prefix)}

    def finalize(self, nc, sems):
        counts = {name: 0 for name in sems}
        for o in self.ops:
            if o.dma:
                o.sem = ENG_QUEUE_SEM[o.eng]
                counts[o.sem] += 16
                o.cnt = counts[o.sem]
            elif o.needs_inc:
                o.sem = ENG_COMPUTE_SEM[o.eng]
                counts[o.sem] += 1
                o.cnt = counts[o.sem]

        bar_targets = []
        vis_counts = {name: 0 for name in sems}
        bi = 0
        for o in self.ops:
            while bi < len(self.barriers) and self.barriers[bi] == o.idx:
                bar_targets.append(dict(vis_counts))
                bi += 1
            if o.sem is not None and o.vis:
                vis_counts[o.sem] = o.cnt
        while bi < len(self.barriers):
            bar_targets.append(dict(vis_counts))
            bi += 1

        eng_clock = {e: {} for e in ("pe", "act", "dve", "pool", "sp")}
        post = [None] * len(self.ops)
        bar_done = [set() for _ in self.barriers]
        for o in self.ops:
            clock = eng_clock[o.eng]
            bw = []
            for b in range(len(self.barriers)):
                if self.barriers[b] <= o.idx and o.eng not in bar_done[b]:
                    bar_done[b].add(o.eng)
                    for s, c in bar_targets[b].items():
                        if c > clock.get(s, 0):
                            bw.append((s, c))
                            clock[s] = c
            o.bar_waits = tuple(bw)
            waits = []
            for d in sorted(o.deps):
                dop = self.ops[d]
                s, c = dop.sem, dop.cnt
                assert s is not None, (o.tag, dop.tag)
                if c > clock.get(s, 0):
                    waits.append((s, c))
                    clock[s] = c
                    for s2, c2 in post[d].items():
                        if c2 > clock.get(s2, 0):
                            clock[s2] = c2
            o.waits = tuple(waits)
            post[o.idx] = dict(clock)

        by_eng = {e: [] for e in ("pe", "act", "dve", "pool", "sp")}
        for o in self.ops:
            by_eng[o.eng].append(o)

        with nc.Block() as block:
            for eng_name, section in (("sp", block.sync), ("pool", block.gpsimd),
                                      ("pe", block.tensor), ("act", block.scalar),
                                      ("dve", block.vector)):
                def emit(eng, _eng=eng_name):
                    for o in by_eng[_eng]:
                        for s, c in o.bar_waits:
                            eng.wait_ge(sems[s], c)
                        for s, c in o.waits:
                            eng.wait_ge(sems[s], c)
                        inst = o.fn(eng)
                        if o.sem is not None:
                            assert inst is not None, o.tag
                            inst.then_inc(sems[o.sem], 16 if o.dma else 1)
                    for s in ALL_SEMS:
                        if counts[s] > 0:
                            eng.wait_ge(sems[s], counts[s])
                section(emit)
        return counts


class Salloc:

    _uid = [0]

    def __init__(self, nc, base=None, limit=None):
        self.nc = nc
        self.cur = nc.sbuf_base if base is None else base
        self.limit = nc.sbuf_top if limit is None else limit

    def alloc(self, name, shape, dtype, align=64):
        per_part = int(np.prod(shape[1:])) * mybir.dt.size(dtype)
        self.cur = (self.cur + align - 1) // align * align
        Salloc._uid[0] += 1
        t = self.nc.alloc_sbuf_tensor_at(f"{name}_{Salloc._uid[0]}", list(shape),
                                         dtype, offset=self.cur)
        self.cur += per_part
        assert self.cur <= self.limit, (name, shape, self.cur, self.limit)
        return t


def ceil_div(a, b):
    return (a + b - 1) // b


class MK:
    def __init__(self, debug_dump=()):
        self.nc = bass.Bass("TRN2", target_bir_lowering=False, debug=False)
        self.p = Prog()
        self.debug_dump = set(debug_dump)
        self._key = [0]
        self._sem_guards = []

    def K(self, pfx):
        self._key[0] += 1
        return f"{pfx}#{self._key[0]}"

    def declare_io(self):
        nc = self.nc

        def ein(name, shape, dtype=F32):
            return nc.dram_tensor(name, list(shape), dtype,
                                  kind="ExternalInput").ap()

        self.x_in = ein("x", [T, D])
        self.w_attn0 = ein("L0_attn_w_attn", [D, 3 * D])
        self.w_proj0 = ein("L0_attn_w_proj", [D, D])
        self.ws1 = ein("L0_ffn_Ws1", [D, 4096])
        self.bs1 = ein("L0_ffn_bs1", [4096])
        self.ws2 = ein("L0_ffn_Ws2", [2048, D])
        self.bs2 = ein("L0_ffn_bs2", [D])
        self.we1 = ein("L0_ffn_We1", [12, D, 6144])
        self.be1 = ein("L0_ffn_be1", [12, 6144])
        self.we2 = ein("L0_ffn_We2", [12, 3072, D])
        self.be2 = ein("L0_ffn_be2", [12, D])
        self.wr0 = ein("L0_ffn_Wr", [D, 12])
        self.br0 = ein("L0_ffn_br", [12])
        self.wq1 = ein("L1_attn_Wq", [NH, D, DH])
        self.wk1 = ein("L1_attn_Wk", [NH, D, DH])
        self.wv1 = ein("L1_attn_Wv", [NH, D, DH])
        self.wo1 = ein("L1_attn_Wo", [D, D])
        self.gamma1 = ein("L1_attn_gamma", [D])
        self.beta1 = ein("L1_attn_beta", [D])
        self.shup = ein("L1_ffn_shared_up", [1, D, 4096])
        self.shdn = ein("L1_ffn_shared_down", [1, 4096, D])
        self.roup = ein("L1_ffn_routed_up", [8, D, 4096])
        self.rodn = ein("L1_ffn_routed_down", [8, 4096, D])
        self.wr1 = ein("L1_ffn_router_w", [D, 8])

        self.out = nc.dram_tensor("out", [T, D], F32, kind="ExternalOutput").ap()

        def scratch(name, shape, dtype):
            kind = "ExternalOutput" if name in self.debug_dump else "Internal"
            return nc.dram_tensor(name, list(shape), dtype, kind=kind).ap()

        self.xres = scratch("xres", [T, D], F32)
        self.xln = scratch("xln", [T, D], BF16)
        self.heads = scratch("heads", [D, T], BF16)
        self.hop = scratch("hop", [2, T], BF16)
        self.watt_b = [scratch("watt0b", [D, 3 * D], BF16),
                       scratch("watt1b", [D, 3 * D], BF16)]
        self.wproj_b = [scratch("wproj0b", [D, D], BF16),
                        scratch("wproj1b", [D, D], BF16)]
        self.ws1_b = scratch("ws1b", [D, 4096], BF16)
        self.ws2_b = scratch("ws2b", [2048, D], BF16)
        self.we1_b = scratch("we1b", [12, D, 6144], BF16)
        self.we2_b = scratch("we2b", [12, 3072, D], BF16)
        self.shup_b = scratch("shupb", [D, 4096], BF16)
        self.shdn_b = scratch("shdnb", [4096, D], BF16)
        self.roup_b = scratch("roupb", [8, D, 4096], BF16)
        self.rodn_b = scratch("rodnb", [8, 4096, D], BF16)
        self.wr_b = [scratch("wr0b", [D, 12], BF16),
                     scratch("wr1b", [D, 8], BF16)]
        self.br0_b = scratch("br0b", [1, 12], BF16)
        self.bs2_b = scratch("bs2b", [1, D], BF16)
        self.be2_b = scratch("be2b", [12, D], BF16)

        idf = np.eye(128, dtype=np.float32)
        tri = np.where(np.arange(128)[:, None] >= np.arange(128)[None, :],
                       np.float32(0.0), np.float32(NEG))
        self.idf_c = nc.inline_tensor(idf, name="ident_f32")
        self.maskA_c = nc.inline_tensor(tri, name="maskA_t")
        self.maskB_c = nc.inline_tensor(np.ascontiguousarray(tri.T), name="maskB_t")
        self.ones_bf_c = nc.inline_tensor(
            np.ones((1, 512), dtype=ml_dtypes.bfloat16), name="ones_bf_t")
        self.ones_f_c = nc.inline_tensor(
            np.ones((1, 512), dtype=np.float32), name="ones_f_t")
        cv = np.tile(np.array([[1e-5, 1e-3, 0.0, 0.0]], dtype=np.float32),
                     (128, 1))
        self.cvals_c = nc.inline_tensor(cv, name="cvals_t")

        sal = Salloc(nc)
        self.identf = sal.alloc("identf", [128, 128], F32)
        self.maskA = sal.alloc("maskA", [128, 128], F32)
        self.maskB = sal.alloc("maskB", [128, 128], F32)
        self.ones_bf = sal.alloc("ones_bf", [1, 512], BF16)
        self.ones_f = sal.alloc("ones_f", [1, 512], F32)
        self.cvals = sal.alloc("cvals", [128, 4], F32)
        self.bias_s1 = sal.alloc("bias_s1", [128, 32], F32)
        self.bias_s2 = sal.alloc("bias_s2", [128, 8], F32)
        self.bias_e1 = sal.alloc("bias_e1", [128, 12, 48], F32)
        self.gam_bc = sal.alloc("gam_bc", [128, D], F32)
        self.bet_bc = sal.alloc("bet_bc", [128, D], F32)
        self.br_row = sal.alloc("br_row", [1, 12], BF16)
        self.bs2_row = sal.alloc("bs2_row", [1, D], BF16)
        self.be2_sb = sal.alloc("be2_sb", [12, D], BF16)
        self.wr_sb = [sal.alloc("wr0_sb", [128, 8, 12], BF16),
                      sal.alloc("wr1_sb", [128, 8, 8], BF16)]
        self.SCRATCH_BASE = sal.cur

        self.psw = [nc.alloc_psum_tensor(f"psw{i}", [128, 1024], F32)
                    for i in range(4)]
        self.psb = [self.psw[i // 2][:, (i % 2) * 512:(i % 2) * 512 + 512]
                    for i in range(8)]
        self.PS = [f"ps{i}" for i in range(8)]

        self.sems = {}
        for name in ALL_SEMS:
            g = nc.semaphore(name)
            self.sems[name] = g.__enter__()
            self._sem_guards.append(g)

    def dma(self, eng, dst, src, reads, writes, vis=True, tag=""):
        self.p.op(eng, lambda e, d=dst, s=src: e.dma_start(out=d, in_=s),
                  reads=reads, writes=writes, dma=True, vis=vis, tag=tag)

    def dma_t(self, dst, src, reads, writes, tag=""):
        self.p.op("sp", lambda e, d=dst, s=src: e.dma_start(out=d, in_=s,
                                                            transpose=True),
                  reads=reads, writes=writes, dma=True, tag=tag)

    def prepass(self):
        def cast(dst_ap, src_ap, key, tag):
            n = int(np.prod(src_ap.shape))
            s = src_ap.flatten() if len(src_ap.shape) > 1 else src_ap
            d = dst_ap.flatten() if len(dst_ap.shape) > 1 else dst_ap
            CH = 1 << 23
            nch = ceil_div(n, CH)
            for i in range(nch):
                lo, hi = i * CH, min(n, (i + 1) * CH)
                self.dma("pool", d[lo:hi], s[lo:hi], (),
                         (key,) if i == nch - 1 else (), vis=False, tag=tag)

        cast(self.watt_b[0], self.w_attn0, "W:watt0", "pp_watt0")
        cast(self.wproj_b[0], self.w_proj0, "W:wproj0", "pp_wproj0")
        cast(self.ws1_b, self.ws1, "W:ws1", "pp_ws1")
        cast(self.ws2_b, self.ws2, "W:ws2", "pp_ws2")
        cast(self.wr_b[0], self.wr0, "W:wr0", "pp_wr0")
        cast(self.br0_b, self.br0.unsqueeze(0), "W:br0", "pp_br0")
        cast(self.be2_b, self.be2, "W:be2", "pp_be2")
        cast(self.bs2_b, self.bs2.unsqueeze(0), "W:bs2", "pp_bs2")
        for e in range(12):
            cast(self.we1_b[e], self.we1[e], f"W:we1_{e}", f"pp_we1_{e}")
            cast(self.we2_b[e], self.we2[e], f"W:we2_{e}", f"pp_we2_{e}")
        for i, w in enumerate((self.wq1, self.wk1, self.wv1)):
            dst = self.watt_b[1][:, i * D:(i + 1) * D]
            dstv = dst.rearrange("d (h j) -> h d j", j=DH)
            for h in range(NH):
                self.dma("pool", dstv[h], w[h], (),
                         (f"W:watt1_{i}",) if h == NH - 1 else (), vis=False,
                         tag=f"pp_watt1_{i}_{h}")
        cast(self.wproj_b[1], self.wo1, "W:wproj1", "pp_wproj1")
        cast(self.wr_b[1], self.wr1, "W:wr1", "pp_wr1")
        cast(self.shup_b, self.shup[0], "W:shup", "pp_shup")
        cast(self.shdn_b, self.shdn[0], "W:shdn", "pp_shdn")
        for e in range(8):
            cast(self.roup_b[e], self.roup[e], f"W:roup_{e}", f"pp_roup_{e}")
            cast(self.rodn_b[e], self.rodn[e], f"W:rodn_{e}", f"pp_rodn_{e}")

    def consts(self):
        p, psb, PS = self.p, self.psb, self.PS
        self.dma("sp", self.identf[:], self.idf_c.ap(), (), ("identf",), tag="c_id")
        self.dma("sp", self.maskA[:], self.maskA_c.ap(), (), ("maskA",), tag="c_mA")
        self.dma("sp", self.maskB[:], self.maskB_c.ap(), (), ("maskB",), tag="c_mB")
        self.dma("sp", self.ones_bf[:], self.ones_bf_c.ap(), (), ("ones_bf",), tag="c_1b")
        self.dma("sp", self.ones_f[:], self.ones_f_c.ap(), (), ("ones_f",), tag="c_1f")
        self.dma("sp", self.cvals[:], self.cvals_c.ap(), (), ("cvals",), tag="c_cv")
        self.dma("sp", self.br_row[:], self.br0_b, ("W:br0",), ("br_row",), tag="c_br")
        self.dma("sp", self.bs2_row[:], self.bs2_b, ("W:bs2",), ("bs2_row",), tag="c_bs2r")
        self.dma("sp", self.be2_sb[:], self.be2_b, ("W:be2",), ("be2_sb",), tag="c_be2")
        for l in range(2):
            self.dma("sp", self.wr_sb[l][:],
                     self.wr_b[l].rearrange("(c p) e -> p c e", p=128),
                     (f"W:wr{l}",), (f"wr_sb{l}",), tag=f"c_wr{l}")

        sal = Salloc(self.nc, self.SCRATCH_BASE)
        bstage = sal.alloc("bstage", [64, 128], F32)

        def bias_prep(src_flat, n_rows, dst, dst_key, tag):
            assert n_rows <= 64
            self.dma("sp", bstage[0:n_rows, :],
                     src_flat.rearrange("(a b) -> a b", b=128),
                     ("c_bstage",), ("c_bstage",), tag=tag + "_ld")
            p.op("pe", lambda e, n=n_rows: e.transpose(
                psb[6][0:128, 0:n], bstage[0:n, :], self.identf[0:n, 0:n]),
                 reads=("c_bstage", "identf"), writes=(PS[6],), tag=tag + "_tr")
            p.op("dve", lambda e, d=dst, n=n_rows: e.tensor_copy(
                d, psb[6][0:128, 0:n]),
                 reads=(PS[6],), writes=(dst_key,), tag=tag + "_ev")

        bias_prep(self.bs1, 32, self.bias_s1[:], "bias_s1", "c_bs1")
        for e in range(12):
            bias_prep(self.be1[e], 48, self.bias_e1[:, e, :], f"bias_e1_{e}",
                      f"c_be1_{e}")

        gb = sal.alloc("gb_stage", [1, 2 * D], F32)
        self.dma("sp", gb[:, 0:D], self.gamma1.unsqueeze(0), (), ("gbs_g",), tag="c_g")
        self.dma("sp", gb[:, D:2 * D], self.beta1.unsqueeze(0), (), ("gbs_b",), tag="c_b")
        for i, (dst, key, rkey) in enumerate(((self.gam_bc, "gam_bc", "gbs_g"),
                                              (self.bet_bc, "bet_bc", "gbs_b"))):
            for c in range(2):
                sl = slice(i * D + c * 512, i * D + (c + 1) * 512)
                b = 4 + c
                p.op("pe", lambda e, sl=sl, b=b: e.matmul(
                    psb[b][:, 0:512], self.ones_f[0:1, 0:128], gb[:, sl],
                    start=True, stop=True),
                     reads=(rkey, "ones_f"), writes=(PS[b],), tag=f"c_gb{i}{c}")
                p.op("dve", lambda e, dst=dst, c=c, b=b: e.tensor_copy(
                    dst[:, c * 512:(c + 1) * 512], psb[b][:, 0:512]),
                     reads=(PS[b],), writes=(f"{key}{c}",), tag=f"c_gbe{i}{c}")

    def ln_tile_ops(self, tag, t, rx, rx_key, stats, xout, xout_key, eps_col,
                    extra_reads=()):
        p, K = self.p, self.K
        sq = self._ln_sq
        s1, s2 = stats[:, 0:1], stats[:, 1:2]
        mean, rr = stats[:, 2:3], stats[:, 3:4]
        negmr, tmp = stats[:, 4:5], stats[:, 5:6]
        ks = K(f"{tag}_stats{t}")
        p.op("act", lambda e: e.activation(out=sq[:], in_=rx[:], func=AF.Square,
                                           accum_out=s2),
             reads=(rx_key,) + tuple(extra_reads), writes=(ks + "s2", "_lnsq"),
             tag=f"{tag}_sq{t}")
        p.op("dve", lambda e: e.reduce_sum(s1, rx[:], axis=AX.X),
             reads=(rx_key,), writes=(ks + "s1",), tag=f"{tag}_s1{t}")
        p.op("dve", lambda e: e.tensor_scalar(out=mean, in0=s1, scalar1=1.0 / D,
                                              scalar2=None, op0=ALU.mult),
             reads=(ks + "s1",), writes=(ks + "mean",), tag=f"{tag}_mn{t}")
        p.op("dve", lambda e: e.tensor_mul(tmp, mean, mean),
             reads=(ks + "mean",), writes=(ks + "m2",), tag=f"{tag}_m2{t}")
        p.op("dve", lambda e: e.scalar_tensor_tensor(
            out=tmp, in0=s2, scalar=1.0 / D, in1=tmp,
            op0=ALU.mult, op1=ALU.subtract),
             reads=(ks + "s2", ks + "m2"), writes=(ks + "var",), tag=f"{tag}_va{t}")
        p.op("act", lambda e: e.activation(out=rr, in_=tmp, func=AF.Sqrt,
                                           bias=self.cvals[:, eps_col:eps_col + 1]),
             reads=(ks + "var", "cvals"), writes=(ks + "sd",), tag=f"{tag}_sd{t}")
        p.op("dve", lambda e: e.reciprocal(rr, rr),
             reads=(ks + "sd",), writes=(ks + "r",), tag=f"{tag}_r{t}")
        p.op("dve", lambda e: e.scalar_tensor_tensor(
            out=negmr, in0=mean, scalar=-1.0, in1=rr, op0=ALU.mult, op1=ALU.mult),
             reads=(ks + "mean", ks + "r"), writes=(ks + "nmr",), tag=f"{tag}_nm{t}")
        p.op("act", lambda e: e.activation(out=xout[:], in_=rx[:], func=AF.Identity,
                                           bias=negmr, scale=rr),
             reads=(rx_key, ks + "nmr", ks + "r"), writes=(xout_key,),
             tag=f"{tag}_ap{t}")

    def ln1(self):
        p = self.p
        sal = Salloc(self.nc, self.SCRATCH_BASE)
        rx = [sal.alloc("ln1_rx", [128, D], F32) for _ in range(3)]
        xo = [sal.alloc("ln1_xo", [128, D], BF16) for _ in range(3)]
        self._ln_sq = sal.alloc("ln1_sq", [128, D], BF16)
        stats = sal.alloc("ln1_stats", [128, 8 * 32], F32)
        for t in range(32):
            b, ob = rx[t % 3], xo[t % 3]
            kb, ko = f"ln1_rx{t % 3}", f"ln1_xo{t % 3}"
            self.dma("sp", b[:], self.x_in[t * 128:(t + 1) * 128, :],
                     (kb,), (kb,), tag=f"ln1_ld{t}")
            self.ln_tile_ops("ln1", t, b, kb, stats[:, t * 8:(t + 1) * 8],
                             ob, ko, 0)
            self.dma("sp", self.xln[t * 128:(t + 1) * 128, :], ob[:],
                     (ko,), (ko,), tag=f"ln1_st{t}")

    def attn(self, l):
        p, psb, PS, K = self.p, self.psb, self.PS, self.K
        nc = self.nc
        sal = Salloc(nc, self.SCRATCH_BASE)
        watt2 = [sal.alloc(f"at_watt{i}", [128, 8, 768], BF16) for i in range(2)]
        xlnf = sal.alloc("at_xlnf", [128, 8, T], BF16)
        qb = [sal.alloc(f"at_qb{i}", [66, T], BF16) for i in range(4)]
        kb = [sal.alloc(f"at_kb{i}", [66, T], BF16) for i in range(4)]
        vsb = sal.alloc("at_vsb", [128, 32, 256], BF16)
        mbuf = sal.alloc("at_mbuf", [128, 256], F32)
        zbuf = sal.alloc("at_zbuf", [128, 256], F32)
        cmb = sal.alloc("at_cmb", [128, 256], F32)
        st2 = sal.alloc("at_st2", [128, 4 * 32], F32)
        hilo = sal.alloc("at_hilo", [32, 256], BF16)
        dxp = sal.alloc("at_dxp", [128, 512], BF16)
        att = [sal.alloc(f"at_att{i}", [128, 512], BF16) for i in range(2)]
        hsb = [sal.alloc(f"at_hsb{i}", [64, 512], BF16) for i in range(2)]

        wb = self.watt_b[l]
        wkeys = (("W:watt0",) if l == 0 else
                 ("W:watt1_0", "W:watt1_1", "W:watt1_2"))

        for kc in range(8):
            self.dma_t(xlnf[:, kc, :], self.xln[:, kc * 128:(kc + 1) * 128],
                       (), (f"at_xlnf{kc}",), tag=f"at_xld{kc}")
        XLNF_ALL = tuple(f"at_xlnf{kc}" for kc in range(8))
        for i in range(4):
            p.op("pool", lambda e, i=i: e.memset(kb[i][64:66, :], 1.0),
                 reads=(), writes=(f"at_kones{i}",), tag=f"at_ko{i}")

        for g in range(4):
            wt = watt2[g % 2]
            WATT_ALL = tuple(f"at_w{g % 2}_{kc}" for kc in range(8))
            for kc in range(8):
                srcv = wb[kc * 128:(kc + 1) * 128, :].rearrange(
                    "p (s g c) -> p s g c", s=3, c=256)[:, :, g, :]
                self.dma("sp", wt[:, kc, :].rearrange("p (s c) -> p s c", c=256),
                         srcv, wkeys, (f"at_w{g % 2}_{kc}",),
                         tag=f"at_wld{g}_{kc}")
            for i in range(4):
                h = 4 * g + i
                for x, colbase, dstb, dstk in ((0, i * DH, qb[i], f"at_q{i}"),
                                               (1, 256 + i * DH, kb[i], f"at_k{i}")):
                    for tt in range(8):
                        b = 6 + (tt % 2)
                        def mmfn(e, cb=colbase, tt=tt, b=b, wt=wt):
                            inst = None
                            for kc in range(8):
                                inst = e.matmul(
                                    psb[b][0:64, 0:512],
                                    wt[:, kc, cb:cb + DH],
                                    xlnf[:, kc, tt * 512:(tt + 1) * 512],
                                    start=(kc == 0), stop=(kc == 7))
                            return inst
                        p.op("pe", mmfn, reads=WATT_ALL + XLNF_ALL,
                             writes=(PS[b],), tag=f"at_qk{g}{i}{x}{tt}")
                        if x == 0:
                            p.op("act", lambda e, i=i, tt=tt, b=b: e.activation(
                                out=qb[i][0:64, tt * 512:(tt + 1) * 512],
                                in_=psb[b][0:64, 0:512], func=AF.Identity,
                                scale=0.125),
                                 reads=(PS[b],), writes=(f"at_q{i}_{tt}",),
                                 tag=f"at_qe{g}{i}{tt}")
                        else:
                            p.op("dve", lambda e, i=i, tt=tt, b=b: e.tensor_copy(
                                kb[i][0:64, tt * 512:(tt + 1) * 512],
                                psb[b][0:64, 0:512]),
                                 reads=(PS[b],), writes=(f"at_k{i}_{tt}",),
                                 tag=f"at_ke{g}{i}{tt}")
            for ts in range(32):
                b = 2 + (ts % 2)
                def vfn(e, ts=ts, b=b, wt=wt):
                    inst = None
                    for kc in range(8):
                        inst = e.matmul(
                            psb[b][:, 0:256],
                            xlnf[:, kc, ts * 128:(ts + 1) * 128],
                            wt[:, kc, 512:768],
                            start=(kc == 0), stop=(kc == 7))
                    return inst
                p.op("pe", vfn, reads=WATT_ALL + XLNF_ALL, writes=(PS[b],),
                     tag=f"at_v{g}{ts}")
                p.op("dve", lambda e, ts=ts, b=b: e.tensor_copy(
                    vsb[:, ts, :], psb[b][:, 0:256]),
                     reads=(PS[b],), writes=(f"at_v{ts}",), tag=f"at_ve{g}{ts}")
            VSB_ALL = tuple(f"at_v{ts}" for ts in range(32))

            for i in range(4):
                h = 4 * g + i
                QROWS = tuple(f"at_q{i}_{tt}" for tt in range(8))
                KROWS = tuple(f"at_k{i}_{tt}" for tt in range(8))
                p.op("dve", lambda e: e.memset(mbuf[:], -NEG),
                     reads=(), writes=("at_mbuf",), tag=f"at_mi{h}")
                p.op("dve", lambda e: e.memset(zbuf[:], 0.0),
                     reads=(), writes=("at_zbuf",), tag=f"at_zi{h}")
                for qt in range(32):
                    kend = (qt + 1) * 128
                    nchunk = ceil_div(kend, 512)
                    for c in range(nchunk):
                        w = min(512, kend - c * 512)
                        b = 4 + (c % 2)
                        p.op("pe", lambda e, qt=qt, c=c, w=w, b=b, i=i: e.matmul(
                            psb[b][:, 0:w], qb[i][0:64, qt * 128:(qt + 1) * 128],
                            kb[i][0:64, c * 512:c * 512 + w],
                            start=True, stop=True),
                             reads=QROWS + KROWS, writes=(PS[b],),
                             tag=f"at_pa{h}_{qt}_{c}")
                        if c == nchunk - 1:
                            p.op("dve", lambda e, w=w, b=b: e.tensor_add(
                                psb[b][:, w - 128:w], psb[b][:, w - 128:w],
                                self.maskA[:]),
                                 reads=(PS[b], "maskA"), writes=(PS[b],),
                                 tag=f"at_pam{h}_{qt}")
                        col = qt * 8 + c
                        p.op("dve", lambda e, w=w, b=b, col=col: e.reduce_max(
                            mbuf[:, col:col + 1], psb[b][:, 0:w], axis=AX.X,
                            negate=True),
                             reads=(PS[b], "at_mbuf"), writes=(f"at_m{col}",),
                             tag=f"at_pam2{h}_{qt}_{c}")
                        p.op("act", lambda e, w=w, b=b, col=col: e.activation(
                            out=dxp[:, 0:w], in_=psb[b][:, 0:w], func=AF.Exp,
                            bias=mbuf[:, col:col + 1],
                            accum_out=zbuf[:, col:col + 1]),
                             reads=(PS[b], f"at_m{col}", "at_mbuf", "at_zbuf"),
                             writes=(f"at_z{col}", "at_dxp"),
                             tag=f"at_pae{h}_{qt}_{c}")
                MS = tuple(f"at_m{qt * 8 + c}" for qt in range(32)
                           for c in range(ceil_div((qt + 1) * 128, 512)))
                ZS = tuple(f"at_z{qt * 8 + c}" for qt in range(32)
                           for c in range(ceil_div((qt + 1) * 128, 512)))
                nM = st2[:, 0:32]
                lnS = st2[:, 32:64]
                negst = st2[:, 64:96]
                S = st2[:, 96:128]
                m3 = mbuf[:].rearrange("p (a b) -> p a b", a=32)
                z3 = zbuf[:].rearrange("p (a b) -> p a b", a=32)
                c3 = cmb[:].rearrange("p (a b) -> p a b", a=32)
                p.op("dve", lambda e: e.tensor_reduce(nM, m3, axis=AX.X,
                                                      op=ALU.min),
                     reads=MS + ("at_mbuf",), writes=("at_nM",), tag=f"at_cm1{h}")
                p.op("dve", lambda e: e.tensor_tensor(
                    out=c3, in0=nM.unsqueeze(2).broadcast_to([128, 32, 8]),
                    in1=m3, op=ALU.subtract),
                     reads=("at_nM", "at_mbuf"), writes=("at_cmb",),
                     tag=f"at_cm2{h}")
                p.op("act", lambda e: e.activation(out=cmb[:], in_=cmb[:],
                                                   func=AF.Exp),
                     reads=("at_cmb",), writes=("at_cmb",), tag=f"at_cm3{h}")
                p.op("dve", lambda e: e.tensor_mul(c3, c3, z3),
                     reads=("at_cmb", "at_zbuf") + ZS, writes=("at_cmb",),
                     tag=f"at_cm4{h}")
                p.op("dve", lambda e: e.tensor_reduce(S, c3, axis=AX.X,
                                                      op=ALU.add),
                     reads=("at_cmb",), writes=("at_S",), tag=f"at_cm5{h}")
                p.op("act", lambda e: e.activation(out=lnS, in_=S, func=AF.Ln),
                     reads=("at_S",), writes=("at_lnS",), tag=f"at_cm6{h}")
                p.op("dve", lambda e: e.tensor_sub(negst, nM, lnS),
                     reads=("at_nM", "at_lnS"), writes=("at_negst",),
                     tag=f"at_cm7{h}")
                p.op("pe", lambda e: e.transpose(psb[6][0:32, 0:128], negst,
                                                 self.identf[:]),
                     reads=("at_negst", "identf"), writes=(PS[6],),
                     tag=f"at_str{h}")
                p.op("dve", lambda e: e.tensor_copy(hilo[:, 0:128],
                                                    psb[6][0:32, 0:128]),
                     reads=(PS[6],), writes=("at_hi",), tag=f"at_hi{h}")
                p.op("dve", lambda e: e.tensor_sub(hilo[:, 128:256],
                                                   psb[6][0:32, 0:128],
                                                   hilo[:, 0:128]),
                     reads=(PS[6], "at_hi"), writes=("at_lo",), tag=f"at_lo{h}")
                self.dma("sp", self.hop[0:1, :], hilo[:, 0:128],
                         ("at_hi",), ("at_hop0",), tag=f"at_hd0{h}")
                self.dma("sp", self.hop[1:2, :], hilo[:, 128:256],
                         ("at_lo",), ("at_hop1",), tag=f"at_hd1{h}")
                self.dma("sp", qb[i][64:66, :], self.hop[:, :],
                         ("at_hop0", "at_hop1"), (f"at_qst{i}",),
                         tag=f"at_hl{h}")

            for i in range(4):
                h = 4 * g + i
                QROWS = tuple(f"at_q{i}_{tt}" for tt in range(8))
                KROWS = tuple(f"at_k{i}_{tt}" for tt in range(8))
                for qc in range(8):
                    avb = qc % 2
                    nkt = 4 * qc + 4
                    for kt in range(nkt):
                        b = 4 + (kt % 2)
                        a = kt % 2
                        p.op("pe", lambda e, i=i, kt=kt, qc=qc, b=b: e.matmul(
                            psb[b][:, 0:512], kb[i][0:66, kt * 128:(kt + 1) * 128],
                            qb[i][0:66, qc * 512:(qc + 1) * 512],
                            start=True, stop=True),
                             reads=KROWS + QROWS + (f"at_qst{i}", f"at_kones{i}"),
                             writes=(PS[b],), tag=f"at_pb{h}_{qc}_{kt}")
                        if kt // 4 == qc:
                            j0 = (kt % 4) * 128
                            p.op("dve", lambda e, b=b, j0=j0: e.tensor_add(
                                psb[b][:, j0:j0 + 128], psb[b][:, j0:j0 + 128],
                                self.maskB[:]),
                                 reads=(PS[b], "maskB"), writes=(PS[b],),
                                 tag=f"at_pbm{h}_{qc}_{kt}")
                            if j0 > 0:
                                p.op("dve", lambda e, a=a, j0=j0: e.memset(
                                    att[a][:, 0:j0], 0.0),
                                     reads=(), writes=(f"at_att{a}",),
                                     tag=f"at_pbz{h}_{qc}_{kt}")
                            p.op("act", lambda e, a=a, b=b, j0=j0: e.activation(
                                out=att[a][:, j0:512], in_=psb[b][:, j0:512],
                                func=AF.Exp),
                                 reads=(PS[b],), writes=(f"at_att{a}",),
                                 tag=f"at_pbe{h}_{qc}_{kt}")
                        else:
                            p.op("act", lambda e, a=a, b=b: e.activation(
                                out=att[a][:], in_=psb[b][:, 0:512],
                                func=AF.Exp),
                                 reads=(PS[b],), writes=(f"at_att{a}",),
                                 tag=f"at_pbe{h}_{qc}_{kt}")
                        p.op("pe", lambda e, i=i, kt=kt, a=a, avb=avb, s=(kt == 0), z=(kt == nkt - 1): e.matmul(
                            psb[avb][0:64, 0:512], vsb[:, kt, i * 64:(i + 1) * 64],
                            att[a][:], start=s, stop=z),
                             reads=(f"at_att{a}",) + VSB_ALL,
                             writes=(PS[avb],), tag=f"at_av{h}_{qc}_{kt}")
                    p.op("dve", lambda e, avb=avb: e.tensor_copy(
                        hsb[avb][:], psb[avb][0:64, 0:512]),
                         reads=(PS[avb],), writes=(f"at_hsb{avb}",),
                         tag=f"at_he{h}_{qc}")
                    self.dma("sp",
                             self.heads[h * 64:(h + 1) * 64,
                                        qc * 512:(qc + 1) * 512],
                             hsb[avb][:], (f"at_hsb{avb}",),
                             (f"at_hst{h}_{qc}",), tag=f"at_hs{h}_{qc}")

    def proj(self, l):
        p, psb, PS = self.p, self.psb, self.PS
        nc = self.nc
        sal = Salloc(nc, self.SCRATCH_BASE)
        wp = sal.alloc("pj_wp", [128, 8, D], BF16)
        hb = [sal.alloc(f"pj_hb{i}", [128, 8, 128], BF16) for i in range(2)]
        rx = [sal.alloc(f"pj_rx{i}", [128, D], F32) for i in range(2)]
        ub = [sal.alloc(f"pj_ub{i}", [128, D], BF16) for i in range(2)]
        yb = [sal.alloc(f"pj_yb{i}", [128, D], F32) for i in range(2)]
        y2 = [sal.alloc(f"pj_y2{i}", [128, D], F32) for i in range(2)]
        rnew = [sal.alloc(f"pj_rn{i}", [128, D], F32) for i in range(2)]
        xo = [sal.alloc(f"pj_xo{i}", [128, D], BF16) for i in range(2)]
        self._ln_sq = sal.alloc("pj_sq", [128, D], BF16)
        stats = sal.alloc("pj_stats", [128, 8 * 32], F32)
        stats2 = sal.alloc("pj_stats2", [128, 8 * 32], F32)

        for kc in range(8):
            self.dma("sp", wp[:, kc, :],
                     self.wproj_b[l][kc * 128:(kc + 1) * 128, :],
                     (f"W:wproj{l}",), (f"pj_wp{kc}",), tag=f"pj_wld{kc}")
        WP_ALL = tuple(f"pj_wp{kc}" for kc in range(8))
        resid_src = self.x_in if l == 0 else self.xres

        for ts in range(32):
            hbb, khb = hb[ts % 2], f"pj_hb{ts % 2}"
            self.dma("sp", hbb[:],
                     self.heads.rearrange("(c p) t -> p c t", p=128)[
                         :, :, ts * 128:(ts + 1) * 128],
                     (), (khb,), tag=f"pj_hld{ts}")
            rxb, krx = rx[ts % 2], f"pj_rx{ts % 2}"
            self.dma("sp", rxb[:], resid_src[ts * 128:(ts + 1) * 128, :],
                     (), (krx,), tag=f"pj_rld{ts}")

            def mmfn(e, hbb=hbb):
                inst = None
                for kc in range(8):
                    for half in range(2):
                        inst = e.matmul(
                            psb[half][:, 0:512], hbb[:, kc, :],
                            wp[:, kc, half * 512:(half + 1) * 512],
                            start=(kc == 0), stop=(kc == 7))
                return inst
            p.op("pe", mmfn, reads=(khb,) + WP_ALL, writes=(PS[0], PS[1]),
                 tag=f"pj_mm{ts}")

            rnb, krn = rnew[ts % 2], f"pj_rn{ts % 2}"
            if l == 0:
                for half in range(2):
                    p.op("dve", lambda e, half=half, rxb=rxb, rnb=rnb: e.tensor_add(
                        rnb[:, half * 512:(half + 1) * 512],
                        psb[half][:, 0:512],
                        rxb[:, half * 512:(half + 1) * 512]),
                         reads=(PS[half], krx), writes=(krn + f"h{half}",),
                         tag=f"pj_add{ts}_{half}")
                rn_r = (krn + "h0", krn + "h1")
            else:
                ubb, kub = ub[ts % 2], f"pj_ub{ts % 2}"
                self.dma("sp", ubb[:], self.xln[ts * 128:(ts + 1) * 128, :],
                         (), (kub,), tag=f"pj_uld{ts}")
                ybb, kyb = yb[ts % 2], f"pj_yb{ts % 2}"
                y2b, ky2 = y2[ts % 2], f"pj_y2{ts % 2}"
                for half in range(2):
                    p.op("dve", lambda e, half=half, ubb=ubb, ybb=ybb: e.tensor_add(
                        ybb[:, half * 512:(half + 1) * 512],
                        psb[half][:, 0:512],
                        ubb[:, half * 512:(half + 1) * 512]),
                         reads=(PS[half], kub, kyb), writes=(kyb,),
                         tag=f"pj_yadd{ts}_{half}")
                self.ln_tile_ops("pjI", ts, ybb, kyb,
                                 stats[:, ts * 8:ts * 8 + 8], y2b, ky2, 1)
                p.op("dve", lambda e, y2b=y2b: e.tensor_mul(
                    y2b[:], y2b[:], self.gam_bc[:]),
                     reads=(ky2, "gam_bc0", "gam_bc1"),
                     writes=(ky2,), tag=f"pj_gm{ts}")
                p.op("dve", lambda e, y2b=y2b: e.tensor_add(
                    y2b[:], y2b[:], self.bet_bc[:]),
                     reads=(ky2, "bet_bc0", "bet_bc1"),
                     writes=(ky2,), tag=f"pj_bt{ts}")
                p.op("dve", lambda e, rxb=rxb, rnb=rnb, y2b=y2b: e.tensor_add(
                    rnb[:], y2b[:], rxb[:]),
                     reads=(ky2, krx), writes=(krn + "h0",),
                     tag=f"pj_add{ts}")
                rn_r = (krn + "h0",)
            self.dma("sp", self.xres[ts * 128:(ts + 1) * 128, :], rnb[:],
                     rn_r, (f"pj_rst{ts}",), tag=f"pj_rs{ts}")
            xob, kxo = xo[ts % 2], f"pj_xo{ts % 2}"
            self.ln_tile_ops("pjL", ts, rnb, rn_r[0],
                             stats2[:, ts * 8:ts * 8 + 8], xob, kxo, 0,
                             extra_reads=rn_r[1:])
            self.dma("sp", self.xln[ts * 128:(ts + 1) * 128, :], xob[:],
                     (kxo,), (f"pj_xst{ts}",), tag=f"pj_xs{ts}")

    def ffn(self, l):
        p, psb, PS, K = self.p, self.psb, self.PS, self.K
        nc = self.nc
        E = 12 if l == 0 else 8
        HA = 3072 if l == 0 else 4096
        NJ = HA // 128
        sal = Salloc(nc, self.SCRATCH_BASE)
        xlnf = sal.alloc("ff_xlnf", [128, 8, 1024], BF16)
        act = sal.alloc("ff_act", [128, 32 if l else 24, 1024], BF16)
        yac = sal.alloc("ff_yac", [128, 8, D], F32)
        wup = [sal.alloc(f"ff_wup{i}", [128, 8, 1536 if l == 0 else 768], BF16)
               for i in range(2)]
        wdn = [sal.alloc(f"ff_wdn{i}", [128, 4, D], BF16) for i in range(2)]
        gsb = sal.alloc("ff_gsb", [128, 2, 512], BF16)
        lgb = sal.alloc("ff_lgb", [128, 8, E], F32)
        gts = sal.alloc("ff_gts", [128, 8, E], F32)
        gsc = sal.alloc("ff_gsc", [128, 8 * 8], F32)
        gfm = sal.alloc("ff_gfm", [12, 1024], BF16) if l == 0 else None
        rx = [sal.alloc(f"ff_rx{i}", [128, D], F32) for i in range(2)]
        ub = [sal.alloc(f"ff_ub{i}", [128, D], BF16) for i in range(2)] \
            if l == 1 else [None, None]
        rn = [sal.alloc(f"ff_rn{i}", [128, D], F32) for i in range(2)]
        xo = [sal.alloc(f"ff_xo{i}", [128, D], BF16) for i in range(2)] \
            if l == 0 else [None, None]
        if l == 0:
            self._ln_sq = sal.alloc("ff_sq", [128, D], BF16)
            stats = sal.alloc("ff_stats", [128, 8 * 32], F32)
        else:
            stats = None

        if l == 0:
            jobs = [("d", self.ws1_b, self.ws2_b, 4096, 2048, None, "s1")]
            jobs += [(f"e{e}", self.we1_b[e], self.we2_b[e], 6144, 3072, e,
                      f"e1_{e}") for e in range(12)]
            wk_up = {"d": ("W:ws1",)}
            wk_dn = {"d": ("W:ws2",)}
            for e in range(12):
                wk_up[f"e{e}"] = (f"W:we1_{e}",)
                wk_dn[f"e{e}"] = (f"W:we2_{e}",)
            geglu = True
        else:
            jobs = [("d", self.shup_b, self.shdn_b, 4096, 4096, None, None)]
            jobs += [(f"e{e}", self.roup_b[e], self.rodn_b[e], 4096, 4096, e,
                      None) for e in range(8)]
            wk_up = {"d": ("W:shup",)}
            wk_dn = {"d": ("W:shdn",)}
            for e in range(8):
                wk_up[f"e{e}"] = (f"W:roup_{e}",)
                wk_dn[f"e{e}"] = (f"W:rodn_{e}",)
            geglu = False

        for tt in range(4):
            t0 = tt * 1024
            for kc in range(8):
                self.dma_t(xlnf[:, kc, :],
                           self.xln[t0:t0 + 1024, kc * 128:(kc + 1) * 128],
                           (f"ff_xlnf{kc}",), (f"ff_xlnf{kc}",),
                           tag=f"ff_xld{tt}_{kc}")
            XLNF = tuple(f"ff_xlnf{kc}" for kc in range(8))

            for ts in range(8):
                def lgfn(e, ts=ts, l=l):
                    inst = None
                    for kc in range(8):
                        inst = e.matmul(
                            psb[6][:, 0:E],
                            xlnf[:, kc, ts * 128:(ts + 1) * 128],
                            self.wr_sb[l][:, kc, :],
                            start=(kc == 0), stop=(kc == 7 and l != 0))
                    if l == 0:
                        inst = e.matmul(psb[6][:, 0:E], self.ones_bf[0:1, 0:128],
                                        self.br_row[:], start=False, stop=True)
                    return inst
                p.op("pe", lgfn, reads=XLNF + (f"wr_sb{l}", "ones_bf", "br_row"),
                     writes=(PS[6],), tag=f"ff_lg{tt}_{ts}")
                p.op("act", lambda e, ts=ts: e.activation(
                    out=lgb[:, ts, :], in_=psb[6][:, 0:E], func=AF.Identity,
                    scale=(1.0 / 1.5) if l == 0 else 1.0),
                     reads=(PS[6],), writes=(f"ff_lg{ts}",), tag=f"ff_lge{tt}_{ts}")
            LG = tuple(f"ff_lg{ts}" for ts in range(8))
            v1 = gsc[:, 0:8]
            v2 = gsc[:, 8:16]
            w1 = gsc[:, 16:24]
            w2 = gsc[:, 24:32]
            ssum = gsc[:, 32:40]
            if l == 0:
                p.op("dve", lambda e: e.tensor_reduce(
                    v1, lgb[:], axis=AX.X, op=ALU.max),
                     reads=LG, writes=("ff_v1",), tag=f"ff_r1{tt}")
                p.op("dve", lambda e: e.tensor_tensor(
                    out=gts[:], in0=lgb[:],
                    in1=v1.unsqueeze(2).broadcast_to([128, 8, E]),
                    op=ALU.is_equal),
                     reads=LG + ("ff_v1",), writes=("ff_m1",), tag=f"ff_m1{tt}")
                p.op("dve", lambda e: e.scalar_tensor_tensor(
                    out=lgb[:], in0=gts[:], scalar=NEG, in1=lgb[:],
                    op0=ALU.mult, op1=ALU.add),
                     reads=("ff_m1",) + LG, writes=("ff_msk",), tag=f"ff_mk{tt}")
                p.op("dve", lambda e: e.tensor_reduce(
                    v2, lgb[:], axis=AX.X, op=ALU.max),
                     reads=("ff_msk",), writes=("ff_v2",), tag=f"ff_r2{tt}")
                p.op("dve", lambda e: e.tensor_sub(w1, v1, v2),
                     reads=("ff_v1", "ff_v2"), writes=("ff_d12",), tag=f"ff_d{tt}")
                p.op("act", lambda e: e.activation(out=w1, in_=w1, func=AF.Sigmoid),
                     reads=("ff_d12",), writes=("ff_w1",), tag=f"ff_s1{tt}")
                p.op("dve", lambda e: e.tensor_scalar(
                    out=w2, in0=w1, scalar1=-1.0, scalar2=1.0,
                    op0=ALU.mult, op1=ALU.add),
                     reads=("ff_w1",), writes=("ff_w2",), tag=f"ff_s2{tt}")
                p.op("dve", lambda e: e.tensor_tensor(
                    out=lgb[:], in0=lgb[:],
                    in1=v2.unsqueeze(2).broadcast_to([128, 8, E]),
                    op=ALU.is_equal),
                     reads=("ff_msk", "ff_v2"), writes=("ff_m2",), tag=f"ff_m2{tt}")
                p.op("dve", lambda e: e.tensor_mul(
                    gts[:], gts[:],
                    w1.unsqueeze(2).broadcast_to([128, 8, E])),
                     reads=("ff_m1", "ff_w1"), writes=("ff_g1",), tag=f"ff_g1{tt}")
                p.op("dve", lambda e: e.tensor_mul(
                    lgb[:], lgb[:],
                    w2.unsqueeze(2).broadcast_to([128, 8, E])),
                     reads=("ff_m2", "ff_w2"), writes=("ff_g2",), tag=f"ff_g2{tt}")
                p.op("dve", lambda e: e.tensor_add(gts[:], gts[:], lgb[:]),
                     reads=("ff_g1", "ff_g2"), writes=("ff_gts",), tag=f"ff_gs{tt}")
                for ts in range(8):
                    p.op("pe", lambda e, ts=ts: e.transpose(
                        psb[7][0:E, 0:128], gts[:, ts, :], self.identf[:]),
                         reads=("ff_gts", "identf"), writes=(PS[7],),
                         tag=f"ff_gt{tt}_{ts}")
                    p.op("dve", lambda e, ts=ts: e.tensor_copy(
                        gfm[:, ts * 128:(ts + 1) * 128], psb[7][0:E, 0:128]),
                         reads=(PS[7],), writes=("ff_gfm",), tag=f"ff_gf{tt}_{ts}")
            else:
                p.op("dve", lambda e: e.tensor_reduce(
                    v1, lgb[:], axis=AX.X, op=ALU.max),
                     reads=LG, writes=("ff_v1",), tag=f"ff_r1{tt}")
                p.op("dve", lambda e: e.tensor_tensor(
                    out=lgb[:], in0=lgb[:],
                    in1=v1.unsqueeze(2).broadcast_to([128, 8, E]),
                    op=ALU.subtract),
                     reads=LG + ("ff_v1",), writes=("ff_sh",), tag=f"ff_sh{tt}")
                p.op("act", lambda e: e.activation(out=lgb[:], in_=lgb[:],
                                                   func=AF.Exp),
                     reads=("ff_sh",), writes=("ff_ex",), tag=f"ff_ex{tt}")
                p.op("dve", lambda e: e.tensor_reduce(
                    ssum, lgb[:], axis=AX.X, op=ALU.add),
                     reads=("ff_ex",), writes=("ff_ss",), tag=f"ff_ss{tt}")
                p.op("dve", lambda e: e.reciprocal(ssum, ssum),
                     reads=("ff_ss",), writes=("ff_rs",), tag=f"ff_rs{tt}")
                p.op("dve", lambda e: e.tensor_mul(
                    lgb[:], lgb[:], ssum.unsqueeze(2).broadcast_to([128, 8, E])),
                     reads=("ff_ex", "ff_rs"), writes=("ff_pr",), tag=f"ff_pr{tt}")
                p.op("dve", lambda e: e.tensor_reduce(
                    v1, lgb[:], axis=AX.X, op=ALU.max),
                     reads=("ff_pr",), writes=("ff_p1",), tag=f"ff_p1{tt}")
                p.op("dve", lambda e: e.tensor_tensor(
                    out=gts[:], in0=lgb[:],
                    in1=v1.unsqueeze(2).broadcast_to([128, 8, E]),
                    op=ALU.is_equal),
                     reads=("ff_pr", "ff_p1"), writes=("ff_m1",), tag=f"ff_m1{tt}")
                p.op("dve", lambda e: e.scalar_tensor_tensor(
                    out=lgb[:], in0=gts[:], scalar=-2.0, in1=lgb[:],
                    op0=ALU.mult, op1=ALU.add),
                     reads=("ff_m1", "ff_pr"), writes=("ff_msk",), tag=f"ff_mk{tt}")
                p.op("dve", lambda e: e.tensor_reduce(
                    v2, lgb[:], axis=AX.X, op=ALU.max),
                     reads=("ff_msk",), writes=("ff_v2",), tag=f"ff_r2{tt}")
                p.op("dve", lambda e: e.tensor_tensor(
                    out=lgb[:], in0=lgb[:],
                    in1=v2.unsqueeze(2).broadcast_to([128, 8, E]),
                    op=ALU.is_equal),
                     reads=("ff_msk", "ff_v2"), writes=("ff_m2",), tag=f"ff_m2{tt}")
                p.op("dve", lambda e: e.tensor_mul(
                    gts[:], gts[:], v1.unsqueeze(2).broadcast_to([128, 8, E])),
                     reads=("ff_m1", "ff_p1"), writes=("ff_g1",), tag=f"ff_g1{tt}")
                p.op("dve", lambda e: e.tensor_mul(
                    lgb[:], lgb[:], v2.unsqueeze(2).broadcast_to([128, 8, E])),
                     reads=("ff_m2", "ff_v2"), writes=("ff_g2",), tag=f"ff_g2{tt}")
                p.op("dve", lambda e: e.tensor_add(gts[:], gts[:], lgb[:]),
                     reads=("ff_g1", "ff_g2"), writes=("ff_gts",), tag=f"ff_gs{tt}")

            p.op("dve", lambda e: e.memset(yac[:], 0.0),
                 reads=(), writes=("ff_yac",), tag=f"ff_yz{tt}")

            for (name, wu, wd, HB, HAe, gate_e, bias) in jobs:
                nj = HAe // 128
                if geglu:
                    nband = ceil_div(nj, 6)
                    for b in range(nband):
                        j0 = b * 6
                        j1 = min(nj, j0 + 6)
                        wbuf, wkb = wup[b % 2], f"ff_wup{b % 2}"
                        for kc in range(8):
                            srcv = wu[kc * 128:(kc + 1) * 128, :].rearrange(
                                "p (s c) -> p s c", s=2)[
                                :, :, j0 * 128:j1 * 128]
                            self.dma("sp",
                                     wbuf[:, kc, 0:2 * (j1 - j0) * 128].rearrange(
                                         "p (s c) -> p s c", s=2),
                                     srcv, wk_up[name] + (wkb,), (wkb,),
                                     tag=f"ff_wul{tt}{name}{b}_{kc}")
                        for j in range(j0, j1):
                            jj = j - j0
                            bs = (j % 2) * 4
                            for ab in range(2):
                                def upfn(e, jj=jj, ab=ab, bs=bs, wbuf=wbuf,
                                         nw=(j1 - j0)):
                                    inst = None
                                    for kc in range(8):
                                        lhs = wbuf[:, kc,
                                                   ab * nw * 128 + jj * 128:
                                                   ab * nw * 128 + (jj + 1) * 128]
                                        for tc in range(2):
                                            inst = e.matmul(
                                                psb[bs + 2 * ab + tc][:, 0:512],
                                                lhs,
                                                xlnf[:, kc,
                                                     tc * 512:(tc + 1) * 512],
                                                start=(kc == 0), stop=(kc == 7))
                                    return inst
                                p.op("pe", upfn,
                                     reads=(wkb,) + XLNF,
                                     writes=(PS[bs + 2 * ab], PS[bs + 2 * ab + 1]),
                                     tag=f"ff_up{tt}{name}_{j}_{ab}")
                            if bias == "s1":
                                bcol_a = self.bias_s1[:, j:j + 1]
                                bcol_g = self.bias_s1[:, 16 + j:17 + j]
                                brd = ("bias_s1",)
                            else:
                                ei = int(bias[3:])
                                bcol_a = self.bias_e1[:, ei, j:j + 1]
                                bcol_g = self.bias_e1[:, ei, 24 + j:25 + j]
                                brd = (f"bias_{bias}",)
                            for tc in range(2):
                                p.op("act", lambda e, bs=bs, tc=tc, bg=bcol_g: e.activation(
                                    out=gsb[:, tc, :], in_=psb[bs + 2 + tc][:, 0:512],
                                    func=AF.Gelu, bias=bg),
                                     reads=(PS[bs + 2 + tc],) + brd,
                                     writes=(f"ff_gsb{tc}",),
                                     tag=f"ff_ge{tt}{name}_{j}_{tc}")
                                p.op("dve", lambda e, bs=bs, tc=tc, j=j, ba=bcol_a: e.scalar_tensor_tensor(
                                    out=act[:, j, tc * 512:(tc + 1) * 512],
                                    in0=psb[bs + tc][:, 0:512], scalar=ba,
                                    in1=gsb[:, tc, :], op0=ALU.add, op1=ALU.mult),
                                     reads=(PS[bs + tc], f"ff_gsb{tc}") + brd,
                                     writes=(f"ff_act{j}",),
                                     tag=f"ff_ae{tt}{name}_{j}_{tc}")
                else:
                    nband = ceil_div(nj, 6)
                    for b in range(nband):
                        j0 = b * 6
                        j1 = min(nj, j0 + 6)
                        wbuf, wkb = wup[b % 2], f"ff_wup{b % 2}"
                        for kc in range(8):
                            self.dma("sp",
                                     wbuf[:, kc, 0:(j1 - j0) * 128],
                                     wu[kc * 128:(kc + 1) * 128,
                                        j0 * 128:j1 * 128],
                                     wk_up[name] + (wkb,), (wkb,),
                                     tag=f"ff_wul{tt}{name}{b}_{kc}")
                        for j in range(j0, j1):
                            jj = j - j0
                            bs = (j % 4) * 2
                            def upfn(e, jj=jj, bs=bs, wbuf=wbuf):
                                inst = None
                                for kc in range(8):
                                    lhs = wbuf[:, kc, jj * 128:(jj + 1) * 128]
                                    for tc in range(2):
                                        inst = e.matmul(
                                            psb[bs + tc][:, 0:512], lhs,
                                            xlnf[:, kc, tc * 512:(tc + 1) * 512],
                                            start=(kc == 0), stop=(kc == 7))
                                return inst
                            p.op("pe", upfn, reads=(wkb,) + XLNF,
                                 writes=(PS[bs], PS[bs + 1]),
                                 tag=f"ff_up{tt}{name}_{j}")
                            for tc in range(2):
                                p.op("act", lambda e, bs=bs, tc=tc, j=j: e.activation(
                                    out=act[:, j, tc * 512:(tc + 1) * 512],
                                    in_=psb[bs + tc][:, 0:512], func=AF.Gelu),
                                     reads=(PS[bs + tc],),
                                     writes=(f"ff_act{j}",),
                                     tag=f"ff_ae{tt}{name}_{j}_{tc}")
                ACT_ALL = tuple(f"ff_act{j}" for j in range(nj))

                for half in range(2):
                    for kc in range(nj):
                        if kc % 4 == 0:
                            wb_, wkd = wdn[(kc // 4) % 2], f"ff_wdn{(kc // 4) % 2}"
                            self.dma("sp",
                                     wb_[:, 0:4, :],
                                     wd[kc * 128:(kc + 4) * 128, :].rearrange(
                                         "(k p) d -> p k d", p=128),
                                     wk_dn[name] + (wkd,), (wkd,),
                                     tag=f"ff_wdl{tt}{name}{half}_{kc}")
                        wb_, wkd = wdn[(kc // 4) % 2], f"ff_wdn{(kc // 4) % 2}"
                        first = (kc == 0)
                        last = (kc == nj - 1)
                        for tsub in range(4):
                            tg = half * 4 + tsub
                            def dnfn(e, kc=kc, tsub=tsub, tg=tg, wb_=wb_,
                                     first=first, last=last, name=name):
                                inst = None
                                if first and name == "d" and l == 0:
                                    inst = e.matmul(
                                        psb[2 * tsub][:, 0:512],
                                        self.ones_bf[0:1, 0:128],
                                        self.bs2_row[:, 0:512],
                                        start=True, stop=False)
                                    inst = e.matmul(
                                        psb[2 * tsub + 1][:, 0:512],
                                        self.ones_bf[0:1, 0:128],
                                        self.bs2_row[:, 512:1024],
                                        start=True, stop=False)
                                    inst = e.matmul(
                                        psb[2 * tsub][:, 0:512],
                                        gfm[:, tg * 128:(tg + 1) * 128],
                                        self.be2_sb[:, 0:512],
                                        start=False, stop=False)
                                    inst = e.matmul(
                                        psb[2 * tsub + 1][:, 0:512],
                                        gfm[:, tg * 128:(tg + 1) * 128],
                                        self.be2_sb[:, 512:1024],
                                        start=False, stop=False)
                                for dh in range(2):
                                    inst = e.matmul(
                                        psb[2 * tsub + dh][:, 0:512],
                                        act[:, kc, tg * 128:(tg + 1) * 128],
                                        wb_[:, kc % 4, dh * 512:(dh + 1) * 512],
                                        start=(first and not (name == "d" and l == 0)),
                                        stop=last)
                                return inst
                            extra = ("ones_bf", "bs2_row", "be2_sb", "ff_gfm") \
                                if (first and name == "d" and l == 0) else ()
                            p.op("pe", dnfn,
                                 reads=(wkd, f"ff_act{kc}") + extra,
                                 writes=(PS[2 * tsub], PS[2 * tsub + 1]),
                                 tag=f"ff_dn{tt}{name}_{half}_{kc}_{tsub}")
                    for tsub in range(4):
                        tg = half * 4 + tsub
                        for dh in range(2):
                            bk = 2 * tsub + dh
                            dst = yac[:, tg, dh * 512:(dh + 1) * 512]
                            if gate_e is None:
                                p.op("dve", lambda e, dst=dst, bk=bk: e.tensor_tensor(
                                    out=dst, in0=dst, in1=psb[bk][:, 0:512],
                                    op=ALU.add),
                                     reads=(PS[bk], "ff_yac"), writes=("ff_yac",),
                                     tag=f"ff_ev{tt}{name}_{tg}_{dh}")
                            else:
                                gcol = gts[:, tg, gate_e:gate_e + 1]
                                p.op("dve", lambda e, dst=dst, bk=bk, gcol=gcol: e.scalar_tensor_tensor(
                                    out=dst, in0=psb[bk][:, 0:512], scalar=gcol,
                                    in1=dst, op0=ALU.mult, op1=ALU.add),
                                     reads=(PS[bk], "ff_yac", "ff_gts"),
                                     writes=("ff_yac",),
                                     tag=f"ff_ev{tt}{name}_{tg}_{dh}")

            for ts in range(8):
                t1 = t0 + ts * 128
                rxb, krx = rx[ts % 2], f"ff_rx{ts % 2}"
                self.dma("sp", rxb[:], self.xres[t1:t1 + 128, :],
                         (krx,), (krx,), tag=f"ff_rld{tt}_{ts}")
                rnb, krn = rn[ts % 2], f"ff_rn{ts % 2}"
                if l == 0:
                    p.op("dve", lambda e, ts=ts, rxb=rxb, rnb=rnb: e.tensor_add(
                        rnb[:], yac[:, ts, :], rxb[:]),
                         reads=("ff_yac", krx, krn), writes=(krn,),
                         tag=f"ff_mg{tt}_{ts}")
                else:
                    ubb, kub = ub[ts % 2], f"ff_ub{ts % 2}"
                    self.dma("sp", ubb[:], self.xln[t1:t1 + 128, :],
                             (kub,), (kub,), tag=f"ff_uld{tt}_{ts}")
                    p.op("dve", lambda e, ts=ts, rxb=rxb, rnb=rnb: e.tensor_add(
                        rnb[:], yac[:, ts, :], rxb[:]),
                         reads=("ff_yac", krx, krn), writes=(krn,),
                         tag=f"ff_mg{tt}_{ts}")
                    p.op("dve", lambda e, rnb=rnb, ubb=ubb: e.tensor_add(
                        rnb[:], rnb[:], ubb[:]),
                         reads=(krn, kub), writes=(krn,),
                         tag=f"ff_mgu{tt}_{ts}")
                if l == 0:
                    self.dma("sp", self.xres[t1:t1 + 128, :], rnb[:],
                             (krn,), (f"ff_rst{tt}_{ts}",), tag=f"ff_rs{tt}_{ts}")
                    xob, kxo = xo[ts % 2], f"ff_xo{ts % 2}"
                    self.ln_tile_ops("ffL", tt * 8 + ts, rnb, krn,
                                     stats[:, (tt * 8 + ts) * 8:(tt * 8 + ts) * 8 + 8],
                                     xob, kxo, 0)
                    self.dma("sp", self.xln[t1:t1 + 128, :], xob[:],
                             (kxo,), (f"ff_xst{tt}_{ts}",), tag=f"ff_xs{tt}_{ts}")
                else:
                    self.dma("sp", self.out[t1:t1 + 128, :], rnb[:],
                             (krn,), (f"ff_ost{tt}_{ts}",), tag=f"ff_os{tt}_{ts}")

    def build(self, phases):
        self.declare_io()
        if "prepass" in phases:
            self.prepass()
        self.consts()
        self.p.barrier()
        if "ln1" in phases:
            self.ln1()
            self.p.barrier()
        if "attn0" in phases:
            self.attn(0)
            self.p.barrier()
        if "proj0" in phases:
            self.proj(0)
            self.p.barrier()
        if "ffn0" in phases:
            self.ffn(0)
            self.p.barrier()
        if "attn1" in phases:
            self.attn(1)
            self.p.barrier()
        if "proj1" in phases:
            self.proj(1)
            self.p.barrier()
        if "ffn1" in phases:
            self.ffn(1)
            self.p.barrier()
        self.p.finalize(self.nc, self.sems)
        return self.nc


ALL_PHASES = ("prepass", "ln1", "attn0", "proj0", "ffn0", "attn1", "proj1",
              "ffn1")


_LAST_MK = []


def build_nc(phases=ALL_PHASES, debug_dump=()):
    mk = MK(debug_dump=debug_dump)
    _LAST_MK.clear()
    _LAST_MK.append(mk)
    mk.build(phases)
    return mk.nc


def kernel(**inputs):
    nc = build_nc()
    xs = np.ascontiguousarray(inputs["L0_attn_x"])
    base = {k: np.ascontiguousarray(v) for k, v in inputs.items()
            if k != "L0_attn_x"}
    in_maps = []
    for c in range(8):
        m = dict(base)
        m["x"] = np.ascontiguousarray(xs[c])
        in_maps.append(m)
    res = run_bass_kernel_spmd(nc, in_maps, core_ids=list(range(8)))
    return np.stack([res.results[c]["out"] for c in range(8)], axis=0)


# version 03
# speedup vs baseline: 1.2405x; 1.2405x over previous
import numpy as np
import ml_dtypes
import concourse.bass as bass
import concourse.mybir as mybir
from concourse.bass_utils import run_bass_kernel_spmd

F32 = mybir.dt.float32
BF16 = mybir.dt.bfloat16
AF = mybir.ActivationFunctionType
ALU = mybir.AluOpType
AX = mybir.AxisListType

T = 4096
D = 1024
NH = 16
DH = 64
NEG = -1.0e30
SB_LIMIT = 212800

ENG_COMPUTE_SEM = {"pe": "c_pe", "act": "c_act", "dve": "c_dve", "pool": "c_pool"}
ENG_QUEUE_SEM = {"sp": "q_sp", "pool": "q_pool", "act": "q_act"}
ALL_SEMS = ("c_pe", "c_act", "c_dve", "c_pool", "q_sp", "q_pool", "q_act")


class _Op:
    __slots__ = ("eng", "fn", "reads", "writes", "dma", "idx", "needs_inc",
                 "sem", "cnt", "waits", "vis", "tag", "deps", "bar_waits")

    def __init__(self, eng, fn, reads, writes, dma, idx, vis, tag):
        self.eng = eng
        self.fn = fn
        self.reads = tuple(reads)
        self.writes = tuple(writes)
        self.dma = dma
        self.idx = idx
        self.needs_inc = dma
        self.sem = None
        self.cnt = None
        self.waits = None
        self.vis = vis
        self.tag = tag
        self.deps = ()
        self.bar_waits = ()


PARANOID = [False]


class Prog:

    def __init__(self):
        self.ops = []
        self.barriers = []
        self._writer = {}
        self._readers = {}

    def op(self, eng, fn, reads=(), writes=(), dma=False, vis=True, tag=""):
        o = _Op(eng, fn, reads, writes, dma, len(self.ops), vis, tag)
        deps = set()
        for k in o.reads:
            w = self._writer.get(k)
            if w is not None:
                deps.add(w)
        for k in o.writes:
            w = self._writer.get(k)
            if w is not None:
                deps.add(w)
            for r in self._readers.get(k, ()):
                deps.add(r)
        deps.discard(o.idx)
        if PARANOID[0] and self.ops:
            deps.add(len(self.ops) - 1)
        real = []
        for d in deps:
            dop = self.ops[d]
            if ((not dop.dma) and dop.eng == eng == "pe" and not o.dma
                    and not PARANOID[0]):
                continue
            real.append(d)
        o.deps = tuple(real)
        for d in real:
            self.ops[d].needs_inc = True
        for k in o.reads:
            self._readers.setdefault(k, []).append(o.idx)
        for k in o.writes:
            self._writer[k] = o.idx
            self._readers[k] = []
        self.ops.append(o)
        return o

    def barrier(self, keep_prefix="W:"):
        self.barriers.append(len(self.ops))
        self._writer = {k: v for k, v in self._writer.items()
                        if k.startswith(keep_prefix)}
        self._readers = {k: v for k, v in self._readers.items()
                         if k.startswith(keep_prefix)}

    def finalize(self, nc, sems):
        counts = {name: 0 for name in sems}
        for o in self.ops:
            if o.dma:
                o.sem = ENG_QUEUE_SEM[o.eng]
                counts[o.sem] += 16
                o.cnt = counts[o.sem]
            elif o.needs_inc:
                o.sem = ENG_COMPUTE_SEM[o.eng]
                counts[o.sem] += 1
                o.cnt = counts[o.sem]

        bar_targets = []
        vis_counts = {name: 0 for name in sems}
        bi = 0
        for o in self.ops:
            while bi < len(self.barriers) and self.barriers[bi] == o.idx:
                bar_targets.append(dict(vis_counts))
                bi += 1
            if o.sem is not None and o.vis:
                vis_counts[o.sem] = o.cnt
        while bi < len(self.barriers):
            bar_targets.append(dict(vis_counts))
            bi += 1

        eng_clock = {e: {} for e in ("pe", "act", "dve", "pool", "sp")}
        post = [None] * len(self.ops)
        bar_done = [set() for _ in self.barriers]
        for o in self.ops:
            clock = eng_clock[o.eng]
            bw = []
            for b in range(len(self.barriers)):
                if self.barriers[b] <= o.idx and o.eng not in bar_done[b]:
                    bar_done[b].add(o.eng)
                    for s, c in bar_targets[b].items():
                        if c > clock.get(s, 0):
                            bw.append((s, c))
                            clock[s] = c
            o.bar_waits = tuple(bw)
            waits = []
            for d in sorted(o.deps):
                dop = self.ops[d]
                s, c = dop.sem, dop.cnt
                assert s is not None, (o.tag, dop.tag)
                if c > clock.get(s, 0):
                    waits.append((s, c))
                    clock[s] = c
                    for s2, c2 in post[d].items():
                        if c2 > clock.get(s2, 0):
                            clock[s2] = c2
            o.waits = tuple(waits)
            post[o.idx] = dict(clock)

        by_eng = {e: [] for e in ("pe", "act", "dve", "pool", "sp")}
        for o in self.ops:
            by_eng[o.eng].append(o)

        with nc.Block() as block:
            for eng_name, section in (("sp", block.sync), ("pool", block.gpsimd),
                                      ("pe", block.tensor), ("act", block.scalar),
                                      ("dve", block.vector)):
                def emit(eng, _eng=eng_name):
                    for o in by_eng[_eng]:
                        for s, c in o.bar_waits:
                            eng.wait_ge(sems[s], c)
                        for s, c in o.waits:
                            eng.wait_ge(sems[s], c)
                        inst = o.fn(eng)
                        if o.sem is not None:
                            assert inst is not None, o.tag
                            inst.then_inc(sems[o.sem], 16 if o.dma else 1)
                    for s in ALL_SEMS:
                        if counts[s] > 0:
                            eng.wait_ge(sems[s], counts[s])
                section(emit)
        return counts


class Salloc:

    _uid = [0]

    def __init__(self, nc, base=None, limit=None):
        self.nc = nc
        self.cur = nc.sbuf_base if base is None else base
        self.limit = nc.sbuf_top if limit is None else limit

    def alloc(self, name, shape, dtype, align=64):
        per_part = int(np.prod(shape[1:])) * mybir.dt.size(dtype)
        self.cur = (self.cur + align - 1) // align * align
        Salloc._uid[0] += 1
        t = self.nc.alloc_sbuf_tensor_at(f"{name}_{Salloc._uid[0]}", list(shape),
                                         dtype, offset=self.cur)
        self.cur += per_part
        assert self.cur <= self.limit, (name, shape, self.cur, self.limit)
        return t


def ceil_div(a, b):
    return (a + b - 1) // b


class MK:
    def __init__(self, debug_dump=()):
        self.nc = bass.Bass("TRN2", target_bir_lowering=False, debug=False)
        self.p = Prog()
        self.debug_dump = set(debug_dump)
        self._key = [0]
        self._sem_guards = []

    def K(self, pfx):
        self._key[0] += 1
        return f"{pfx}#{self._key[0]}"

    def declare_io(self):
        nc = self.nc

        def ein(name, shape, dtype=F32):
            return nc.dram_tensor(name, list(shape), dtype,
                                  kind="ExternalInput").ap()

        self.x_in = ein("x", [T, D])
        self.w_attn0 = ein("L0_attn_w_attn", [D, 3 * D])
        self.w_proj0 = ein("L0_attn_w_proj", [D, D])
        self.ws1 = ein("L0_ffn_Ws1", [D, 4096])
        self.bs1 = ein("L0_ffn_bs1", [4096])
        self.ws2 = ein("L0_ffn_Ws2", [2048, D])
        self.bs2 = ein("L0_ffn_bs2", [D])
        self.we1 = ein("L0_ffn_We1", [12, D, 6144])
        self.be1 = ein("L0_ffn_be1", [12, 6144])
        self.we2 = ein("L0_ffn_We2", [12, 3072, D])
        self.be2 = ein("L0_ffn_be2", [12, D])
        self.wr0 = ein("L0_ffn_Wr", [D, 12])
        self.br0 = ein("L0_ffn_br", [12])
        self.wq1 = ein("L1_attn_Wq", [NH, D, DH])
        self.wk1 = ein("L1_attn_Wk", [NH, D, DH])
        self.wv1 = ein("L1_attn_Wv", [NH, D, DH])
        self.wo1 = ein("L1_attn_Wo", [D, D])
        self.gamma1 = ein("L1_attn_gamma", [D])
        self.beta1 = ein("L1_attn_beta", [D])
        self.shup = ein("L1_ffn_shared_up", [1, D, 4096])
        self.shdn = ein("L1_ffn_shared_down", [1, 4096, D])
        self.roup = ein("L1_ffn_routed_up", [8, D, 4096])
        self.rodn = ein("L1_ffn_routed_down", [8, 4096, D])
        self.wr1 = ein("L1_ffn_router_w", [D, 8])

        self.out = nc.dram_tensor("out", [T, D], F32, kind="ExternalOutput").ap()

        def scratch(name, shape, dtype):
            kind = "ExternalOutput" if name in self.debug_dump else "Internal"
            return nc.dram_tensor(name, list(shape), dtype, kind=kind).ap()

        self.xres = scratch("xres", [T, D], F32)
        self.xln = scratch("xln", [T, D], BF16)
        self.heads = scratch("heads", [D, T], BF16)
        self.hop = scratch("hop", [2, T], BF16)
        self.watt_b = [scratch("watt0b", [D, 3 * D], BF16),
                       scratch("watt1b", [D, 3 * D], BF16)]
        self.wproj_b = [scratch("wproj0b", [D, D], BF16),
                        scratch("wproj1b", [D, D], BF16)]
        self.ws1_b = scratch("ws1b", [D, 4096], BF16)
        self.ws2_b = scratch("ws2b", [2048, D], BF16)
        self.we1_b = scratch("we1b", [12, D, 6144], BF16)
        self.we2_b = scratch("we2b", [12, 3072, D], BF16)
        self.shup_b = scratch("shupb", [D, 4096], BF16)
        self.shdn_b = scratch("shdnb", [4096, D], BF16)
        self.roup_b = scratch("roupb", [8, D, 4096], BF16)
        self.rodn_b = scratch("rodnb", [8, 4096, D], BF16)
        self.wr_b = [scratch("wr0b", [D, 12], BF16),
                     scratch("wr1b", [D, 8], BF16)]
        self.br0_b = scratch("br0b", [1, 12], BF16)
        self.bs2_b = scratch("bs2b", [1, D], BF16)
        self.be2_b = scratch("be2b", [12, D], BF16)

        idf = np.eye(128, dtype=np.float32)
        tri = np.where(np.arange(128)[:, None] >= np.arange(128)[None, :],
                       np.float32(0.0), np.float32(NEG))
        self.idf_c = nc.inline_tensor(idf, name="ident_f32")
        self.maskA_c = nc.inline_tensor(tri, name="maskA_t")
        self.maskB_c = nc.inline_tensor(np.ascontiguousarray(tri.T), name="maskB_t")
        self.ones_bf_c = nc.inline_tensor(
            np.ones((1, 512), dtype=ml_dtypes.bfloat16), name="ones_bf_t")
        self.ones_f_c = nc.inline_tensor(
            np.ones((1, 512), dtype=np.float32), name="ones_f_t")
        cv = np.tile(np.array([[1e-5, 1e-3, 0.0, 0.0]], dtype=np.float32),
                     (128, 1))
        self.cvals_c = nc.inline_tensor(cv, name="cvals_t")

        sal = Salloc(nc)
        self.identf = sal.alloc("identf", [128, 128], F32)
        self.maskA = sal.alloc("maskA", [128, 128], F32)
        self.maskB = sal.alloc("maskB", [128, 128], F32)
        self.ones_bf = sal.alloc("ones_bf", [1, 512], BF16)
        self.ones_f = sal.alloc("ones_f", [1, 512], F32)
        self.cvals = sal.alloc("cvals", [128, 4], F32)
        self.bias_s1 = sal.alloc("bias_s1", [128, 32], F32)
        self.bias_s2 = sal.alloc("bias_s2", [128, 8], F32)
        self.bias_e1 = sal.alloc("bias_e1", [128, 12, 48], F32)
        self.gam_bc = sal.alloc("gam_bc", [128, D], F32)
        self.bet_bc = sal.alloc("bet_bc", [128, D], F32)
        self.br_row = sal.alloc("br_row", [1, 12], BF16)
        self.bs2_row = sal.alloc("bs2_row", [1, D], BF16)
        self.be2_sb = sal.alloc("be2_sb", [12, D], BF16)
        self.wr_sb = [sal.alloc("wr0_sb", [128, 8, 12], BF16),
                      sal.alloc("wr1_sb", [128, 8, 8], BF16)]
        self.SCRATCH_BASE = sal.cur

        self.psw = [nc.alloc_psum_tensor(f"psw{i}", [128, 1024], F32)
                    for i in range(4)]
        self.psb = [self.psw[i // 2][:, (i % 2) * 512:(i % 2) * 512 + 512]
                    for i in range(8)]
        self.PS = [f"ps{i}" for i in range(8)]

        self.sems = {}
        for name in ALL_SEMS:
            g = nc.semaphore(name)
            self.sems[name] = g.__enter__()
            self._sem_guards.append(g)

    def dma(self, eng, dst, src, reads, writes, vis=True, tag=""):
        self.p.op(eng, lambda e, d=dst, s=src: e.dma_start(out=d, in_=s),
                  reads=reads, writes=writes, dma=True, vis=vis, tag=tag)

    def dma_t(self, dst, src, reads, writes, tag=""):
        self.p.op("sp", lambda e, d=dst, s=src: e.dma_start(out=d, in_=s,
                                                            transpose=True),
                  reads=reads, writes=writes, dma=True, tag=tag)

    def prepass(self):
        def cast(dst_ap, src_ap, key, tag):
            n = int(np.prod(src_ap.shape))
            s = src_ap.flatten() if len(src_ap.shape) > 1 else src_ap
            d = dst_ap.flatten() if len(dst_ap.shape) > 1 else dst_ap
            CH = 1 << 23
            nch = ceil_div(n, CH)
            for i in range(nch):
                lo, hi = i * CH, min(n, (i + 1) * CH)
                self.dma("pool", d[lo:hi], s[lo:hi], (),
                         (key,) if i == nch - 1 else (), vis=False, tag=tag)

        cast(self.watt_b[0], self.w_attn0, "W:watt0", "pp_watt0")
        cast(self.wproj_b[0], self.w_proj0, "W:wproj0", "pp_wproj0")
        cast(self.ws1_b, self.ws1, "W:ws1", "pp_ws1")
        cast(self.ws2_b, self.ws2, "W:ws2", "pp_ws2")
        cast(self.wr_b[0], self.wr0, "W:wr0", "pp_wr0")
        cast(self.br0_b, self.br0.unsqueeze(0), "W:br0", "pp_br0")
        cast(self.be2_b, self.be2, "W:be2", "pp_be2")
        cast(self.bs2_b, self.bs2.unsqueeze(0), "W:bs2", "pp_bs2")
        for e in range(12):
            cast(self.we1_b[e], self.we1[e], f"W:we1_{e}", f"pp_we1_{e}")
            cast(self.we2_b[e], self.we2[e], f"W:we2_{e}", f"pp_we2_{e}")
        for i, w in enumerate((self.wq1, self.wk1, self.wv1)):
            dst = self.watt_b[1][:, i * D:(i + 1) * D]
            dstv = dst.rearrange("d (h j) -> h d j", j=DH)
            for h in range(NH):
                self.dma("pool", dstv[h], w[h], (),
                         (f"W:watt1_{i}",) if h == NH - 1 else (), vis=False,
                         tag=f"pp_watt1_{i}_{h}")
        cast(self.wproj_b[1], self.wo1, "W:wproj1", "pp_wproj1")
        cast(self.wr_b[1], self.wr1, "W:wr1", "pp_wr1")
        cast(self.shup_b, self.shup[0], "W:shup", "pp_shup")
        cast(self.shdn_b, self.shdn[0], "W:shdn", "pp_shdn")
        for e in range(8):
            cast(self.roup_b[e], self.roup[e], f"W:roup_{e}", f"pp_roup_{e}")
            cast(self.rodn_b[e], self.rodn[e], f"W:rodn_{e}", f"pp_rodn_{e}")

    def consts(self):
        p, psb, PS = self.p, self.psb, self.PS
        self.dma("sp", self.identf[:], self.idf_c.ap(), (), ("identf",), tag="c_id")
        self.dma("sp", self.maskA[:], self.maskA_c.ap(), (), ("maskA",), tag="c_mA")
        self.dma("sp", self.maskB[:], self.maskB_c.ap(), (), ("maskB",), tag="c_mB")
        self.dma("sp", self.ones_bf[:], self.ones_bf_c.ap(), (), ("ones_bf",), tag="c_1b")
        self.dma("sp", self.ones_f[:], self.ones_f_c.ap(), (), ("ones_f",), tag="c_1f")
        self.dma("sp", self.cvals[:], self.cvals_c.ap(), (), ("cvals",), tag="c_cv")
        self.dma("sp", self.br_row[:], self.br0_b, ("W:br0",), ("br_row",), tag="c_br")
        self.dma("sp", self.bs2_row[:], self.bs2_b, ("W:bs2",), ("bs2_row",), tag="c_bs2r")
        self.dma("sp", self.be2_sb[:], self.be2_b, ("W:be2",), ("be2_sb",), tag="c_be2")
        for l in range(2):
            self.dma("sp", self.wr_sb[l][:],
                     self.wr_b[l].rearrange("(c p) e -> p c e", p=128),
                     (f"W:wr{l}",), (f"wr_sb{l}",), tag=f"c_wr{l}")

        sal = Salloc(self.nc, self.SCRATCH_BASE)
        bstage = sal.alloc("bstage", [64, 128], F32)

        def bias_prep(src_flat, n_rows, dst, dst_key, tag):
            assert n_rows <= 64
            self.dma("sp", bstage[0:n_rows, :],
                     src_flat.rearrange("(a b) -> a b", b=128),
                     ("c_bstage",), ("c_bstage",), tag=tag + "_ld")
            p.op("pe", lambda e, n=n_rows: e.transpose(
                psb[6][0:128, 0:n], bstage[0:n, :], self.identf[0:n, 0:n]),
                 reads=("c_bstage", "identf"), writes=(PS[6],), tag=tag + "_tr")
            p.op("dve", lambda e, d=dst, n=n_rows: e.tensor_copy(
                d, psb[6][0:128, 0:n]),
                 reads=(PS[6],), writes=(dst_key,), tag=tag + "_ev")

        bias_prep(self.bs1, 32, self.bias_s1[:], "bias_s1", "c_bs1")
        for e in range(12):
            bias_prep(self.be1[e], 48, self.bias_e1[:, e, :], f"bias_e1_{e}",
                      f"c_be1_{e}")

        gb = sal.alloc("gb_stage", [1, 2 * D], F32)
        self.dma("sp", gb[:, 0:D], self.gamma1.unsqueeze(0), (), ("gbs_g",), tag="c_g")
        self.dma("sp", gb[:, D:2 * D], self.beta1.unsqueeze(0), (), ("gbs_b",), tag="c_b")
        for i, (dst, key, rkey) in enumerate(((self.gam_bc, "gam_bc", "gbs_g"),
                                              (self.bet_bc, "bet_bc", "gbs_b"))):
            for c in range(2):
                sl = slice(i * D + c * 512, i * D + (c + 1) * 512)
                b = 4 + c
                p.op("pe", lambda e, sl=sl, b=b: e.matmul(
                    psb[b][:, 0:512], self.ones_f[0:1, 0:128], gb[:, sl],
                    start=True, stop=True),
                     reads=(rkey, "ones_f"), writes=(PS[b],), tag=f"c_gb{i}{c}")
                p.op("dve", lambda e, dst=dst, c=c, b=b: e.tensor_copy(
                    dst[:, c * 512:(c + 1) * 512], psb[b][:, 0:512]),
                     reads=(PS[b],), writes=(f"{key}{c}",), tag=f"c_gbe{i}{c}")

    def ln_tile_ops(self, tag, t, rx, rx_key, stats, xout, xout_key, eps_col,
                    extra_reads=()):
        p, K = self.p, self.K
        sq = self._ln_sq
        s1, s2 = stats[:, 0:1], stats[:, 1:2]
        mean, rr = stats[:, 2:3], stats[:, 3:4]
        negmr, tmp = stats[:, 4:5], stats[:, 5:6]
        ks = K(f"{tag}_stats{t}")
        p.op("act", lambda e: e.activation(out=sq[:], in_=rx[:], func=AF.Square,
                                           accum_out=s2),
             reads=(rx_key,) + tuple(extra_reads), writes=(ks + "s2", "_lnsq"),
             tag=f"{tag}_sq{t}")
        p.op("dve", lambda e: e.reduce_sum(s1, rx[:], axis=AX.X),
             reads=(rx_key,), writes=(ks + "s1",), tag=f"{tag}_s1{t}")
        p.op("dve", lambda e: e.tensor_scalar(out=mean, in0=s1, scalar1=1.0 / D,
                                              scalar2=None, op0=ALU.mult),
             reads=(ks + "s1",), writes=(ks + "mean",), tag=f"{tag}_mn{t}")
        p.op("dve", lambda e: e.tensor_mul(tmp, mean, mean),
             reads=(ks + "mean",), writes=(ks + "m2",), tag=f"{tag}_m2{t}")
        p.op("dve", lambda e: e.scalar_tensor_tensor(
            out=tmp, in0=s2, scalar=1.0 / D, in1=tmp,
            op0=ALU.mult, op1=ALU.subtract),
             reads=(ks + "s2", ks + "m2"), writes=(ks + "var",), tag=f"{tag}_va{t}")
        p.op("act", lambda e: e.activation(out=rr, in_=tmp, func=AF.Sqrt,
                                           bias=self.cvals[:, eps_col:eps_col + 1]),
             reads=(ks + "var", "cvals"), writes=(ks + "sd",), tag=f"{tag}_sd{t}")
        p.op("dve", lambda e: e.reciprocal(rr, rr),
             reads=(ks + "sd",), writes=(ks + "r",), tag=f"{tag}_r{t}")
        p.op("dve", lambda e: e.scalar_tensor_tensor(
            out=negmr, in0=mean, scalar=-1.0, in1=rr, op0=ALU.mult, op1=ALU.mult),
             reads=(ks + "mean", ks + "r"), writes=(ks + "nmr",), tag=f"{tag}_nm{t}")
        p.op("act", lambda e: e.activation(out=xout[:], in_=rx[:], func=AF.Identity,
                                           bias=negmr, scale=rr),
             reads=(rx_key, ks + "nmr", ks + "r"), writes=(xout_key,),
             tag=f"{tag}_ap{t}")

    def ln1(self):
        p = self.p
        sal = Salloc(self.nc, self.SCRATCH_BASE)
        rx = [sal.alloc("ln1_rx", [128, D], F32) for _ in range(3)]
        xo = [sal.alloc("ln1_xo", [128, D], BF16) for _ in range(3)]
        self._ln_sq = sal.alloc("ln1_sq", [128, D], BF16)
        stats = sal.alloc("ln1_stats", [128, 8 * 32], F32)
        for t in range(32):
            b, ob = rx[t % 3], xo[t % 3]
            kb, ko = f"ln1_rx{t % 3}", f"ln1_xo{t % 3}"
            self.dma("sp", b[:], self.x_in[t * 128:(t + 1) * 128, :],
                     (kb,), (kb,), tag=f"ln1_ld{t}")
            self.ln_tile_ops("ln1", t, b, kb, stats[:, t * 8:(t + 1) * 8],
                             ob, ko, 0)
            self.dma("sp", self.xln[t * 128:(t + 1) * 128, :], ob[:],
                     (ko,), (ko,), tag=f"ln1_st{t}")

    def attn(self, l):
        p, psb, PS, K = self.p, self.psb, self.PS, self.K
        nc = self.nc
        sal = Salloc(nc, self.SCRATCH_BASE)
        watt2 = [sal.alloc(f"at_watt{i}", [128, 8, 768], BF16) for i in range(2)]
        xlnf = sal.alloc("at_xlnf", [128, 8, T], BF16)
        qb = [sal.alloc(f"at_qb{i}", [66, T], BF16) for i in range(4)]
        kb = [sal.alloc(f"at_kb{i}", [66, T], BF16) for i in range(4)]
        vsb = sal.alloc("at_vsb", [128, 32, 256], BF16)
        mbuf = sal.alloc("at_mbuf", [128, 128], F32)
        zbuf = sal.alloc("at_zbuf", [128, 128], F32)
        cmb = sal.alloc("at_cmb", [128, 128], F32)
        st2 = sal.alloc("at_st2", [128, 4 * 32], F32)
        hilo = sal.alloc("at_hilo", [32, 256], BF16)
        dxp = sal.alloc("at_dxp", [128, 1024], BF16)
        att = [sal.alloc(f"at_att{i}", [128, 1024], BF16) for i in range(2)]
        hsb = [sal.alloc(f"at_hsb{i}", [64, 1024], BF16) for i in range(2)]

        wb = self.watt_b[l]
        wkeys = (("W:watt0",) if l == 0 else
                 ("W:watt1_0", "W:watt1_1", "W:watt1_2"))

        for kc in range(8):
            self.dma_t(xlnf[:, kc, :], self.xln[:, kc * 128:(kc + 1) * 128],
                       (), (f"at_xlnf{kc}",), tag=f"at_xld{kc}")
        XLNF_ALL = tuple(f"at_xlnf{kc}" for kc in range(8))
        for i in range(4):
            p.op("pool", lambda e, i=i: e.memset(kb[i][64:66, :], 1.0),
                 reads=(), writes=(f"at_kones{i}",), tag=f"at_ko{i}")

        for g in range(4):
            wt = watt2[g % 2]
            WATT_ALL = tuple(f"at_w{g % 2}_{kc}" for kc in range(8))
            for kc in range(8):
                srcv = wb[kc * 128:(kc + 1) * 128, :].rearrange(
                    "p (s g c) -> p s g c", s=3, c=256)[:, :, g, :]
                self.dma("sp", wt[:, kc, :].rearrange("p (s c) -> p s c", c=256),
                         srcv, wkeys, (f"at_w{g % 2}_{kc}",),
                         tag=f"at_wld{g}_{kc}")
            for i in range(4):
                h = 4 * g + i
                for x, colbase, dstb, dstk in ((0, i * DH, qb[i], f"at_q{i}"),
                                               (1, 256 + i * DH, kb[i], f"at_k{i}")):
                    for tt in range(8):
                        b = 6 + (tt % 2)
                        def mmfn(e, cb=colbase, tt=tt, b=b, wt=wt):
                            inst = None
                            for kc in range(8):
                                inst = e.matmul(
                                    psb[b][0:64, 0:512],
                                    wt[:, kc, cb:cb + DH],
                                    xlnf[:, kc, tt * 512:(tt + 1) * 512],
                                    start=(kc == 0), stop=(kc == 7))
                            return inst
                        p.op("pe", mmfn, reads=WATT_ALL + XLNF_ALL,
                             writes=(PS[b],), tag=f"at_qk{g}{i}{x}{tt}")
                        if x == 0:
                            p.op("act", lambda e, i=i, tt=tt, b=b: e.activation(
                                out=qb[i][0:64, tt * 512:(tt + 1) * 512],
                                in_=psb[b][0:64, 0:512], func=AF.Identity,
                                scale=0.125),
                                 reads=(PS[b],), writes=(f"at_q{i}_{tt}",),
                                 tag=f"at_qe{g}{i}{tt}")
                        else:
                            p.op("dve", lambda e, i=i, tt=tt, b=b: e.tensor_copy(
                                kb[i][0:64, tt * 512:(tt + 1) * 512],
                                psb[b][0:64, 0:512]),
                                 reads=(PS[b],), writes=(f"at_k{i}_{tt}",),
                                 tag=f"at_ke{g}{i}{tt}")
            for ts in range(32):
                b = 2 + (ts % 2)
                def vfn(e, ts=ts, b=b, wt=wt):
                    inst = None
                    for kc in range(8):
                        inst = e.matmul(
                            psb[b][:, 0:256],
                            xlnf[:, kc, ts * 128:(ts + 1) * 128],
                            wt[:, kc, 512:768],
                            start=(kc == 0), stop=(kc == 7))
                    return inst
                p.op("pe", vfn, reads=WATT_ALL + XLNF_ALL, writes=(PS[b],),
                     tag=f"at_v{g}{ts}")
                p.op("dve", lambda e, ts=ts, b=b: e.tensor_copy(
                    vsb[:, ts, :], psb[b][:, 0:256]),
                     reads=(PS[b],), writes=(f"at_v{ts}",), tag=f"at_ve{g}{ts}")
            VSB_ALL = tuple(f"at_v{ts}" for ts in range(32))

            psw, PSW = self.psw, [(("ps0", "ps1"), ("ps2", "ps3"), ("ps4", "ps5"))]
            for i in range(4):
                h = 4 * g + i
                QROWS = tuple(f"at_q{i}_{tt}" for tt in range(8))
                KROWS = tuple(f"at_k{i}_{tt}" for tt in range(8))
                p.op("dve", lambda e: e.memset(mbuf[:, 0:128], -NEG),
                     reads=(), writes=("at_mbuf",), tag=f"at_mi{h}")
                p.op("dve", lambda e: e.memset(zbuf[:, 0:128], 0.0),
                     reads=(), writes=("at_zbuf",), tag=f"at_zi{h}")
                for qt in range(32):
                    kend = (qt + 1) * 128
                    nchunk = ceil_div(kend, 1024)
                    for c in range(nchunk):
                        w = min(1024, kend - c * 1024)
                        si = (qt * nchunk + c) % 3 if False else (qt + c) % 3
                        pw = psw[si]
                        PKEYS = (PS[2 * si], PS[2 * si + 1])
                        def pafn(e, qt=qt, c=c, w=w, pw=pw, i=i):
                            inst = None
                            for sub in range(ceil_div(w, 512)):
                                ww = min(512, w - sub * 512)
                                inst = e.matmul(
                                    pw[:, sub * 512:sub * 512 + ww],
                                    qb[i][0:64, qt * 128:(qt + 1) * 128],
                                    kb[i][0:64,
                                          c * 1024 + sub * 512:
                                          c * 1024 + sub * 512 + ww],
                                    start=True, stop=True)
                            return inst
                        p.op("pe", pafn, reads=QROWS + KROWS, writes=PKEYS,
                             tag=f"at_pa{h}_{qt}_{c}")
                        if c == nchunk - 1:
                            p.op("dve", lambda e, w=w, pw=pw: e.tensor_add(
                                pw[:, w - 128:w], pw[:, w - 128:w],
                                self.maskA[:]),
                                 reads=PKEYS + ("maskA",), writes=PKEYS,
                                 tag=f"at_pam{h}_{qt}")
                        col = qt * 4 + c
                        p.op("dve", lambda e, w=w, pw=pw, col=col: e.reduce_max(
                            mbuf[:, col:col + 1], pw[:, 0:w], axis=AX.X,
                            negate=True),
                             reads=PKEYS + ("at_mbuf",), writes=(f"at_m{col}",),
                             tag=f"at_pam2{h}_{qt}_{c}")
                        p.op("act", lambda e, w=w, pw=pw, col=col: e.activation(
                            out=dxp[:, 0:w], in_=pw[:, 0:w], func=AF.Exp,
                            bias=mbuf[:, col:col + 1],
                            accum_out=zbuf[:, col:col + 1]),
                             reads=PKEYS + (f"at_m{col}", "at_mbuf", "at_zbuf"),
                             writes=(f"at_z{col}", "at_dxp"),
                             tag=f"at_pae{h}_{qt}_{c}")
                MS = tuple(f"at_m{qt * 4 + c}" for qt in range(32)
                           for c in range(ceil_div((qt + 1) * 128, 1024)))
                ZS = tuple(f"at_z{qt * 4 + c}" for qt in range(32)
                           for c in range(ceil_div((qt + 1) * 128, 1024)))
                nM = st2[:, 0:32]
                lnS = st2[:, 32:64]
                negst = st2[:, 64:96]
                S = st2[:, 96:128]
                m3 = mbuf[:, 0:128].rearrange("p (a b) -> p a b", a=32)
                z3 = zbuf[:, 0:128].rearrange("p (a b) -> p a b", a=32)
                c3 = cmb[:, 0:128].rearrange("p (a b) -> p a b", a=32)
                p.op("dve", lambda e: e.tensor_reduce(nM, m3, axis=AX.X,
                                                      op=ALU.min),
                     reads=MS + ("at_mbuf",), writes=("at_nM",), tag=f"at_cm1{h}")
                p.op("dve", lambda e: e.tensor_tensor(
                    out=c3, in0=nM.unsqueeze(2).broadcast_to([128, 32, 4]),
                    in1=m3, op=ALU.subtract),
                     reads=("at_nM", "at_mbuf"), writes=("at_cmb",),
                     tag=f"at_cm2{h}")
                p.op("act", lambda e: e.activation(out=cmb[:, 0:128],
                                                   in_=cmb[:, 0:128],
                                                   func=AF.Exp),
                     reads=("at_cmb",), writes=("at_cmb",), tag=f"at_cm3{h}")
                p.op("dve", lambda e: e.tensor_mul(c3, c3, z3),
                     reads=("at_cmb", "at_zbuf") + ZS, writes=("at_cmb",),
                     tag=f"at_cm4{h}")
                p.op("dve", lambda e: e.tensor_reduce(S, c3, axis=AX.X,
                                                      op=ALU.add),
                     reads=("at_cmb",), writes=("at_S",), tag=f"at_cm5{h}")
                p.op("act", lambda e: e.activation(out=lnS, in_=S, func=AF.Ln),
                     reads=("at_S",), writes=("at_lnS",), tag=f"at_cm6{h}")
                p.op("dve", lambda e: e.tensor_sub(negst, nM, lnS),
                     reads=("at_nM", "at_lnS"), writes=("at_negst",),
                     tag=f"at_cm7{h}")
                p.op("pe", lambda e: e.transpose(psb[6][0:32, 0:128], negst,
                                                 self.identf[:]),
                     reads=("at_negst", "identf"), writes=(PS[6],),
                     tag=f"at_str{h}")
                p.op("dve", lambda e: e.tensor_copy(hilo[:, 0:128],
                                                    psb[6][0:32, 0:128]),
                     reads=(PS[6],), writes=("at_hi",), tag=f"at_hi{h}")
                p.op("dve", lambda e: e.tensor_sub(hilo[:, 128:256],
                                                   psb[6][0:32, 0:128],
                                                   hilo[:, 0:128]),
                     reads=(PS[6], "at_hi"), writes=("at_lo",), tag=f"at_lo{h}")
                self.dma("sp", self.hop[0:1, :], hilo[:, 0:128],
                         ("at_hi",), ("at_hop0",), tag=f"at_hd0{h}")
                self.dma("sp", self.hop[1:2, :], hilo[:, 128:256],
                         ("at_lo",), ("at_hop1",), tag=f"at_hd1{h}")
                self.dma("sp", qb[i][64:66, :], self.hop[:, :],
                         ("at_hop0", "at_hop1"), (f"at_qst{i}",),
                         tag=f"at_hl{h}")

            for i in range(4):
                h = 4 * g + i
                QROWS = tuple(f"at_q{i}_{tt}" for tt in range(8))
                KROWS = tuple(f"at_k{i}_{tt}" for tt in range(8))
                for pr in range(4):
                    qce, qco = 2 * pr, 2 * pr + 1
                    av = self.psw[2]
                    AVK = (PS[4], PS[5])
                    for kt in range(8 * pr + 8):
                        wide = kt <= 8 * pr + 3
                        sc = self.psw[kt % 2]
                        SCK = (PS[2 * (kt % 2)], PS[2 * (kt % 2) + 1])
                        a = kt % 2
                        def scfn(e, i=i, kt=kt, qce=qce, wide=wide, sc=sc):
                            inst = None
                            if wide:
                                inst = e.matmul(
                                    sc[:, 0:512],
                                    kb[i][0:66, kt * 128:(kt + 1) * 128],
                                    qb[i][0:66, qce * 512:(qce + 1) * 512],
                                    start=True, stop=True)
                            inst = e.matmul(
                                sc[:, 512:1024],
                                kb[i][0:66, kt * 128:(kt + 1) * 128],
                                qb[i][0:66, (qce + 1) * 512:(qce + 2) * 512],
                                start=True, stop=True)
                            return inst
                        p.op("pe", scfn,
                             reads=KROWS + QROWS + (f"at_qst{i}", f"at_kones{i}"),
                             writes=SCK, tag=f"at_pb{h}_{pr}_{kt}")
                        dge = (kt // 4 == qce)
                        dgo = (kt // 4 == qco)
                        if dge:
                            j0 = (kt % 4) * 128
                            p.op("dve", lambda e, sc=sc, j0=j0: e.tensor_add(
                                sc[:, j0:j0 + 128], sc[:, j0:j0 + 128],
                                self.maskB[:]),
                                 reads=SCK + ("maskB",), writes=SCK,
                                 tag=f"at_pbm{h}_{pr}_{kt}")
                            if j0 > 0:
                                p.op("dve", lambda e, a=a, j0=j0: e.memset(
                                    att[a][:, 0:j0], 0.0),
                                     reads=(), writes=(f"at_att{a}",),
                                     tag=f"at_pbz{h}_{pr}_{kt}")
                            p.op("act", lambda e, a=a, sc=sc, j0=j0: e.activation(
                                out=att[a][:, j0:1024], in_=sc[:, j0:1024],
                                func=AF.Exp),
                                 reads=SCK, writes=(f"at_att{a}",),
                                 tag=f"at_pbe{h}_{pr}_{kt}")
                        elif dgo:
                            j0 = 512 + (kt % 4) * 128
                            p.op("dve", lambda e, sc=sc, j0=j0: e.tensor_add(
                                sc[:, j0:j0 + 128], sc[:, j0:j0 + 128],
                                self.maskB[:]),
                                 reads=SCK + ("maskB",), writes=SCK,
                                 tag=f"at_pbm{h}_{pr}_{kt}")
                            if j0 > 512:
                                p.op("dve", lambda e, a=a, j0=j0: e.memset(
                                    att[a][:, 512:j0], 0.0),
                                     reads=(), writes=(f"at_att{a}",),
                                     tag=f"at_pbz{h}_{pr}_{kt}")
                            p.op("act", lambda e, a=a, sc=sc, j0=j0: e.activation(
                                out=att[a][:, j0:1024], in_=sc[:, j0:1024],
                                func=AF.Exp),
                                 reads=SCK, writes=(f"at_att{a}",),
                                 tag=f"at_pbe{h}_{pr}_{kt}")
                        else:
                            lo = 0 if wide else 512
                            p.op("act", lambda e, a=a, sc=sc, lo=lo: e.activation(
                                out=att[a][:, lo:1024], in_=sc[:, lo:1024],
                                func=AF.Exp),
                                 reads=SCK, writes=(f"at_att{a}",),
                                 tag=f"at_pbe{h}_{pr}_{kt}")
                        def avfn(e, i=i, kt=kt, pr=pr, a=a, wide=wide, av=av):
                            inst = None
                            if wide:
                                inst = e.matmul(
                                    av[0:64, 0:512],
                                    vsb[:, kt, i * 64:(i + 1) * 64],
                                    att[a][:, 0:512],
                                    start=(kt == 0), stop=(kt == 8 * pr + 3))
                            inst = e.matmul(
                                av[0:64, 512:1024],
                                vsb[:, kt, i * 64:(i + 1) * 64],
                                att[a][:, 512:1024],
                                start=(kt == 0), stop=(kt == 8 * pr + 7))
                            return inst
                        p.op("pe", avfn,
                             reads=(f"at_att{a}",) + VSB_ALL,
                             writes=AVK, tag=f"at_av{h}_{pr}_{kt}")
                    p.op("dve", lambda e, av=av, pr=pr: e.tensor_copy(
                        hsb[pr % 2][:, 0:1024], av[0:64, 0:1024]),
                         reads=AVK, writes=(f"at_hsb{pr % 2}",),
                         tag=f"at_he{h}_{pr}")
                    self.dma("sp",
                             self.heads[h * 64:(h + 1) * 64,
                                        pr * 1024:(pr + 1) * 1024],
                             hsb[pr % 2][:, 0:1024], (f"at_hsb{pr % 2}",),
                             (f"at_hst{h}_{pr}",), tag=f"at_hs{h}_{pr}")

    def proj(self, l):
        p, psb, PS = self.p, self.psb, self.PS
        nc = self.nc
        sal = Salloc(nc, self.SCRATCH_BASE)
        wp = sal.alloc("pj_wp", [128, 8, D], BF16)
        hb = [sal.alloc(f"pj_hb{i}", [128, 8, 128], BF16) for i in range(2)]
        rx = [sal.alloc(f"pj_rx{i}", [128, D], F32) for i in range(2)]
        ub = [sal.alloc(f"pj_ub{i}", [128, D], BF16) for i in range(2)]
        yb = [sal.alloc(f"pj_yb{i}", [128, D], F32) for i in range(2)]
        y2 = [sal.alloc(f"pj_y2{i}", [128, D], F32) for i in range(2)]
        rnew = [sal.alloc(f"pj_rn{i}", [128, D], F32) for i in range(2)]
        xo = [sal.alloc(f"pj_xo{i}", [128, D], BF16) for i in range(2)]
        self._ln_sq = sal.alloc("pj_sq", [128, D], BF16)
        stats = sal.alloc("pj_stats", [128, 8 * 32], F32)
        stats2 = sal.alloc("pj_stats2", [128, 8 * 32], F32)

        for kc in range(8):
            self.dma("sp", wp[:, kc, :],
                     self.wproj_b[l][kc * 128:(kc + 1) * 128, :],
                     (f"W:wproj{l}",), (f"pj_wp{kc}",), tag=f"pj_wld{kc}")
        WP_ALL = tuple(f"pj_wp{kc}" for kc in range(8))
        resid_src = self.x_in if l == 0 else self.xres

        for ts in range(32):
            hbb, khb = hb[ts % 2], f"pj_hb{ts % 2}"
            self.dma("sp", hbb[:],
                     self.heads.rearrange("(c p) t -> p c t", p=128)[
                         :, :, ts * 128:(ts + 1) * 128],
                     (), (khb,), tag=f"pj_hld{ts}")
            rxb, krx = rx[ts % 2], f"pj_rx{ts % 2}"
            self.dma("sp", rxb[:], resid_src[ts * 128:(ts + 1) * 128, :],
                     (), (krx,), tag=f"pj_rld{ts}")

            def mmfn(e, hbb=hbb):
                inst = None
                for kc in range(8):
                    for half in range(2):
                        inst = e.matmul(
                            psb[half][:, 0:512], hbb[:, kc, :],
                            wp[:, kc, half * 512:(half + 1) * 512],
                            start=(kc == 0), stop=(kc == 7))
                return inst
            p.op("pe", mmfn, reads=(khb,) + WP_ALL, writes=(PS[0], PS[1]),
                 tag=f"pj_mm{ts}")

            rnb, krn = rnew[ts % 2], f"pj_rn{ts % 2}"
            if l == 0:
                for half in range(2):
                    p.op("dve", lambda e, half=half, rxb=rxb, rnb=rnb: e.tensor_add(
                        rnb[:, half * 512:(half + 1) * 512],
                        psb[half][:, 0:512],
                        rxb[:, half * 512:(half + 1) * 512]),
                         reads=(PS[half], krx), writes=(krn + f"h{half}",),
                         tag=f"pj_add{ts}_{half}")
                rn_r = (krn + "h0", krn + "h1")
            else:
                ubb, kub = ub[ts % 2], f"pj_ub{ts % 2}"
                self.dma("sp", ubb[:], self.xln[ts * 128:(ts + 1) * 128, :],
                         (), (kub,), tag=f"pj_uld{ts}")
                ybb, kyb = yb[ts % 2], f"pj_yb{ts % 2}"
                y2b, ky2 = y2[ts % 2], f"pj_y2{ts % 2}"
                for half in range(2):
                    p.op("dve", lambda e, half=half, ubb=ubb, ybb=ybb: e.tensor_add(
                        ybb[:, half * 512:(half + 1) * 512],
                        psb[half][:, 0:512],
                        ubb[:, half * 512:(half + 1) * 512]),
                         reads=(PS[half], kub, kyb), writes=(kyb,),
                         tag=f"pj_yadd{ts}_{half}")
                self.ln_tile_ops("pjI", ts, ybb, kyb,
                                 stats[:, ts * 8:ts * 8 + 8], y2b, ky2, 1)
                p.op("dve", lambda e, y2b=y2b: e.tensor_mul(
                    y2b[:], y2b[:], self.gam_bc[:]),
                     reads=(ky2, "gam_bc0", "gam_bc1"),
                     writes=(ky2,), tag=f"pj_gm{ts}")
                p.op("dve", lambda e, y2b=y2b: e.tensor_add(
                    y2b[:], y2b[:], self.bet_bc[:]),
                     reads=(ky2, "bet_bc0", "bet_bc1"),
                     writes=(ky2,), tag=f"pj_bt{ts}")
                p.op("dve", lambda e, rxb=rxb, rnb=rnb, y2b=y2b: e.tensor_add(
                    rnb[:], y2b[:], rxb[:]),
                     reads=(ky2, krx), writes=(krn + "h0",),
                     tag=f"pj_add{ts}")
                rn_r = (krn + "h0",)
            self.dma("sp", self.xres[ts * 128:(ts + 1) * 128, :], rnb[:],
                     rn_r, (f"pj_rst{ts}",), tag=f"pj_rs{ts}")
            xob, kxo = xo[ts % 2], f"pj_xo{ts % 2}"
            self.ln_tile_ops("pjL", ts, rnb, rn_r[0],
                             stats2[:, ts * 8:ts * 8 + 8], xob, kxo, 0,
                             extra_reads=rn_r[1:])
            self.dma("sp", self.xln[ts * 128:(ts + 1) * 128, :], xob[:],
                     (kxo,), (f"pj_xst{ts}",), tag=f"pj_xs{ts}")

    def ffn(self, l):
        p, psb, PS, K = self.p, self.psb, self.PS, self.K
        nc = self.nc
        E = 12 if l == 0 else 8
        HA = 3072 if l == 0 else 4096
        NJ = HA // 128
        sal = Salloc(nc, self.SCRATCH_BASE)
        xlnf = sal.alloc("ff_xlnf", [128, 8, 1024], BF16)
        act = sal.alloc("ff_act", [128, 32 if l else 24, 1024], BF16)
        yac = sal.alloc("ff_yac", [128, 8, D], F32)
        wup = [sal.alloc(f"ff_wup{i}", [128, 8, 1536 if l == 0 else 768], BF16)
               for i in range(2)]
        wdn = [sal.alloc(f"ff_wdn{i}", [128, 4, D], BF16) for i in range(2)]
        gsb = sal.alloc("ff_gsb", [128, 2, 512], BF16)
        lgb = sal.alloc("ff_lgb", [128, 8, E], F32)
        gts = sal.alloc("ff_gts", [128, 8, E], F32)
        gsc = sal.alloc("ff_gsc", [128, 8 * 8], F32)
        gfm = sal.alloc("ff_gfm", [12, 1024], BF16) if l == 0 else None
        rx = [sal.alloc(f"ff_rx{i}", [128, D], F32) for i in range(2)]
        ub = [sal.alloc(f"ff_ub{i}", [128, D], BF16) for i in range(2)] \
            if l == 1 else [None, None]
        rn = [sal.alloc(f"ff_rn{i}", [128, D], F32) for i in range(2)]
        xo = [sal.alloc(f"ff_xo{i}", [128, D], BF16) for i in range(2)] \
            if l == 0 else [None, None]
        if l == 0:
            self._ln_sq = sal.alloc("ff_sq", [128, D], BF16)
            stats = sal.alloc("ff_stats", [128, 8 * 32], F32)
        else:
            stats = None

        if l == 0:
            jobs = [("d", self.ws1_b, self.ws2_b, 4096, 2048, None, "s1")]
            jobs += [(f"e{e}", self.we1_b[e], self.we2_b[e], 6144, 3072, e,
                      f"e1_{e}") for e in range(12)]
            wk_up = {"d": ("W:ws1",)}
            wk_dn = {"d": ("W:ws2",)}
            for e in range(12):
                wk_up[f"e{e}"] = (f"W:we1_{e}",)
                wk_dn[f"e{e}"] = (f"W:we2_{e}",)
            geglu = True
        else:
            jobs = [("d", self.shup_b, self.shdn_b, 4096, 4096, None, None)]
            jobs += [(f"e{e}", self.roup_b[e], self.rodn_b[e], 4096, 4096, e,
                      None) for e in range(8)]
            wk_up = {"d": ("W:shup",)}
            wk_dn = {"d": ("W:shdn",)}
            for e in range(8):
                wk_up[f"e{e}"] = (f"W:roup_{e}",)
                wk_dn[f"e{e}"] = (f"W:rodn_{e}",)
            geglu = False

        for tt in range(4):
            t0 = tt * 1024
            for kc in range(8):
                self.dma_t(xlnf[:, kc, :],
                           self.xln[t0:t0 + 1024, kc * 128:(kc + 1) * 128],
                           (f"ff_xlnf{kc}",), (f"ff_xlnf{kc}",),
                           tag=f"ff_xld{tt}_{kc}")
            XLNF = tuple(f"ff_xlnf{kc}" for kc in range(8))

            for ts in range(8):
                def lgfn(e, ts=ts, l=l):
                    inst = None
                    for kc in range(8):
                        inst = e.matmul(
                            psb[6][:, 0:E],
                            xlnf[:, kc, ts * 128:(ts + 1) * 128],
                            self.wr_sb[l][:, kc, :],
                            start=(kc == 0), stop=(kc == 7 and l != 0))
                    if l == 0:
                        inst = e.matmul(psb[6][:, 0:E], self.ones_bf[0:1, 0:128],
                                        self.br_row[:], start=False, stop=True)
                    return inst
                p.op("pe", lgfn, reads=XLNF + (f"wr_sb{l}", "ones_bf", "br_row"),
                     writes=(PS[6],), tag=f"ff_lg{tt}_{ts}")
                p.op("act", lambda e, ts=ts: e.activation(
                    out=lgb[:, ts, :], in_=psb[6][:, 0:E], func=AF.Identity,
                    scale=(1.0 / 1.5) if l == 0 else 1.0),
                     reads=(PS[6],), writes=(f"ff_lg{ts}",), tag=f"ff_lge{tt}_{ts}")
            LG = tuple(f"ff_lg{ts}" for ts in range(8))
            v1 = gsc[:, 0:8]
            v2 = gsc[:, 8:16]
            w1 = gsc[:, 16:24]
            w2 = gsc[:, 24:32]
            ssum = gsc[:, 32:40]
            if l == 0:
                p.op("dve", lambda e: e.tensor_reduce(
                    v1, lgb[:], axis=AX.X, op=ALU.max),
                     reads=LG, writes=("ff_v1",), tag=f"ff_r1{tt}")
                p.op("dve", lambda e: e.tensor_tensor(
                    out=gts[:], in0=lgb[:],
                    in1=v1.unsqueeze(2).broadcast_to([128, 8, E]),
                    op=ALU.is_equal),
                     reads=LG + ("ff_v1",), writes=("ff_m1",), tag=f"ff_m1{tt}")
                p.op("dve", lambda e: e.scalar_tensor_tensor(
                    out=lgb[:], in0=gts[:], scalar=NEG, in1=lgb[:],
                    op0=ALU.mult, op1=ALU.add),
                     reads=("ff_m1",) + LG, writes=("ff_msk",), tag=f"ff_mk{tt}")
                p.op("dve", lambda e: e.tensor_reduce(
                    v2, lgb[:], axis=AX.X, op=ALU.max),
                     reads=("ff_msk",), writes=("ff_v2",), tag=f"ff_r2{tt}")
                p.op("dve", lambda e: e.tensor_sub(w1, v1, v2),
                     reads=("ff_v1", "ff_v2"), writes=("ff_d12",), tag=f"ff_d{tt}")
                p.op("act", lambda e: e.activation(out=w1, in_=w1, func=AF.Sigmoid),
                     reads=("ff_d12",), writes=("ff_w1",), tag=f"ff_s1{tt}")
                p.op("dve", lambda e: e.tensor_scalar(
                    out=w2, in0=w1, scalar1=-1.0, scalar2=1.0,
                    op0=ALU.mult, op1=ALU.add),
                     reads=("ff_w1",), writes=("ff_w2",), tag=f"ff_s2{tt}")
                p.op("dve", lambda e: e.tensor_tensor(
                    out=lgb[:], in0=lgb[:],
                    in1=v2.unsqueeze(2).broadcast_to([128, 8, E]),
                    op=ALU.is_equal),
                     reads=("ff_msk", "ff_v2"), writes=("ff_m2",), tag=f"ff_m2{tt}")
                p.op("dve", lambda e: e.tensor_mul(
                    gts[:], gts[:],
                    w1.unsqueeze(2).broadcast_to([128, 8, E])),
                     reads=("ff_m1", "ff_w1"), writes=("ff_g1",), tag=f"ff_g1{tt}")
                p.op("dve", lambda e: e.tensor_mul(
                    lgb[:], lgb[:],
                    w2.unsqueeze(2).broadcast_to([128, 8, E])),
                     reads=("ff_m2", "ff_w2"), writes=("ff_g2",), tag=f"ff_g2{tt}")
                p.op("dve", lambda e: e.tensor_add(gts[:], gts[:], lgb[:]),
                     reads=("ff_g1", "ff_g2"), writes=("ff_gts",), tag=f"ff_gs{tt}")
                for ts in range(8):
                    p.op("pe", lambda e, ts=ts: e.transpose(
                        psb[7][0:E, 0:128], gts[:, ts, :], self.identf[:]),
                         reads=("ff_gts", "identf"), writes=(PS[7],),
                         tag=f"ff_gt{tt}_{ts}")
                    p.op("dve", lambda e, ts=ts: e.tensor_copy(
                        gfm[:, ts * 128:(ts + 1) * 128], psb[7][0:E, 0:128]),
                         reads=(PS[7],), writes=("ff_gfm",), tag=f"ff_gf{tt}_{ts}")
            else:
                p.op("dve", lambda e: e.tensor_reduce(
                    v1, lgb[:], axis=AX.X, op=ALU.max),
                     reads=LG, writes=("ff_v1",), tag=f"ff_r1{tt}")
                p.op("dve", lambda e: e.tensor_tensor(
                    out=lgb[:], in0=lgb[:],
                    in1=v1.unsqueeze(2).broadcast_to([128, 8, E]),
                    op=ALU.subtract),
                     reads=LG + ("ff_v1",), writes=("ff_sh",), tag=f"ff_sh{tt}")
                p.op("act", lambda e: e.activation(out=lgb[:], in_=lgb[:],
                                                   func=AF.Exp),
                     reads=("ff_sh",), writes=("ff_ex",), tag=f"ff_ex{tt}")
                p.op("dve", lambda e: e.tensor_reduce(
                    ssum, lgb[:], axis=AX.X, op=ALU.add),
                     reads=("ff_ex",), writes=("ff_ss",), tag=f"ff_ss{tt}")
                p.op("dve", lambda e: e.reciprocal(ssum, ssum),
                     reads=("ff_ss",), writes=("ff_rs",), tag=f"ff_rs{tt}")
                p.op("dve", lambda e: e.tensor_mul(
                    lgb[:], lgb[:], ssum.unsqueeze(2).broadcast_to([128, 8, E])),
                     reads=("ff_ex", "ff_rs"), writes=("ff_pr",), tag=f"ff_pr{tt}")
                p.op("dve", lambda e: e.tensor_reduce(
                    v1, lgb[:], axis=AX.X, op=ALU.max),
                     reads=("ff_pr",), writes=("ff_p1",), tag=f"ff_p1{tt}")
                p.op("dve", lambda e: e.tensor_tensor(
                    out=gts[:], in0=lgb[:],
                    in1=v1.unsqueeze(2).broadcast_to([128, 8, E]),
                    op=ALU.is_equal),
                     reads=("ff_pr", "ff_p1"), writes=("ff_m1",), tag=f"ff_m1{tt}")
                p.op("dve", lambda e: e.scalar_tensor_tensor(
                    out=lgb[:], in0=gts[:], scalar=-2.0, in1=lgb[:],
                    op0=ALU.mult, op1=ALU.add),
                     reads=("ff_m1", "ff_pr"), writes=("ff_msk",), tag=f"ff_mk{tt}")
                p.op("dve", lambda e: e.tensor_reduce(
                    v2, lgb[:], axis=AX.X, op=ALU.max),
                     reads=("ff_msk",), writes=("ff_v2",), tag=f"ff_r2{tt}")
                p.op("dve", lambda e: e.tensor_tensor(
                    out=lgb[:], in0=lgb[:],
                    in1=v2.unsqueeze(2).broadcast_to([128, 8, E]),
                    op=ALU.is_equal),
                     reads=("ff_msk", "ff_v2"), writes=("ff_m2",), tag=f"ff_m2{tt}")
                p.op("dve", lambda e: e.tensor_mul(
                    gts[:], gts[:], v1.unsqueeze(2).broadcast_to([128, 8, E])),
                     reads=("ff_m1", "ff_p1"), writes=("ff_g1",), tag=f"ff_g1{tt}")
                p.op("dve", lambda e: e.tensor_mul(
                    lgb[:], lgb[:], v2.unsqueeze(2).broadcast_to([128, 8, E])),
                     reads=("ff_m2", "ff_v2"), writes=("ff_g2",), tag=f"ff_g2{tt}")
                p.op("dve", lambda e: e.tensor_add(gts[:], gts[:], lgb[:]),
                     reads=("ff_g1", "ff_g2"), writes=("ff_gts",), tag=f"ff_gs{tt}")

            p.op("dve", lambda e: e.memset(yac[:], 0.0),
                 reads=(), writes=("ff_yac",), tag=f"ff_yz{tt}")

            for (name, wu, wd, HB, HAe, gate_e, bias) in jobs:
                nj = HAe // 128
                if geglu:
                    nband = ceil_div(nj, 6)
                    for b in range(nband):
                        j0 = b * 6
                        j1 = min(nj, j0 + 6)
                        wbuf, wkb = wup[b % 2], f"ff_wup{b % 2}"
                        for kc in range(8):
                            srcv = wu[kc * 128:(kc + 1) * 128, :].rearrange(
                                "p (s c) -> p s c", s=2)[
                                :, :, j0 * 128:j1 * 128]
                            self.dma("sp",
                                     wbuf[:, kc, 0:2 * (j1 - j0) * 128].rearrange(
                                         "p (s c) -> p s c", s=2),
                                     srcv, wk_up[name] + (wkb,), (wkb,),
                                     tag=f"ff_wul{tt}{name}{b}_{kc}")
                        for j in range(j0, j1):
                            jj = j - j0
                            bs = (j % 2) * 4
                            for ab in range(2):
                                def upfn(e, jj=jj, ab=ab, bs=bs, wbuf=wbuf,
                                         nw=(j1 - j0)):
                                    inst = None
                                    for kc in range(8):
                                        lhs = wbuf[:, kc,
                                                   ab * nw * 128 + jj * 128:
                                                   ab * nw * 128 + (jj + 1) * 128]
                                        for tc in range(2):
                                            inst = e.matmul(
                                                psb[bs + 2 * ab + tc][:, 0:512],
                                                lhs,
                                                xlnf[:, kc,
                                                     tc * 512:(tc + 1) * 512],
                                                start=(kc == 0), stop=(kc == 7))
                                    return inst
                                p.op("pe", upfn,
                                     reads=(wkb,) + XLNF,
                                     writes=(PS[bs + 2 * ab], PS[bs + 2 * ab + 1]),
                                     tag=f"ff_up{tt}{name}_{j}_{ab}")
                            if bias == "s1":
                                bcol_a = self.bias_s1[:, j:j + 1]
                                bcol_g = self.bias_s1[:, 16 + j:17 + j]
                                brd = ("bias_s1",)
                            else:
                                ei = int(bias[3:])
                                bcol_a = self.bias_e1[:, ei, j:j + 1]
                                bcol_g = self.bias_e1[:, ei, 24 + j:25 + j]
                                brd = (f"bias_{bias}",)
                            for tc in range(2):
                                p.op("act", lambda e, bs=bs, tc=tc, bg=bcol_g: e.activation(
                                    out=gsb[:, tc, :], in_=psb[bs + 2 + tc][:, 0:512],
                                    func=AF.Gelu, bias=bg),
                                     reads=(PS[bs + 2 + tc],) + brd,
                                     writes=(f"ff_gsb{tc}",),
                                     tag=f"ff_ge{tt}{name}_{j}_{tc}")
                                p.op("dve", lambda e, bs=bs, tc=tc, j=j, ba=bcol_a: e.scalar_tensor_tensor(
                                    out=act[:, j, tc * 512:(tc + 1) * 512],
                                    in0=psb[bs + tc][:, 0:512], scalar=ba,
                                    in1=gsb[:, tc, :], op0=ALU.add, op1=ALU.mult),
                                     reads=(PS[bs + tc], f"ff_gsb{tc}") + brd,
                                     writes=(f"ff_act{j}",),
                                     tag=f"ff_ae{tt}{name}_{j}_{tc}")
                else:
                    nband = ceil_div(nj, 6)
                    for b in range(nband):
                        j0 = b * 6
                        j1 = min(nj, j0 + 6)
                        wbuf, wkb = wup[b % 2], f"ff_wup{b % 2}"
                        for kc in range(8):
                            self.dma("sp",
                                     wbuf[:, kc, 0:(j1 - j0) * 128],
                                     wu[kc * 128:(kc + 1) * 128,
                                        j0 * 128:j1 * 128],
                                     wk_up[name] + (wkb,), (wkb,),
                                     tag=f"ff_wul{tt}{name}{b}_{kc}")
                        for j in range(j0, j1):
                            jj = j - j0
                            bs = (j % 4) * 2
                            def upfn(e, jj=jj, bs=bs, wbuf=wbuf):
                                inst = None
                                for kc in range(8):
                                    lhs = wbuf[:, kc, jj * 128:(jj + 1) * 128]
                                    for tc in range(2):
                                        inst = e.matmul(
                                            psb[bs + tc][:, 0:512], lhs,
                                            xlnf[:, kc, tc * 512:(tc + 1) * 512],
                                            start=(kc == 0), stop=(kc == 7))
                                return inst
                            p.op("pe", upfn, reads=(wkb,) + XLNF,
                                 writes=(PS[bs], PS[bs + 1]),
                                 tag=f"ff_up{tt}{name}_{j}")
                            for tc in range(2):
                                p.op("act", lambda e, bs=bs, tc=tc, j=j: e.activation(
                                    out=act[:, j, tc * 512:(tc + 1) * 512],
                                    in_=psb[bs + tc][:, 0:512], func=AF.Gelu),
                                     reads=(PS[bs + tc],),
                                     writes=(f"ff_act{j}",),
                                     tag=f"ff_ae{tt}{name}_{j}_{tc}")
                ACT_ALL = tuple(f"ff_act{j}" for j in range(nj))

                for half in range(2):
                    for kc in range(nj):
                        if kc % 4 == 0:
                            wb_, wkd = wdn[(kc // 4) % 2], f"ff_wdn{(kc // 4) % 2}"
                            self.dma("sp",
                                     wb_[:, 0:4, :],
                                     wd[kc * 128:(kc + 4) * 128, :].rearrange(
                                         "(k p) d -> p k d", p=128),
                                     wk_dn[name] + (wkd,), (wkd,),
                                     tag=f"ff_wdl{tt}{name}{half}_{kc}")
                        wb_, wkd = wdn[(kc // 4) % 2], f"ff_wdn{(kc // 4) % 2}"
                        first = (kc == 0)
                        last = (kc == nj - 1)
                        for tsub in range(4):
                            tg = half * 4 + tsub
                            def dnfn(e, kc=kc, tsub=tsub, tg=tg, wb_=wb_,
                                     first=first, last=last, name=name):
                                inst = None
                                if first and name == "d" and l == 0:
                                    inst = e.matmul(
                                        psb[2 * tsub][:, 0:512],
                                        self.ones_bf[0:1, 0:128],
                                        self.bs2_row[:, 0:512],
                                        start=True, stop=False)
                                    inst = e.matmul(
                                        psb[2 * tsub + 1][:, 0:512],
                                        self.ones_bf[0:1, 0:128],
                                        self.bs2_row[:, 512:1024],
                                        start=True, stop=False)
                                    inst = e.matmul(
                                        psb[2 * tsub][:, 0:512],
                                        gfm[:, tg * 128:(tg + 1) * 128],
                                        self.be2_sb[:, 0:512],
                                        start=False, stop=False)
                                    inst = e.matmul(
                                        psb[2 * tsub + 1][:, 0:512],
                                        gfm[:, tg * 128:(tg + 1) * 128],
                                        self.be2_sb[:, 512:1024],
                                        start=False, stop=False)
                                for dh in range(2):
                                    inst = e.matmul(
                                        psb[2 * tsub + dh][:, 0:512],
                                        act[:, kc, tg * 128:(tg + 1) * 128],
                                        wb_[:, kc % 4, dh * 512:(dh + 1) * 512],
                                        start=(first and not (name == "d" and l == 0)),
                                        stop=last)
                                return inst
                            extra = ("ones_bf", "bs2_row", "be2_sb", "ff_gfm") \
                                if (first and name == "d" and l == 0) else ()
                            p.op("pe", dnfn,
                                 reads=(wkd, f"ff_act{kc}") + extra,
                                 writes=(PS[2 * tsub], PS[2 * tsub + 1]),
                                 tag=f"ff_dn{tt}{name}_{half}_{kc}_{tsub}")
                    for tsub in range(4):
                        tg = half * 4 + tsub
                        for dh in range(2):
                            bk = 2 * tsub + dh
                            dst = yac[:, tg, dh * 512:(dh + 1) * 512]
                            if gate_e is None:
                                p.op("dve", lambda e, dst=dst, bk=bk: e.tensor_tensor(
                                    out=dst, in0=dst, in1=psb[bk][:, 0:512],
                                    op=ALU.add),
                                     reads=(PS[bk], "ff_yac"), writes=("ff_yac",),
                                     tag=f"ff_ev{tt}{name}_{tg}_{dh}")
                            else:
                                gcol = gts[:, tg, gate_e:gate_e + 1]
                                p.op("dve", lambda e, dst=dst, bk=bk, gcol=gcol: e.scalar_tensor_tensor(
                                    out=dst, in0=psb[bk][:, 0:512], scalar=gcol,
                                    in1=dst, op0=ALU.mult, op1=ALU.add),
                                     reads=(PS[bk], "ff_yac", "ff_gts"),
                                     writes=("ff_yac",),
                                     tag=f"ff_ev{tt}{name}_{tg}_{dh}")

            for ts in range(8):
                t1 = t0 + ts * 128
                rxb, krx = rx[ts % 2], f"ff_rx{ts % 2}"
                self.dma("sp", rxb[:], self.xres[t1:t1 + 128, :],
                         (krx,), (krx,), tag=f"ff_rld{tt}_{ts}")
                rnb, krn = rn[ts % 2], f"ff_rn{ts % 2}"
                if l == 0:
                    p.op("dve", lambda e, ts=ts, rxb=rxb, rnb=rnb: e.tensor_add(
                        rnb[:], yac[:, ts, :], rxb[:]),
                         reads=("ff_yac", krx, krn), writes=(krn,),
                         tag=f"ff_mg{tt}_{ts}")
                else:
                    ubb, kub = ub[ts % 2], f"ff_ub{ts % 2}"
                    self.dma("sp", ubb[:], self.xln[t1:t1 + 128, :],
                             (kub,), (kub,), tag=f"ff_uld{tt}_{ts}")
                    p.op("dve", lambda e, ts=ts, rxb=rxb, rnb=rnb: e.tensor_add(
                        rnb[:], yac[:, ts, :], rxb[:]),
                         reads=("ff_yac", krx, krn), writes=(krn,),
                         tag=f"ff_mg{tt}_{ts}")
                    p.op("dve", lambda e, rnb=rnb, ubb=ubb: e.tensor_add(
                        rnb[:], rnb[:], ubb[:]),
                         reads=(krn, kub), writes=(krn,),
                         tag=f"ff_mgu{tt}_{ts}")
                if l == 0:
                    self.dma("sp", self.xres[t1:t1 + 128, :], rnb[:],
                             (krn,), (f"ff_rst{tt}_{ts}",), tag=f"ff_rs{tt}_{ts}")
                    xob, kxo = xo[ts % 2], f"ff_xo{ts % 2}"
                    self.ln_tile_ops("ffL", tt * 8 + ts, rnb, krn,
                                     stats[:, (tt * 8 + ts) * 8:(tt * 8 + ts) * 8 + 8],
                                     xob, kxo, 0)
                    self.dma("sp", self.xln[t1:t1 + 128, :], xob[:],
                             (kxo,), (f"ff_xst{tt}_{ts}",), tag=f"ff_xs{tt}_{ts}")
                else:
                    self.dma("sp", self.out[t1:t1 + 128, :], rnb[:],
                             (krn,), (f"ff_ost{tt}_{ts}",), tag=f"ff_os{tt}_{ts}")

    def build(self, phases):
        self.declare_io()
        if "prepass" in phases:
            self.prepass()
        self.consts()
        self.p.barrier()
        if "ln1" in phases:
            self.ln1()
            self.p.barrier()
        if "attn0" in phases:
            self.attn(0)
            self.p.barrier()
        if "proj0" in phases:
            self.proj(0)
            self.p.barrier()
        if "ffn0" in phases:
            self.ffn(0)
            self.p.barrier()
        if "attn1" in phases:
            self.attn(1)
            self.p.barrier()
        if "proj1" in phases:
            self.proj(1)
            self.p.barrier()
        if "ffn1" in phases:
            self.ffn(1)
            self.p.barrier()
        self.p.finalize(self.nc, self.sems)
        return self.nc


ALL_PHASES = ("prepass", "ln1", "attn0", "proj0", "ffn0", "attn1", "proj1",
              "ffn1")


_LAST_MK = []


def build_nc(phases=ALL_PHASES, debug_dump=()):
    mk = MK(debug_dump=debug_dump)
    _LAST_MK.clear()
    _LAST_MK.append(mk)
    mk.build(phases)
    return mk.nc


def kernel(**inputs):
    nc = build_nc()
    xs = np.ascontiguousarray(inputs["L0_attn_x"])
    base = {k: np.ascontiguousarray(v) for k, v in inputs.items()
            if k != "L0_attn_x"}
    in_maps = []
    for c in range(8):
        m = dict(base)
        m["x"] = np.ascontiguousarray(xs[c])
        in_maps.append(m)
    res = run_bass_kernel_spmd(nc, in_maps, core_ids=list(range(8)))
    return np.stack([res.results[c]["out"] for c in range(8)], axis=0)


# version 06
# speedup vs baseline: 1.3758x; 1.1091x over previous
import numpy as np
import ml_dtypes
import concourse.bass as bass
import concourse.mybir as mybir
from concourse.bass_utils import run_bass_kernel_spmd

F32 = mybir.dt.float32
BF16 = mybir.dt.bfloat16
AF = mybir.ActivationFunctionType
ALU = mybir.AluOpType
AX = mybir.AxisListType

T = 4096
D = 1024
NH = 16
DH = 64
NEG = -1.0e30
SB_LIMIT = 212800

ENG_COMPUTE_SEM = {"pe": "c_pe", "act": "c_act", "dve": "c_dve", "pool": "c_pool"}
ENG_QUEUE_SEM = {"sp": "q_sp", "pool": "q_pool", "act": "q_act"}
ALL_SEMS = ("c_pe", "c_act", "c_dve", "c_pool", "q_sp", "q_pool", "q_act")


class _Op:
    __slots__ = ("eng", "fn", "reads", "writes", "dma", "idx", "needs_inc",
                 "sem", "cnt", "waits", "vis", "tag", "deps", "bar_waits")

    def __init__(self, eng, fn, reads, writes, dma, idx, vis, tag):
        self.eng = eng
        self.fn = fn
        self.reads = tuple(reads)
        self.writes = tuple(writes)
        self.dma = dma
        self.idx = idx
        self.needs_inc = dma
        self.sem = None
        self.cnt = None
        self.waits = None
        self.vis = vis
        self.tag = tag
        self.deps = ()
        self.bar_waits = ()


PARANOID = [False]


class Prog:

    def __init__(self):
        self.ops = []
        self.barriers = []
        self._writer = {}
        self._readers = {}

    def op(self, eng, fn, reads=(), writes=(), dma=False, vis=True, tag=""):
        o = _Op(eng, fn, reads, writes, dma, len(self.ops), vis, tag)
        deps = set()
        for k in o.reads:
            w = self._writer.get(k)
            if w is not None:
                deps.add(w)
        for k in o.writes:
            w = self._writer.get(k)
            if w is not None:
                deps.add(w)
            for r in self._readers.get(k, ()):
                deps.add(r)
        deps.discard(o.idx)
        if PARANOID[0] and self.ops:
            deps.add(len(self.ops) - 1)
        real = []
        for d in deps:
            dop = self.ops[d]
            if ((not dop.dma) and dop.eng == eng == "pe" and not o.dma
                    and not PARANOID[0]):
                continue
            real.append(d)
        o.deps = tuple(real)
        for d in real:
            self.ops[d].needs_inc = True
        for k in o.reads:
            self._readers.setdefault(k, []).append(o.idx)
        for k in o.writes:
            self._writer[k] = o.idx
            self._readers[k] = []
        self.ops.append(o)
        return o

    def barrier(self, keep_prefix="W:"):
        self.barriers.append(len(self.ops))
        self._writer = {k: v for k, v in self._writer.items()
                        if k.startswith(keep_prefix)}
        self._readers = {k: v for k, v in self._readers.items()
                         if k.startswith(keep_prefix)}

    def finalize(self, nc, sems):
        counts = {name: 0 for name in sems}
        for o in self.ops:
            if o.dma:
                o.sem = ENG_QUEUE_SEM[o.eng]
                counts[o.sem] += 16
                o.cnt = counts[o.sem]
            elif o.needs_inc:
                o.sem = ENG_COMPUTE_SEM[o.eng]
                counts[o.sem] += 1
                o.cnt = counts[o.sem]

        bar_targets = []
        vis_counts = {name: 0 for name in sems}
        bi = 0
        for o in self.ops:
            while bi < len(self.barriers) and self.barriers[bi] == o.idx:
                bar_targets.append(dict(vis_counts))
                bi += 1
            if o.sem is not None and o.vis:
                vis_counts[o.sem] = o.cnt
        while bi < len(self.barriers):
            bar_targets.append(dict(vis_counts))
            bi += 1

        eng_clock = {e: {} for e in ("pe", "act", "dve", "pool", "sp")}
        post = [None] * len(self.ops)
        bar_done = [set() for _ in self.barriers]
        for o in self.ops:
            clock = eng_clock[o.eng]
            bw = []
            for b in range(len(self.barriers)):
                if self.barriers[b] <= o.idx and o.eng not in bar_done[b]:
                    bar_done[b].add(o.eng)
                    for s, c in bar_targets[b].items():
                        if c > clock.get(s, 0):
                            bw.append((s, c))
                            clock[s] = c
            o.bar_waits = tuple(bw)
            waits = []
            for d in sorted(o.deps):
                dop = self.ops[d]
                s, c = dop.sem, dop.cnt
                assert s is not None, (o.tag, dop.tag)
                if c > clock.get(s, 0):
                    waits.append((s, c))
                    clock[s] = c
                    for s2, c2 in post[d].items():
                        if c2 > clock.get(s2, 0):
                            clock[s2] = c2
            o.waits = tuple(waits)
            post[o.idx] = dict(clock)

        by_eng = {e: [] for e in ("pe", "act", "dve", "pool", "sp")}
        for o in self.ops:
            by_eng[o.eng].append(o)

        with nc.Block() as block:
            for eng_name, section in (("sp", block.sync), ("pool", block.gpsimd),
                                      ("pe", block.tensor), ("act", block.scalar),
                                      ("dve", block.vector)):
                def emit(eng, _eng=eng_name):
                    for o in by_eng[_eng]:
                        for s, c in o.bar_waits:
                            eng.wait_ge(sems[s], c)
                        for s, c in o.waits:
                            eng.wait_ge(sems[s], c)
                        inst = o.fn(eng)
                        if o.sem is not None:
                            assert inst is not None, o.tag
                            inst.then_inc(sems[o.sem], 16 if o.dma else 1)
                    for s in ALL_SEMS:
                        if counts[s] > 0:
                            eng.wait_ge(sems[s], counts[s])
                section(emit)
        return counts


class Salloc:

    _uid = [0]

    def __init__(self, nc, base=None, limit=None):
        self.nc = nc
        self.cur = nc.sbuf_base if base is None else base
        self.limit = nc.sbuf_top if limit is None else limit

    def alloc(self, name, shape, dtype, align=64):
        per_part = int(np.prod(shape[1:])) * mybir.dt.size(dtype)
        self.cur = (self.cur + align - 1) // align * align
        Salloc._uid[0] += 1
        t = self.nc.alloc_sbuf_tensor_at(f"{name}_{Salloc._uid[0]}", list(shape),
                                         dtype, offset=self.cur)
        self.cur += per_part
        assert self.cur <= self.limit, (name, shape, self.cur, self.limit)
        return t


def ceil_div(a, b):
    return (a + b - 1) // b


class MK:
    def __init__(self, debug_dump=()):
        self.nc = bass.Bass("TRN2", target_bir_lowering=False, debug=False)
        self.p = Prog()
        self.debug_dump = set(debug_dump)
        self._key = [0]
        self._sem_guards = []

    def K(self, pfx):
        self._key[0] += 1
        return f"{pfx}#{self._key[0]}"

    def declare_io(self):
        nc = self.nc

        def ein(name, shape, dtype=F32):
            return nc.dram_tensor(name, list(shape), dtype,
                                  kind="ExternalInput").ap()

        self.x_in = ein("x", [T, D])
        self.w_attn0 = ein("L0_attn_w_attn", [D, 3 * D])
        self.w_proj0 = ein("L0_attn_w_proj", [D, D])
        self.ws1 = ein("L0_ffn_Ws1", [D, 4096])
        self.bs1 = ein("L0_ffn_bs1", [4096])
        self.ws2 = ein("L0_ffn_Ws2", [2048, D])
        self.bs2 = ein("L0_ffn_bs2", [D])
        self.we1 = ein("L0_ffn_We1", [12, D, 6144])
        self.be1 = ein("L0_ffn_be1", [12, 6144])
        self.we2 = ein("L0_ffn_We2", [12, 3072, D])
        self.be2 = ein("L0_ffn_be2", [12, D])
        self.wr0 = ein("L0_ffn_Wr", [D, 12])
        self.br0 = ein("L0_ffn_br", [12])
        self.wq1 = ein("L1_attn_Wq", [NH, D, DH])
        self.wk1 = ein("L1_attn_Wk", [NH, D, DH])
        self.wv1 = ein("L1_attn_Wv", [NH, D, DH])
        self.wo1 = ein("L1_attn_Wo", [D, D])
        self.gamma1 = ein("L1_attn_gamma", [D])
        self.beta1 = ein("L1_attn_beta", [D])
        self.shup = ein("L1_ffn_shared_up", [1, D, 4096])
        self.shdn = ein("L1_ffn_shared_down", [1, 4096, D])
        self.roup = ein("L1_ffn_routed_up", [8, D, 4096])
        self.rodn = ein("L1_ffn_routed_down", [8, 4096, D])
        self.wr1 = ein("L1_ffn_router_w", [D, 8])

        self.out = nc.dram_tensor("out", [T, D], F32, kind="ExternalOutput").ap()

        def scratch(name, shape, dtype):
            kind = "ExternalOutput" if name in self.debug_dump else "Internal"
            return nc.dram_tensor(name, list(shape), dtype, kind=kind).ap()

        self.xres = scratch("xres", [T, D], F32)
        self.xln = scratch("xln", [T, D], BF16)
        self.heads = scratch("heads", [D, T], BF16)
        self.watt_b = [scratch("watt0b", [D, 3 * D], BF16),
                       scratch("watt1b", [D, 3 * D], BF16)]
        self.wproj_b = [scratch("wproj0b", [D, D], BF16),
                        scratch("wproj1b", [D, D], BF16)]
        self.ws1_b = scratch("ws1b", [D, 4096], BF16)
        self.ws2_b = scratch("ws2b", [2048, D], BF16)
        self.we1_b = scratch("we1b", [12, D, 6144], BF16)
        self.we2_b = scratch("we2b", [12, 3072, D], BF16)
        self.shup_b = scratch("shupb", [D, 4096], BF16)
        self.shdn_b = scratch("shdnb", [4096, D], BF16)
        self.roup_b = scratch("roupb", [8, D, 4096], BF16)
        self.rodn_b = scratch("rodnb", [8, 4096, D], BF16)
        self.wr_b = [scratch("wr0b", [D, 12], BF16),
                     scratch("wr1b", [D, 8], BF16)]
        self.br0_b = scratch("br0b", [1, 12], BF16)
        self.bs2_b = scratch("bs2b", [1, D], BF16)
        self.be2_b = scratch("be2b", [12, D], BF16)

        idf = np.eye(128, dtype=np.float32)
        tri = np.where(np.arange(128)[:, None] >= np.arange(128)[None, :],
                       np.float32(0.0), np.float32(NEG))
        self.idf_c = nc.inline_tensor(idf, name="ident_f32")
        self.maskA_c = nc.inline_tensor(tri, name="maskA_t")
        self.maskB_c = nc.inline_tensor(np.ascontiguousarray(tri.T), name="maskB_t")
        self.ones_bf_c = nc.inline_tensor(
            np.ones((1, 512), dtype=ml_dtypes.bfloat16), name="ones_bf_t")
        self.ones_f_c = nc.inline_tensor(
            np.ones((1, 512), dtype=np.float32), name="ones_f_t")
        cv = np.tile(np.array([[1e-5, 1e-3, 0.0, 0.0]], dtype=np.float32),
                     (128, 1))
        self.cvals_c = nc.inline_tensor(cv, name="cvals_t")

        sal = Salloc(nc)
        self.identf = sal.alloc("identf", [128, 128], F32)
        self.maskA = sal.alloc("maskA", [128, 128], F32)
        self.maskB = sal.alloc("maskB", [128, 128], F32)
        self.ones_bf = sal.alloc("ones_bf", [1, 512], BF16)
        self.ones_f = sal.alloc("ones_f", [1, 512], F32)
        self.cvals = sal.alloc("cvals", [128, 4], F32)
        self.bias_s1 = sal.alloc("bias_s1", [128, 32], F32)
        self.bias_s2 = sal.alloc("bias_s2", [128, 8], F32)
        self.bias_e1 = sal.alloc("bias_e1", [128, 12, 48], F32)
        self.gam_bc = sal.alloc("gam_bc", [128, D], F32)
        self.bet_bc = sal.alloc("bet_bc", [128, D], F32)
        self.br_row = sal.alloc("br_row", [1, 12], BF16)
        self.bs2_row = sal.alloc("bs2_row", [1, D], BF16)
        self.be2_sb = sal.alloc("be2_sb", [12, D], BF16)
        self.wr_sb = [sal.alloc("wr0_sb", [128, 8, 12], BF16),
                      sal.alloc("wr1_sb", [128, 8, 8], BF16)]
        self.SCRATCH_BASE = sal.cur

        self.psw = [nc.alloc_psum_tensor(f"psw{i}", [128, 1024], F32)
                    for i in range(4)]
        self.psb = [self.psw[i // 2][:, (i % 2) * 512:(i % 2) * 512 + 512]
                    for i in range(8)]
        self.PS = [f"ps{i}" for i in range(8)]

        self.sems = {}
        for name in ALL_SEMS:
            g = nc.semaphore(name)
            self.sems[name] = g.__enter__()
            self._sem_guards.append(g)

    def dma(self, eng, dst, src, reads, writes, vis=True, tag=""):
        self.p.op(eng, lambda e, d=dst, s=src: e.dma_start(out=d, in_=s),
                  reads=reads, writes=writes, dma=True, vis=vis, tag=tag)

    def dma_t(self, dst, src, reads, writes, tag=""):
        self.p.op("sp", lambda e, d=dst, s=src: e.dma_start(out=d, in_=s,
                                                            transpose=True),
                  reads=reads, writes=writes, dma=True, tag=tag)

    def prepass(self):
        def cast(dst_ap, src_ap, key, tag):
            n = int(np.prod(src_ap.shape))
            s = src_ap.flatten() if len(src_ap.shape) > 1 else src_ap
            d = dst_ap.flatten() if len(dst_ap.shape) > 1 else dst_ap
            CH = 1 << 23
            nch = ceil_div(n, CH)
            for i in range(nch):
                lo, hi = i * CH, min(n, (i + 1) * CH)
                self.dma("pool", d[lo:hi], s[lo:hi], (),
                         (key,) if i == nch - 1 else (), vis=False, tag=tag)

        cast(self.watt_b[0], self.w_attn0, "W:watt0", "pp_watt0")
        cast(self.wproj_b[0], self.w_proj0, "W:wproj0", "pp_wproj0")
        cast(self.ws1_b, self.ws1, "W:ws1", "pp_ws1")
        cast(self.ws2_b, self.ws2, "W:ws2", "pp_ws2")
        cast(self.wr_b[0], self.wr0, "W:wr0", "pp_wr0")
        cast(self.br0_b, self.br0.unsqueeze(0), "W:br0", "pp_br0")
        cast(self.be2_b, self.be2, "W:be2", "pp_be2")
        cast(self.bs2_b, self.bs2.unsqueeze(0), "W:bs2", "pp_bs2")
        for e in range(12):
            cast(self.we1_b[e], self.we1[e], f"W:we1_{e}", f"pp_we1_{e}")
            cast(self.we2_b[e], self.we2[e], f"W:we2_{e}", f"pp_we2_{e}")
        for i, w in enumerate((self.wq1, self.wk1, self.wv1)):
            dst = self.watt_b[1][:, i * D:(i + 1) * D]
            dstv = dst.rearrange("d (h j) -> h d j", j=DH)
            for h in range(NH):
                self.dma("pool", dstv[h], w[h], (),
                         (f"W:watt1_{i}",) if h == NH - 1 else (), vis=False,
                         tag=f"pp_watt1_{i}_{h}")
        cast(self.wproj_b[1], self.wo1, "W:wproj1", "pp_wproj1")
        cast(self.wr_b[1], self.wr1, "W:wr1", "pp_wr1")
        cast(self.shup_b, self.shup[0], "W:shup", "pp_shup")
        cast(self.shdn_b, self.shdn[0], "W:shdn", "pp_shdn")
        for e in range(8):
            cast(self.roup_b[e], self.roup[e], f"W:roup_{e}", f"pp_roup_{e}")
            cast(self.rodn_b[e], self.rodn[e], f"W:rodn_{e}", f"pp_rodn_{e}")

    def consts(self):
        p, psb, PS = self.p, self.psb, self.PS
        self.dma("sp", self.identf[:], self.idf_c.ap(), (), ("identf",), tag="c_id")
        self.dma("sp", self.maskA[:], self.maskA_c.ap(), (), ("maskA",), tag="c_mA")
        self.dma("sp", self.maskB[:], self.maskB_c.ap(), (), ("maskB",), tag="c_mB")
        self.dma("sp", self.ones_bf[:], self.ones_bf_c.ap(), (), ("ones_bf",), tag="c_1b")
        self.dma("sp", self.ones_f[:], self.ones_f_c.ap(), (), ("ones_f",), tag="c_1f")
        self.dma("sp", self.cvals[:], self.cvals_c.ap(), (), ("cvals",), tag="c_cv")
        self.dma("sp", self.br_row[:], self.br0_b, ("W:br0",), ("br_row",), tag="c_br")
        self.dma("sp", self.bs2_row[:], self.bs2_b, ("W:bs2",), ("bs2_row",), tag="c_bs2r")
        self.dma("sp", self.be2_sb[:], self.be2_b, ("W:be2",), ("be2_sb",), tag="c_be2")
        for l in range(2):
            self.dma("sp", self.wr_sb[l][:],
                     self.wr_b[l].rearrange("(c p) e -> p c e", p=128),
                     (f"W:wr{l}",), (f"wr_sb{l}",), tag=f"c_wr{l}")

        sal = Salloc(self.nc, self.SCRATCH_BASE)
        bstage = sal.alloc("bstage", [64, 128], F32)

        def bias_prep(src_flat, n_rows, dst, dst_key, tag):
            assert n_rows <= 64
            self.dma("sp", bstage[0:n_rows, :],
                     src_flat.rearrange("(a b) -> a b", b=128),
                     ("c_bstage",), ("c_bstage",), tag=tag + "_ld")
            p.op("pe", lambda e, n=n_rows: e.transpose(
                psb[6][0:128, 0:n], bstage[0:n, :], self.identf[0:n, 0:n]),
                 reads=("c_bstage", "identf"), writes=(PS[6],), tag=tag + "_tr")
            p.op("dve", lambda e, d=dst, n=n_rows: e.tensor_copy(
                d, psb[6][0:128, 0:n]),
                 reads=(PS[6],), writes=(dst_key,), tag=tag + "_ev")

        bias_prep(self.bs1, 32, self.bias_s1[:], "bias_s1", "c_bs1")
        for e in range(12):
            bias_prep(self.be1[e], 48, self.bias_e1[:, e, :], f"bias_e1_{e}",
                      f"c_be1_{e}")

        gb = sal.alloc("gb_stage", [1, 2 * D], F32)
        self.dma("sp", gb[:, 0:D], self.gamma1.unsqueeze(0), (), ("gbs_g",), tag="c_g")
        self.dma("sp", gb[:, D:2 * D], self.beta1.unsqueeze(0), (), ("gbs_b",), tag="c_b")
        for i, (dst, key, rkey) in enumerate(((self.gam_bc, "gam_bc", "gbs_g"),
                                              (self.bet_bc, "bet_bc", "gbs_b"))):
            for c in range(2):
                sl = slice(i * D + c * 512, i * D + (c + 1) * 512)
                b = 4 + c
                p.op("pe", lambda e, sl=sl, b=b: e.matmul(
                    psb[b][:, 0:512], self.ones_f[0:1, 0:128], gb[:, sl],
                    start=True, stop=True),
                     reads=(rkey, "ones_f"), writes=(PS[b],), tag=f"c_gb{i}{c}")
                p.op("dve", lambda e, dst=dst, c=c, b=b: e.tensor_copy(
                    dst[:, c * 512:(c + 1) * 512], psb[b][:, 0:512]),
                     reads=(PS[b],), writes=(f"{key}{c}",), tag=f"c_gbe{i}{c}")

    def ln_tile_ops(self, tag, t, rx, rx_key, stats, xout, xout_key, eps_col,
                    extra_reads=()):
        p, K = self.p, self.K
        sq = self._ln_sq
        s1, s2 = stats[:, 0:1], stats[:, 1:2]
        mean, rr = stats[:, 2:3], stats[:, 3:4]
        negmr, tmp = stats[:, 4:5], stats[:, 5:6]
        ks = K(f"{tag}_stats{t}")
        p.op("act", lambda e: e.activation(out=sq[:], in_=rx[:], func=AF.Square,
                                           accum_out=s2),
             reads=(rx_key,) + tuple(extra_reads), writes=(ks + "s2", "_lnsq"),
             tag=f"{tag}_sq{t}")
        p.op("dve", lambda e: e.reduce_sum(s1, rx[:], axis=AX.X),
             reads=(rx_key,), writes=(ks + "s1",), tag=f"{tag}_s1{t}")
        p.op("dve", lambda e: e.tensor_scalar(out=mean, in0=s1, scalar1=1.0 / D,
                                              scalar2=None, op0=ALU.mult),
             reads=(ks + "s1",), writes=(ks + "mean",), tag=f"{tag}_mn{t}")
        p.op("dve", lambda e: e.tensor_mul(tmp, mean, mean),
             reads=(ks + "mean",), writes=(ks + "m2",), tag=f"{tag}_m2{t}")
        p.op("dve", lambda e: e.scalar_tensor_tensor(
            out=tmp, in0=s2, scalar=1.0 / D, in1=tmp,
            op0=ALU.mult, op1=ALU.subtract),
             reads=(ks + "s2", ks + "m2"), writes=(ks + "var",), tag=f"{tag}_va{t}")
        p.op("act", lambda e: e.activation(out=rr, in_=tmp, func=AF.Sqrt,
                                           bias=self.cvals[:, eps_col:eps_col + 1]),
             reads=(ks + "var", "cvals"), writes=(ks + "sd",), tag=f"{tag}_sd{t}")
        p.op("dve", lambda e: e.reciprocal(rr, rr),
             reads=(ks + "sd",), writes=(ks + "r",), tag=f"{tag}_r{t}")
        p.op("dve", lambda e: e.scalar_tensor_tensor(
            out=negmr, in0=mean, scalar=-1.0, in1=rr, op0=ALU.mult, op1=ALU.mult),
             reads=(ks + "mean", ks + "r"), writes=(ks + "nmr",), tag=f"{tag}_nm{t}")
        p.op("act", lambda e: e.activation(out=xout[:], in_=rx[:], func=AF.Identity,
                                           bias=negmr, scale=rr),
             reads=(rx_key, ks + "nmr", ks + "r"), writes=(xout_key,),
             tag=f"{tag}_ap{t}")

    def ln1(self):
        p = self.p
        sal = Salloc(self.nc, self.SCRATCH_BASE)
        rx = [sal.alloc("ln1_rx", [128, D], F32) for _ in range(3)]
        xo = [sal.alloc("ln1_xo", [128, D], BF16) for _ in range(3)]
        self._ln_sq = sal.alloc("ln1_sq", [128, D], BF16)
        stats = sal.alloc("ln1_stats", [128, 8 * 32], F32)
        for t in range(32):
            b, ob = rx[t % 3], xo[t % 3]
            kb, ko = f"ln1_rx{t % 3}", f"ln1_xo{t % 3}"
            self.dma("sp", b[:], self.x_in[t * 128:(t + 1) * 128, :],
                     (kb,), (kb,), tag=f"ln1_ld{t}")
            self.ln_tile_ops("ln1", t, b, kb, stats[:, t * 8:(t + 1) * 8],
                             ob, ko, 0)
            self.dma("sp", self.xln[t * 128:(t + 1) * 128, :], ob[:],
                     (ko,), (ko,), tag=f"ln1_st{t}")

    def attn(self, l):
        p, psb, PS, K = self.p, self.psb, self.PS, self.K
        nc = self.nc
        sal = Salloc(nc, self.SCRATCH_BASE)
        watt2 = [sal.alloc(f"at_watt{i}", [128, 8, 384], BF16) for i in range(2)]
        xlnf = sal.alloc("at_xlnf", [128, 8, T], BF16)
        qb = [sal.alloc(f"at_qb{i}", [66, T], BF16) for i in range(2)]
        kb = [sal.alloc(f"at_kb{i}", [66, T], BF16) for i in range(2)]
        vsb = sal.alloc("at_vsb", [128, 32, 2, 65], BF16)
        att = [sal.alloc(f"at_att{i}", [128, 1024], BF16) for i in range(3)]
        hsb = [sal.alloc(f"at_hsb{i}", [64, 1024], BF16) for i in range(2)]
        qsq = [sal.alloc(f"at_qsq{i}", [64, T], BF16) for i in range(2)]
        kmx = sal.alloc("at_kmx", [65, 16], F32)
        stmp = sal.alloc("at_stmp", [65, 2, 1024], F32)
        rbc = [sal.alloc(f"at_rbc{i}", [64, 1024], F32) for i in range(2)]
        oc_b = sal.alloc("at_ocb", [65, 64], BF16)
        oc_f = sal.alloc("at_ocf", [65, 64], F32)

        wb = self.watt_b[l]
        wkeys = (("W:watt0",) if l == 0 else
                 ("W:watt1_0", "W:watt1_1", "W:watt1_2"))

        for kc in range(8):
            self.dma_t(xlnf[:, kc, :], self.xln[:, kc * 128:(kc + 1) * 128],
                       (), (f"at_xlnf{kc}",), tag=f"at_xld{kc}")
        XLNF_ALL = tuple(f"at_xlnf{kc}" for kc in range(8))
        for i in range(2):
            p.op("pool", lambda e, i=i: e.memset(kb[i][64:66, :], 0.0),
                 reads=(), writes=(f"at_kones{i}",), tag=f"at_ko{i}")
            p.op("pool", lambda e, i=i: e.memset(kb[i][64:65, :], -0.51),
                 reads=(f"at_kones{i}",), writes=(f"at_kones{i}",),
                 tag=f"at_ko2{i}")
            p.op("pool", lambda e, i=i: e.memset(qb[i][64:66, :], 0.0),
                 reads=(), writes=(f"at_qz{i}",), tag=f"at_qzm{i}")
        p.op("pool", lambda e: e.memset(vsb[:, :, :, 64:65], 1.0),
             reads=(), writes=("at_vones",), tag="at_vo")
        p.op("pool", lambda e: e.memset(oc_b[:], 1.0),
             reads=(), writes=("at_ocb",), tag="at_ocb")
        p.op("pool", lambda e: e.memset(oc_f[:], 1.0),
             reads=(), writes=("at_ocf",), tag="at_ocf")

        for g in range(8):
            wt = watt2[g % 2]
            WATT_ALL = tuple(f"at_w{g % 2}_{kc}" for kc in range(8))
            for kc in range(8):
                srcv = wb[kc * 128:(kc + 1) * 128, :].rearrange(
                    "p (s g c) -> p s g c", s=3, c=128)[:, :, g, :]
                self.dma("sp", wt[:, kc, :].rearrange("p (s c) -> p s c", c=128),
                         srcv, wkeys, (f"at_w{g % 2}_{kc}",),
                         tag=f"at_wld{g}_{kc}")
            for i in range(2):
                h = 2 * g + i
                for x, colbase, dstb, dstk in ((0, i * DH, qb[i], f"at_q{i}"),
                                               (1, 128 + i * DH, kb[i], f"at_k{i}")):
                    for tt in range(8):
                        b = 4 + (tt % 4)
                        def mmfn(e, cb=colbase, tt=tt, b=b, wt=wt):
                            inst = None
                            for kc in range(8):
                                inst = e.matmul(
                                    psb[b][0:64, 0:512],
                                    wt[:, kc, cb:cb + DH],
                                    xlnf[:, kc, tt * 512:(tt + 1) * 512],
                                    start=(kc == 0), stop=(kc == 7))
                            return inst
                        p.op("pe", mmfn, reads=WATT_ALL + XLNF_ALL,
                             writes=(PS[b],), tag=f"at_qk{g}{i}{x}{tt}")
                        p.op("dve", lambda e, dstb=dstb, tt=tt, b=b: e.tensor_copy(
                            dstb[0:64, tt * 512:(tt + 1) * 512],
                            psb[b][0:64, 0:512]),
                             reads=(PS[b],), writes=(f"{dstk}_{tt}",),
                             tag=f"at_qke{g}{i}{x}{tt}")
            for i in range(2):
                h = 2 * g + i
                QROWS = tuple(f"at_q{i}_{tt}" for tt in range(8))
                KROWS = tuple(f"at_k{i}_{tt}" for tt in range(8))
                sq = qsq[i]
                p.op("pool", lambda e, i=i, sq=sq: e.tensor_mul(
                    sq[0:64, :], kb[i][0:64, :], kb[i][0:64, :]),
                     reads=KROWS, writes=(f"at_sq{i}",), tag=f"at_ksq{h}")
                for c in range(8):
                    bb = 6 + (c % 2)
                    p.op("pe", lambda e, sq=sq, c=c, bb=bb: e.matmul(
                        psb[bb][64:65, 0:512], oc_b[0:64, 0:1],
                        sq[0:64, c * 512:(c + 1) * 512],
                        start=True, stop=True),
                         reads=(f"at_sq{i}", "at_ocb"), writes=(PS[bb],),
                         tag=f"at_kn{h}_{c}")
                    p.op("dve", lambda e, c=c, bb=bb: e.reduce_max(
                        kmx[64:65, c:c + 1], psb[bb][64:65, 0:512], axis=AX.X),
                         reads=(PS[bb],), writes=(f"at_kmx{c}",),
                         tag=f"at_knm{h}_{c}")
                p.op("dve", lambda e: e.reduce_max(
                    kmx[64:65, 8:9], kmx[64:65, 0:8], axis=AX.X),
                     reads=tuple(f"at_kmx{c}" for c in range(8)),
                     writes=("at_kmxf",), tag=f"at_kmf{h}")
                p.op("pool", lambda e, i=i, sq=sq: e.tensor_mul(
                    sq[0:64, :], qb[i][0:64, :], qb[i][0:64, :]),
                     reads=QROWS, writes=(f"at_sq{i}",), tag=f"at_qsq{h}")
                for c in range(8):
                    bb = 6 + (c % 2)
                    p.op("pe", lambda e, sq=sq, c=c, bb=bb: e.matmul(
                        psb[bb][64:65, 0:512], oc_b[0:64, 0:1],
                        sq[0:64, c * 512:(c + 1) * 512],
                        start=True, stop=True),
                         reads=(f"at_sq{i}", "at_ocb"), writes=(PS[bb],),
                         tag=f"at_qn{h}_{c}")
                    p.op("dve", lambda e, i=i, c=c, bb=bb: e.tensor_scalar(
                        out=qb[i][64:65, c * 512:(c + 1) * 512],
                        in0=psb[bb][64:65, 0:512], scalar1=kmx[64:65, 8:9],
                        scalar2=None, op0=ALU.add),
                         reads=(PS[bb], "at_kmxf", f"at_qz{i}"),
                         writes=(f"at_qrow{i}",), tag=f"at_qnr{h}_{c}")
            for ts in range(32):
                b = ts % 4
                def vfn(e, ts=ts, b=b, wt=wt):
                    inst = None
                    for kc in range(8):
                        inst = e.matmul(
                            psb[b][:, 0:128],
                            xlnf[:, kc, ts * 128:(ts + 1) * 128],
                            wt[:, kc, 256:384],
                            start=(kc == 0), stop=(kc == 7))
                    return inst
                p.op("pe", vfn, reads=WATT_ALL + XLNF_ALL, writes=(PS[b],),
                     tag=f"at_v{g}{ts}")
                p.op("dve", lambda e, ts=ts, b=b: e.tensor_copy(
                    vsb[:, ts, :, 0:64],
                    psb[b][:, 0:128].rearrange("p (h c) -> p h c", c=64)),
                     reads=(PS[b],), writes=(f"at_v{ts}",), tag=f"at_ve{g}{ts}")
            VSB_ALL = tuple(f"at_v{ts}" for ts in range(32))

            for i in range(2):
                h = 2 * g + i
                QROWS = tuple(f"at_q{i}_{tt}" for tt in range(8))
                KROWS = tuple(f"at_k{i}_{tt}" for tt in range(8))
                for pr in range(4):
                    qce, qco = 2 * pr, 2 * pr + 1
                    av = self.psw[2]
                    AVK = (PS[4], PS[5])
                    def emit_av(kt):
                        wide = kt <= 8 * pr + 3
                        a = kt % 3
                        def avfn(e, i=i, kt=kt, pr=pr, a=a, wide=wide, av=av):
                            inst = None
                            if wide:
                                inst = e.matmul(
                                    av[0:65, 0:512],
                                    vsb[:, kt, i, 0:65],
                                    att[a][:, 0:512],
                                    start=(kt == 0), stop=(kt == 8 * pr + 3))
                            inst = e.matmul(
                                av[0:65, 512:1024],
                                vsb[:, kt, i, 0:65],
                                att[a][:, 512:1024],
                                start=(kt == 0), stop=(kt == 8 * pr + 7))
                            return inst
                        p.op("pe", avfn,
                             reads=(f"at_att{a}", "at_vones") + VSB_ALL,
                             writes=AVK, tag=f"at_av{h}_{pr}_{kt}")

                    for kt in range(8 * pr + 8):
                        wide = kt <= 8 * pr + 3
                        sc = self.psw[kt % 2]
                        SCK = (PS[2 * (kt % 2)], PS[2 * (kt % 2) + 1])
                        a = kt % 3
                        def scfn(e, i=i, kt=kt, qce=qce, wide=wide, sc=sc):
                            inst = None
                            if wide:
                                inst = e.matmul(
                                    sc[:, 0:512],
                                    kb[i][0:66, kt * 128:(kt + 1) * 128],
                                    qb[i][0:66, qce * 512:(qce + 1) * 512],
                                    start=True, stop=True)
                            inst = e.matmul(
                                sc[:, 512:1024],
                                kb[i][0:66, kt * 128:(kt + 1) * 128],
                                qb[i][0:66, (qce + 1) * 512:(qce + 2) * 512],
                                start=True, stop=True)
                            return inst
                        p.op("pe", scfn,
                             reads=KROWS + QROWS + (f"at_qrow{i}", f"at_kones{i}"),
                             writes=SCK, tag=f"at_pb{h}_{pr}_{kt}")
                        dge = (kt // 4 == qce)
                        dgo = (kt // 4 == qco)
                        if dge:
                            j0 = (kt % 4) * 128
                            p.op("dve", lambda e, sc=sc, j0=j0: e.tensor_add(
                                sc[:, j0:j0 + 128], sc[:, j0:j0 + 128],
                                self.maskB[:]),
                                 reads=SCK + ("maskB",), writes=SCK,
                                 tag=f"at_pbm{h}_{pr}_{kt}")
                            if j0 > 0:
                                p.op("dve", lambda e, a=a, j0=j0: e.memset(
                                    att[a][:, 0:j0], 0.0),
                                     reads=(), writes=(f"at_att{a}",),
                                     tag=f"at_pbz{h}_{pr}_{kt}")
                            p.op("act", lambda e, a=a, sc=sc, j0=j0: e.activation(
                                out=att[a][:, j0:1024], in_=sc[:, j0:1024],
                                func=AF.Exp, scale=0.125),
                                 reads=SCK, writes=(f"at_att{a}",),
                                 tag=f"at_pbe{h}_{pr}_{kt}")
                        elif dgo:
                            j0 = 512 + (kt % 4) * 128
                            p.op("dve", lambda e, sc=sc, j0=j0: e.tensor_add(
                                sc[:, j0:j0 + 128], sc[:, j0:j0 + 128],
                                self.maskB[:]),
                                 reads=SCK + ("maskB",), writes=SCK,
                                 tag=f"at_pbm{h}_{pr}_{kt}")
                            if j0 > 512:
                                p.op("dve", lambda e, a=a, j0=j0: e.memset(
                                    att[a][:, 512:j0], 0.0),
                                     reads=(), writes=(f"at_att{a}",),
                                     tag=f"at_pbz{h}_{pr}_{kt}")
                            p.op("act", lambda e, a=a, sc=sc, j0=j0: e.activation(
                                out=att[a][:, j0:1024], in_=sc[:, j0:1024],
                                func=AF.Exp, scale=0.125),
                                 reads=SCK, writes=(f"at_att{a}",),
                                 tag=f"at_pbe{h}_{pr}_{kt}")
                        else:
                            lo = 0 if wide else 512
                            p.op("act", lambda e, a=a, sc=sc, lo=lo: e.activation(
                                out=att[a][:, lo:1024], in_=sc[:, lo:1024],
                                func=AF.Exp, scale=0.125),
                                 reads=SCK, writes=(f"at_att{a}",),
                                 tag=f"at_pbe{h}_{pr}_{kt}")
                        if kt >= 1:
                            emit_av(kt - 1)
                    emit_av(8 * pr + 7)
                    p.op("dve", lambda e, av=av, pr=pr: e.reciprocal(
                        stmp[64:65, pr % 2, :], av[64:65, 0:1024]),
                         reads=AVK, writes=(f"at_rcp{pr % 2}",),
                         tag=f"at_rc{h}_{pr}")
                    p.op("pe", lambda e, pr=pr: (
                        e.matmul(psb[6][0:64, 0:512], oc_f[64:65, 0:64],
                                 stmp[64:65, pr % 2, 0:512],
                                 start=True, stop=True),
                        e.matmul(psb[7][0:64, 0:512], oc_f[64:65, 0:64],
                                 stmp[64:65, pr % 2, 512:1024],
                                 start=True, stop=True))[-1],
                         reads=(f"at_rcp{pr % 2}", "at_ocf"),
                         writes=(PS[6], PS[7]), tag=f"at_rb{h}_{pr}")
                    p.op("dve", lambda e, pr=pr: e.tensor_copy(
                        rbc[pr % 2][0:64, 0:512], psb[6][0:64, 0:512]),
                         reads=(PS[6],), writes=(f"at_rbc{pr % 2}a",),
                         tag=f"at_rbe{h}_{pr}a")
                    p.op("dve", lambda e, pr=pr: e.tensor_copy(
                        rbc[pr % 2][0:64, 512:1024], psb[7][0:64, 0:512]),
                         reads=(PS[7],), writes=(f"at_rbc{pr % 2}b",),
                         tag=f"at_rbe{h}_{pr}b")
                    p.op("dve", lambda e, av=av, pr=pr: e.tensor_tensor(
                        out=hsb[pr % 2][:, 0:1024], in0=av[0:64, 0:1024],
                        in1=rbc[pr % 2][0:64, 0:1024], op=ALU.mult),
                         reads=AVK + (f"at_rbc{pr % 2}a", f"at_rbc{pr % 2}b"),
                         writes=(f"at_hsb{pr % 2}",), tag=f"at_he{h}_{pr}")
                    self.dma("sp",
                             self.heads[h * 64:(h + 1) * 64,
                                        pr * 1024:(pr + 1) * 1024],
                             hsb[pr % 2][:, 0:1024], (f"at_hsb{pr % 2}",),
                             (f"at_hst{h}_{pr}",), tag=f"at_hs{h}_{pr}")

    def proj(self, l):
        p, psb, PS = self.p, self.psb, self.PS
        nc = self.nc
        sal = Salloc(nc, self.SCRATCH_BASE)
        wp = sal.alloc("pj_wp", [128, 8, D], BF16)
        hb = [sal.alloc(f"pj_hb{i}", [128, 8, 128], BF16) for i in range(2)]
        rx = [sal.alloc(f"pj_rx{i}", [128, D], F32) for i in range(2)]
        ub = [sal.alloc(f"pj_ub{i}", [128, D], BF16) for i in range(2)]
        yb = [sal.alloc(f"pj_yb{i}", [128, D], F32) for i in range(2)]
        y2 = [sal.alloc(f"pj_y2{i}", [128, D], F32) for i in range(2)]
        rnew = [sal.alloc(f"pj_rn{i}", [128, D], F32) for i in range(2)]
        xo = [sal.alloc(f"pj_xo{i}", [128, D], BF16) for i in range(2)]
        self._ln_sq = sal.alloc("pj_sq", [128, D], BF16)
        stats = sal.alloc("pj_stats", [128, 8 * 32], F32)
        stats2 = sal.alloc("pj_stats2", [128, 8 * 32], F32)

        for kc in range(8):
            self.dma("sp", wp[:, kc, :],
                     self.wproj_b[l][kc * 128:(kc + 1) * 128, :],
                     (f"W:wproj{l}",), (f"pj_wp{kc}",), tag=f"pj_wld{kc}")
        WP_ALL = tuple(f"pj_wp{kc}" for kc in range(8))
        resid_src = self.x_in if l == 0 else self.xres

        for ts in range(32):
            hbb, khb = hb[ts % 2], f"pj_hb{ts % 2}"
            self.dma("sp", hbb[:],
                     self.heads.rearrange("(c p) t -> p c t", p=128)[
                         :, :, ts * 128:(ts + 1) * 128],
                     (), (khb,), tag=f"pj_hld{ts}")
            rxb, krx = rx[ts % 2], f"pj_rx{ts % 2}"
            self.dma("sp", rxb[:], resid_src[ts * 128:(ts + 1) * 128, :],
                     (), (krx,), tag=f"pj_rld{ts}")

            def mmfn(e, hbb=hbb):
                inst = None
                for kc in range(8):
                    for half in range(2):
                        inst = e.matmul(
                            psb[half][:, 0:512], hbb[:, kc, :],
                            wp[:, kc, half * 512:(half + 1) * 512],
                            start=(kc == 0), stop=(kc == 7))
                return inst
            p.op("pe", mmfn, reads=(khb,) + WP_ALL, writes=(PS[0], PS[1]),
                 tag=f"pj_mm{ts}")

            rnb, krn = rnew[ts % 2], f"pj_rn{ts % 2}"
            if l == 0:
                for half in range(2):
                    p.op("dve", lambda e, half=half, rxb=rxb, rnb=rnb: e.tensor_add(
                        rnb[:, half * 512:(half + 1) * 512],
                        psb[half][:, 0:512],
                        rxb[:, half * 512:(half + 1) * 512]),
                         reads=(PS[half], krx), writes=(krn + f"h{half}",),
                         tag=f"pj_add{ts}_{half}")
                rn_r = (krn + "h0", krn + "h1")
            else:
                ubb, kub = ub[ts % 2], f"pj_ub{ts % 2}"
                self.dma("sp", ubb[:], self.xln[ts * 128:(ts + 1) * 128, :],
                         (), (kub,), tag=f"pj_uld{ts}")
                ybb, kyb = yb[ts % 2], f"pj_yb{ts % 2}"
                y2b, ky2 = y2[ts % 2], f"pj_y2{ts % 2}"
                for half in range(2):
                    p.op("dve", lambda e, half=half, ubb=ubb, ybb=ybb: e.tensor_add(
                        ybb[:, half * 512:(half + 1) * 512],
                        psb[half][:, 0:512],
                        ubb[:, half * 512:(half + 1) * 512]),
                         reads=(PS[half], kub, kyb), writes=(kyb,),
                         tag=f"pj_yadd{ts}_{half}")
                self.ln_tile_ops("pjI", ts, ybb, kyb,
                                 stats[:, ts * 8:ts * 8 + 8], y2b, ky2, 1)
                p.op("dve", lambda e, y2b=y2b: e.tensor_mul(
                    y2b[:], y2b[:], self.gam_bc[:]),
                     reads=(ky2, "gam_bc0", "gam_bc1"),
                     writes=(ky2,), tag=f"pj_gm{ts}")
                p.op("dve", lambda e, y2b=y2b: e.tensor_add(
                    y2b[:], y2b[:], self.bet_bc[:]),
                     reads=(ky2, "bet_bc0", "bet_bc1"),
                     writes=(ky2,), tag=f"pj_bt{ts}")
                p.op("dve", lambda e, rxb=rxb, rnb=rnb, y2b=y2b: e.tensor_add(
                    rnb[:], y2b[:], rxb[:]),
                     reads=(ky2, krx), writes=(krn + "h0",),
                     tag=f"pj_add{ts}")
                rn_r = (krn + "h0",)
            self.dma("sp", self.xres[ts * 128:(ts + 1) * 128, :], rnb[:],
                     rn_r, (f"pj_rst{ts}",), tag=f"pj_rs{ts}")
            xob, kxo = xo[ts % 2], f"pj_xo{ts % 2}"
            self.ln_tile_ops("pjL", ts, rnb, rn_r[0],
                             stats2[:, ts * 8:ts * 8 + 8], xob, kxo, 0,
                             extra_reads=rn_r[1:])
            self.dma("sp", self.xln[ts * 128:(ts + 1) * 128, :], xob[:],
                     (kxo,), (f"pj_xst{ts}",), tag=f"pj_xs{ts}")

    def ffn(self, l):
        p, psb, PS, K = self.p, self.psb, self.PS, self.K
        nc = self.nc
        E = 12 if l == 0 else 8
        HA = 3072 if l == 0 else 4096
        NJ = HA // 128
        sal = Salloc(nc, self.SCRATCH_BASE)
        xlnf = sal.alloc("ff_xlnf", [128, 8, 1024], BF16)
        act = sal.alloc("ff_act", [128, 32 if l else 24, 1024], BF16)
        yac = sal.alloc("ff_yac", [128, 8, D], F32)
        wup = [sal.alloc(f"ff_wup{i}", [128, 8, 1536 if l == 0 else 768], BF16)
               for i in range(2)]
        wdn = [sal.alloc(f"ff_wdn{i}", [128, 4, D], BF16) for i in range(2)]
        gsb = sal.alloc("ff_gsb", [128, 2, 512], BF16)
        lgb = sal.alloc("ff_lgb", [128, 8, E], F32)
        gts = sal.alloc("ff_gts", [128, 8, E], F32)
        gsc = sal.alloc("ff_gsc", [128, 8 * 8], F32)
        gfm = sal.alloc("ff_gfm", [12, 1024], BF16) if l == 0 else None
        rx = [sal.alloc(f"ff_rx{i}", [128, D], F32) for i in range(2)]
        ub = [sal.alloc(f"ff_ub{i}", [128, D], BF16) for i in range(2)] \
            if l == 1 else [None, None]
        rn = [sal.alloc(f"ff_rn{i}", [128, D], F32) for i in range(2)]
        xo = [sal.alloc(f"ff_xo{i}", [128, D], BF16) for i in range(2)] \
            if l == 0 else [None, None]
        if l == 0:
            self._ln_sq = sal.alloc("ff_sq", [128, D], BF16)
            stats = sal.alloc("ff_stats", [128, 8 * 32], F32)
        else:
            stats = None

        if l == 0:
            jobs = [("d", self.ws1_b, self.ws2_b, 4096, 2048, None, "s1")]
            jobs += [(f"e{e}", self.we1_b[e], self.we2_b[e], 6144, 3072, e,
                      f"e1_{e}") for e in range(12)]
            wk_up = {"d": ("W:ws1",)}
            wk_dn = {"d": ("W:ws2",)}
            for e in range(12):
                wk_up[f"e{e}"] = (f"W:we1_{e}",)
                wk_dn[f"e{e}"] = (f"W:we2_{e}",)
            geglu = True
        else:
            jobs = [("d", self.shup_b, self.shdn_b, 4096, 4096, None, None)]
            jobs += [(f"e{e}", self.roup_b[e], self.rodn_b[e], 4096, 4096, e,
                      None) for e in range(8)]
            wk_up = {"d": ("W:shup",)}
            wk_dn = {"d": ("W:shdn",)}
            for e in range(8):
                wk_up[f"e{e}"] = (f"W:roup_{e}",)
                wk_dn[f"e{e}"] = (f"W:rodn_{e}",)
            geglu = False

        for tt in range(4):
            t0 = tt * 1024
            for kc in range(8):
                self.dma_t(xlnf[:, kc, :],
                           self.xln[t0:t0 + 1024, kc * 128:(kc + 1) * 128],
                           (f"ff_xlnf{kc}",), (f"ff_xlnf{kc}",),
                           tag=f"ff_xld{tt}_{kc}")
            XLNF = tuple(f"ff_xlnf{kc}" for kc in range(8))

            for ts in range(8):
                def lgfn(e, ts=ts, l=l):
                    inst = None
                    for kc in range(8):
                        inst = e.matmul(
                            psb[6][:, 0:E],
                            xlnf[:, kc, ts * 128:(ts + 1) * 128],
                            self.wr_sb[l][:, kc, :],
                            start=(kc == 0), stop=(kc == 7 and l != 0))
                    if l == 0:
                        inst = e.matmul(psb[6][:, 0:E], self.ones_bf[0:1, 0:128],
                                        self.br_row[:], start=False, stop=True)
                    return inst
                p.op("pe", lgfn, reads=XLNF + (f"wr_sb{l}", "ones_bf", "br_row"),
                     writes=(PS[6],), tag=f"ff_lg{tt}_{ts}")
                p.op("act", lambda e, ts=ts: e.activation(
                    out=lgb[:, ts, :], in_=psb[6][:, 0:E], func=AF.Identity,
                    scale=(1.0 / 1.5) if l == 0 else 1.0),
                     reads=(PS[6],), writes=(f"ff_lg{ts}",), tag=f"ff_lge{tt}_{ts}")
            LG = tuple(f"ff_lg{ts}" for ts in range(8))
            v1 = gsc[:, 0:8]
            v2 = gsc[:, 8:16]
            w1 = gsc[:, 16:24]
            w2 = gsc[:, 24:32]
            ssum = gsc[:, 32:40]
            if l == 0:
                p.op("dve", lambda e: e.tensor_reduce(
                    v1, lgb[:], axis=AX.X, op=ALU.max),
                     reads=LG, writes=("ff_v1",), tag=f"ff_r1{tt}")
                p.op("dve", lambda e: e.tensor_tensor(
                    out=gts[:], in0=lgb[:],
                    in1=v1.unsqueeze(2).broadcast_to([128, 8, E]),
                    op=ALU.is_equal),
                     reads=LG + ("ff_v1",), writes=("ff_m1",), tag=f"ff_m1{tt}")
                p.op("dve", lambda e: e.scalar_tensor_tensor(
                    out=lgb[:], in0=gts[:], scalar=NEG, in1=lgb[:],
                    op0=ALU.mult, op1=ALU.add),
                     reads=("ff_m1",) + LG, writes=("ff_msk",), tag=f"ff_mk{tt}")
                p.op("dve", lambda e: e.tensor_reduce(
                    v2, lgb[:], axis=AX.X, op=ALU.max),
                     reads=("ff_msk",), writes=("ff_v2",), tag=f"ff_r2{tt}")
                p.op("dve", lambda e: e.tensor_sub(w1, v1, v2),
                     reads=("ff_v1", "ff_v2"), writes=("ff_d12",), tag=f"ff_d{tt}")
                p.op("act", lambda e: e.activation(out=w1, in_=w1, func=AF.Sigmoid),
                     reads=("ff_d12",), writes=("ff_w1",), tag=f"ff_s1{tt}")
                p.op("dve", lambda e: e.tensor_scalar(
                    out=w2, in0=w1, scalar1=-1.0, scalar2=1.0,
                    op0=ALU.mult, op1=ALU.add),
                     reads=("ff_w1",), writes=("ff_w2",), tag=f"ff_s2{tt}")
                p.op("dve", lambda e: e.tensor_tensor(
                    out=lgb[:], in0=lgb[:],
                    in1=v2.unsqueeze(2).broadcast_to([128, 8, E]),
                    op=ALU.is_equal),
                     reads=("ff_msk", "ff_v2"), writes=("ff_m2",), tag=f"ff_m2{tt}")
                p.op("dve", lambda e: e.tensor_mul(
                    gts[:], gts[:],
                    w1.unsqueeze(2).broadcast_to([128, 8, E])),
                     reads=("ff_m1", "ff_w1"), writes=("ff_g1",), tag=f"ff_g1{tt}")
                p.op("dve", lambda e: e.tensor_mul(
                    lgb[:], lgb[:],
                    w2.unsqueeze(2).broadcast_to([128, 8, E])),
                     reads=("ff_m2", "ff_w2"), writes=("ff_g2",), tag=f"ff_g2{tt}")
                p.op("dve", lambda e: e.tensor_add(gts[:], gts[:], lgb[:]),
                     reads=("ff_g1", "ff_g2"), writes=("ff_gts",), tag=f"ff_gs{tt}")
                for ts in range(8):
                    p.op("pe", lambda e, ts=ts: e.transpose(
                        psb[7][0:E, 0:128], gts[:, ts, :], self.identf[:]),
                         reads=("ff_gts", "identf"), writes=(PS[7],),
                         tag=f"ff_gt{tt}_{ts}")
                    p.op("dve", lambda e, ts=ts: e.tensor_copy(
                        gfm[:, ts * 128:(ts + 1) * 128], psb[7][0:E, 0:128]),
                         reads=(PS[7],), writes=("ff_gfm",), tag=f"ff_gf{tt}_{ts}")
            else:
                p.op("dve", lambda e: e.tensor_reduce(
                    v1, lgb[:], axis=AX.X, op=ALU.max),
                     reads=LG, writes=("ff_v1",), tag=f"ff_r1{tt}")
                p.op("dve", lambda e: e.tensor_tensor(
                    out=lgb[:], in0=lgb[:],
                    in1=v1.unsqueeze(2).broadcast_to([128, 8, E]),
                    op=ALU.subtract),
                     reads=LG + ("ff_v1",), writes=("ff_sh",), tag=f"ff_sh{tt}")
                p.op("act", lambda e: e.activation(out=lgb[:], in_=lgb[:],
                                                   func=AF.Exp),
                     reads=("ff_sh",), writes=("ff_ex",), tag=f"ff_ex{tt}")
                p.op("dve", lambda e: e.tensor_reduce(
                    ssum, lgb[:], axis=AX.X, op=ALU.add),
                     reads=("ff_ex",), writes=("ff_ss",), tag=f"ff_ss{tt}")
                p.op("dve", lambda e: e.reciprocal(ssum, ssum),
                     reads=("ff_ss",), writes=("ff_rs",), tag=f"ff_rs{tt}")
                p.op("dve", lambda e: e.tensor_mul(
                    lgb[:], lgb[:], ssum.unsqueeze(2).broadcast_to([128, 8, E])),
                     reads=("ff_ex", "ff_rs"), writes=("ff_pr",), tag=f"ff_pr{tt}")
                p.op("dve", lambda e: e.tensor_reduce(
                    v1, lgb[:], axis=AX.X, op=ALU.max),
                     reads=("ff_pr",), writes=("ff_p1",), tag=f"ff_p1{tt}")
                p.op("dve", lambda e: e.tensor_tensor(
                    out=gts[:], in0=lgb[:],
                    in1=v1.unsqueeze(2).broadcast_to([128, 8, E]),
                    op=ALU.is_equal),
                     reads=("ff_pr", "ff_p1"), writes=("ff_m1",), tag=f"ff_m1{tt}")
                p.op("dve", lambda e: e.scalar_tensor_tensor(
                    out=lgb[:], in0=gts[:], scalar=-2.0, in1=lgb[:],
                    op0=ALU.mult, op1=ALU.add),
                     reads=("ff_m1", "ff_pr"), writes=("ff_msk",), tag=f"ff_mk{tt}")
                p.op("dve", lambda e: e.tensor_reduce(
                    v2, lgb[:], axis=AX.X, op=ALU.max),
                     reads=("ff_msk",), writes=("ff_v2",), tag=f"ff_r2{tt}")
                p.op("dve", lambda e: e.tensor_tensor(
                    out=lgb[:], in0=lgb[:],
                    in1=v2.unsqueeze(2).broadcast_to([128, 8, E]),
                    op=ALU.is_equal),
                     reads=("ff_msk", "ff_v2"), writes=("ff_m2",), tag=f"ff_m2{tt}")
                p.op("dve", lambda e: e.tensor_mul(
                    gts[:], gts[:], v1.unsqueeze(2).broadcast_to([128, 8, E])),
                     reads=("ff_m1", "ff_p1"), writes=("ff_g1",), tag=f"ff_g1{tt}")
                p.op("dve", lambda e: e.tensor_mul(
                    lgb[:], lgb[:], v2.unsqueeze(2).broadcast_to([128, 8, E])),
                     reads=("ff_m2", "ff_v2"), writes=("ff_g2",), tag=f"ff_g2{tt}")
                p.op("dve", lambda e: e.tensor_add(gts[:], gts[:], lgb[:]),
                     reads=("ff_g1", "ff_g2"), writes=("ff_gts",), tag=f"ff_gs{tt}")

            p.op("dve", lambda e: e.memset(yac[:], 0.0),
                 reads=(), writes=("ff_yac",), tag=f"ff_yz{tt}")

            for (name, wu, wd, HB, HAe, gate_e, bias) in jobs:
                nj = HAe // 128
                if geglu:
                    nband = ceil_div(nj, 6)
                    for b in range(nband):
                        j0 = b * 6
                        j1 = min(nj, j0 + 6)
                        wbuf, wkb = wup[b % 2], f"ff_wup{b % 2}"
                        for kc in range(8):
                            srcv = wu[kc * 128:(kc + 1) * 128, :].rearrange(
                                "p (s c) -> p s c", s=2)[
                                :, :, j0 * 128:j1 * 128]
                            self.dma("sp",
                                     wbuf[:, kc, 0:2 * (j1 - j0) * 128].rearrange(
                                         "p (s c) -> p s c", s=2),
                                     srcv, wk_up[name] + (wkb,), (wkb,),
                                     tag=f"ff_wul{tt}{name}{b}_{kc}")
                        for j in range(j0, j1):
                            jj = j - j0
                            bs = (j % 2) * 4
                            for ab in range(2):
                                def upfn(e, jj=jj, ab=ab, bs=bs, wbuf=wbuf,
                                         nw=(j1 - j0)):
                                    inst = None
                                    for kc in range(8):
                                        lhs = wbuf[:, kc,
                                                   ab * nw * 128 + jj * 128:
                                                   ab * nw * 128 + (jj + 1) * 128]
                                        for tc in range(2):
                                            inst = e.matmul(
                                                psb[bs + 2 * ab + tc][:, 0:512],
                                                lhs,
                                                xlnf[:, kc,
                                                     tc * 512:(tc + 1) * 512],
                                                start=(kc == 0), stop=(kc == 7))
                                    return inst
                                p.op("pe", upfn,
                                     reads=(wkb,) + XLNF,
                                     writes=(PS[bs + 2 * ab], PS[bs + 2 * ab + 1]),
                                     tag=f"ff_up{tt}{name}_{j}_{ab}")
                            if bias == "s1":
                                bcol_a = self.bias_s1[:, j:j + 1]
                                bcol_g = self.bias_s1[:, 16 + j:17 + j]
                                brd = ("bias_s1",)
                            else:
                                ei = int(bias[3:])
                                bcol_a = self.bias_e1[:, ei, j:j + 1]
                                bcol_g = self.bias_e1[:, ei, 24 + j:25 + j]
                                brd = (f"bias_{bias}",)
                            for tc in range(2):
                                p.op("act", lambda e, bs=bs, tc=tc, bg=bcol_g: e.activation(
                                    out=gsb[:, tc, :], in_=psb[bs + 2 + tc][:, 0:512],
                                    func=AF.Gelu, bias=bg),
                                     reads=(PS[bs + 2 + tc],) + brd,
                                     writes=(f"ff_gsb{tc}",),
                                     tag=f"ff_ge{tt}{name}_{j}_{tc}")
                                p.op("dve", lambda e, bs=bs, tc=tc, j=j, ba=bcol_a: e.scalar_tensor_tensor(
                                    out=act[:, j, tc * 512:(tc + 1) * 512],
                                    in0=psb[bs + tc][:, 0:512], scalar=ba,
                                    in1=gsb[:, tc, :], op0=ALU.add, op1=ALU.mult),
                                     reads=(PS[bs + tc], f"ff_gsb{tc}") + brd,
                                     writes=(f"ff_act{j}",),
                                     tag=f"ff_ae{tt}{name}_{j}_{tc}")
                else:
                    nband = ceil_div(nj, 6)
                    for b in range(nband):
                        j0 = b * 6
                        j1 = min(nj, j0 + 6)
                        wbuf, wkb = wup[b % 2], f"ff_wup{b % 2}"
                        for kc in range(8):
                            self.dma("sp",
                                     wbuf[:, kc, 0:(j1 - j0) * 128],
                                     wu[kc * 128:(kc + 1) * 128,
                                        j0 * 128:j1 * 128],
                                     wk_up[name] + (wkb,), (wkb,),
                                     tag=f"ff_wul{tt}{name}{b}_{kc}")
                        for j in range(j0, j1):
                            jj = j - j0
                            bs = (j % 4) * 2
                            def upfn(e, jj=jj, bs=bs, wbuf=wbuf):
                                inst = None
                                for kc in range(8):
                                    lhs = wbuf[:, kc, jj * 128:(jj + 1) * 128]
                                    for tc in range(2):
                                        inst = e.matmul(
                                            psb[bs + tc][:, 0:512], lhs,
                                            xlnf[:, kc, tc * 512:(tc + 1) * 512],
                                            start=(kc == 0), stop=(kc == 7))
                                return inst
                            p.op("pe", upfn, reads=(wkb,) + XLNF,
                                 writes=(PS[bs], PS[bs + 1]),
                                 tag=f"ff_up{tt}{name}_{j}")
                            for tc in range(2):
                                p.op("act", lambda e, bs=bs, tc=tc, j=j: e.activation(
                                    out=act[:, j, tc * 512:(tc + 1) * 512],
                                    in_=psb[bs + tc][:, 0:512], func=AF.Gelu),
                                     reads=(PS[bs + tc],),
                                     writes=(f"ff_act{j}",),
                                     tag=f"ff_ae{tt}{name}_{j}_{tc}")
                ACT_ALL = tuple(f"ff_act{j}" for j in range(nj))

                for half in range(2):
                    for kc in range(nj):
                        if kc % 4 == 0:
                            wb_, wkd = wdn[(kc // 4) % 2], f"ff_wdn{(kc // 4) % 2}"
                            self.dma("sp",
                                     wb_[:, 0:4, :],
                                     wd[kc * 128:(kc + 4) * 128, :].rearrange(
                                         "(k p) d -> p k d", p=128),
                                     wk_dn[name] + (wkd,), (wkd,),
                                     tag=f"ff_wdl{tt}{name}{half}_{kc}")
                        wb_, wkd = wdn[(kc // 4) % 2], f"ff_wdn{(kc // 4) % 2}"
                        first = (kc == 0)
                        last = (kc == nj - 1)
                        for tsub in range(4):
                            tg = half * 4 + tsub
                            def dnfn(e, kc=kc, tsub=tsub, tg=tg, wb_=wb_,
                                     first=first, last=last, name=name):
                                inst = None
                                if first and name == "d" and l == 0:
                                    inst = e.matmul(
                                        psb[2 * tsub][:, 0:512],
                                        self.ones_bf[0:1, 0:128],
                                        self.bs2_row[:, 0:512],
                                        start=True, stop=False)
                                    inst = e.matmul(
                                        psb[2 * tsub + 1][:, 0:512],
                                        self.ones_bf[0:1, 0:128],
                                        self.bs2_row[:, 512:1024],
                                        start=True, stop=False)
                                    inst = e.matmul(
                                        psb[2 * tsub][:, 0:512],
                                        gfm[:, tg * 128:(tg + 1) * 128],
                                        self.be2_sb[:, 0:512],
                                        start=False, stop=False)
                                    inst = e.matmul(
                                        psb[2 * tsub + 1][:, 0:512],
                                        gfm[:, tg * 128:(tg + 1) * 128],
                                        self.be2_sb[:, 512:1024],
                                        start=False, stop=False)
                                for dh in range(2):
                                    inst = e.matmul(
                                        psb[2 * tsub + dh][:, 0:512],
                                        act[:, kc, tg * 128:(tg + 1) * 128],
                                        wb_[:, kc % 4, dh * 512:(dh + 1) * 512],
                                        start=(first and not (name == "d" and l == 0)),
                                        stop=last)
                                return inst
                            extra = ("ones_bf", "bs2_row", "be2_sb", "ff_gfm") \
                                if (first and name == "d" and l == 0) else ()
                            p.op("pe", dnfn,
                                 reads=(wkd, f"ff_act{kc}") + extra,
                                 writes=(PS[2 * tsub], PS[2 * tsub + 1]),
                                 tag=f"ff_dn{tt}{name}_{half}_{kc}_{tsub}")
                    for tsub in range(4):
                        tg = half * 4 + tsub
                        for dh in range(2):
                            bk = 2 * tsub + dh
                            dst = yac[:, tg, dh * 512:(dh + 1) * 512]
                            if gate_e is None:
                                p.op("dve", lambda e, dst=dst, bk=bk: e.tensor_tensor(
                                    out=dst, in0=dst, in1=psb[bk][:, 0:512],
                                    op=ALU.add),
                                     reads=(PS[bk], "ff_yac"), writes=("ff_yac",),
                                     tag=f"ff_ev{tt}{name}_{tg}_{dh}")
                            else:
                                gcol = gts[:, tg, gate_e:gate_e + 1]
                                p.op("dve", lambda e, dst=dst, bk=bk, gcol=gcol: e.scalar_tensor_tensor(
                                    out=dst, in0=psb[bk][:, 0:512], scalar=gcol,
                                    in1=dst, op0=ALU.mult, op1=ALU.add),
                                     reads=(PS[bk], "ff_yac", "ff_gts"),
                                     writes=("ff_yac",),
                                     tag=f"ff_ev{tt}{name}_{tg}_{dh}")

            for ts in range(8):
                t1 = t0 + ts * 128
                rxb, krx = rx[ts % 2], f"ff_rx{ts % 2}"
                self.dma("sp", rxb[:], self.xres[t1:t1 + 128, :],
                         (krx,), (krx,), tag=f"ff_rld{tt}_{ts}")
                rnb, krn = rn[ts % 2], f"ff_rn{ts % 2}"
                if l == 0:
                    p.op("dve", lambda e, ts=ts, rxb=rxb, rnb=rnb: e.tensor_add(
                        rnb[:], yac[:, ts, :], rxb[:]),
                         reads=("ff_yac", krx, krn), writes=(krn,),
                         tag=f"ff_mg{tt}_{ts}")
                else:
                    ubb, kub = ub[ts % 2], f"ff_ub{ts % 2}"
                    self.dma("sp", ubb[:], self.xln[t1:t1 + 128, :],
                             (kub,), (kub,), tag=f"ff_uld{tt}_{ts}")
                    p.op("dve", lambda e, ts=ts, rxb=rxb, rnb=rnb: e.tensor_add(
                        rnb[:], yac[:, ts, :], rxb[:]),
                         reads=("ff_yac", krx, krn), writes=(krn,),
                         tag=f"ff_mg{tt}_{ts}")
                    p.op("dve", lambda e, rnb=rnb, ubb=ubb: e.tensor_add(
                        rnb[:], rnb[:], ubb[:]),
                         reads=(krn, kub), writes=(krn,),
                         tag=f"ff_mgu{tt}_{ts}")
                if l == 0:
                    self.dma("sp", self.xres[t1:t1 + 128, :], rnb[:],
                             (krn,), (f"ff_rst{tt}_{ts}",), tag=f"ff_rs{tt}_{ts}")
                    xob, kxo = xo[ts % 2], f"ff_xo{ts % 2}"
                    self.ln_tile_ops("ffL", tt * 8 + ts, rnb, krn,
                                     stats[:, (tt * 8 + ts) * 8:(tt * 8 + ts) * 8 + 8],
                                     xob, kxo, 0)
                    self.dma("sp", self.xln[t1:t1 + 128, :], xob[:],
                             (kxo,), (f"ff_xst{tt}_{ts}",), tag=f"ff_xs{tt}_{ts}")
                else:
                    self.dma("sp", self.out[t1:t1 + 128, :], rnb[:],
                             (krn,), (f"ff_ost{tt}_{ts}",), tag=f"ff_os{tt}_{ts}")

    def build(self, phases):
        self.declare_io()
        if "prepass" in phases:
            self.prepass()
        self.consts()
        self.p.barrier()
        if "ln1" in phases:
            self.ln1()
            self.p.barrier()
        if "attn0" in phases:
            self.attn(0)
            self.p.barrier()
        if "proj0" in phases:
            self.proj(0)
            self.p.barrier()
        if "ffn0" in phases:
            self.ffn(0)
            self.p.barrier()
        if "attn1" in phases:
            self.attn(1)
            self.p.barrier()
        if "proj1" in phases:
            self.proj(1)
            self.p.barrier()
        if "ffn1" in phases:
            self.ffn(1)
            self.p.barrier()
        self.p.finalize(self.nc, self.sems)
        return self.nc


ALL_PHASES = ("prepass", "ln1", "attn0", "proj0", "ffn0", "attn1", "proj1",
              "ffn1")


_LAST_MK = []


def build_nc(phases=ALL_PHASES, debug_dump=()):
    mk = MK(debug_dump=debug_dump)
    _LAST_MK.clear()
    _LAST_MK.append(mk)
    mk.build(phases)
    return mk.nc


def kernel(**inputs):
    nc = build_nc()
    xs = np.ascontiguousarray(inputs["L0_attn_x"])
    base = {k: np.ascontiguousarray(v) for k, v in inputs.items()
            if k != "L0_attn_x"}
    in_maps = []
    for c in range(8):
        m = dict(base)
        m["x"] = np.ascontiguousarray(xs[c])
        in_maps.append(m)
    res = run_bass_kernel_spmd(nc, in_maps, core_ids=list(range(8)))
    return np.stack([res.results[c]["out"] for c in range(8)], axis=0)


# version 07
# speedup vs baseline: 1.5144x; 1.1008x over previous
import numpy as np
import ml_dtypes
import concourse.bass as bass
import concourse.mybir as mybir
from concourse.bass_utils import run_bass_kernel_spmd

F32 = mybir.dt.float32
BF16 = mybir.dt.bfloat16
AF = mybir.ActivationFunctionType
ALU = mybir.AluOpType
AX = mybir.AxisListType

T = 4096
D = 1024
NH = 16
DH = 64
NEG = -1.0e30
SB_LIMIT = 212800

ENG_COMPUTE_SEM = {"pe": "c_pe", "act": "c_act", "dve": "c_dve", "pool": "c_pool"}
ENG_QUEUE_SEM = {"sp": "q_sp", "pool": "q_pool", "act": "q_act"}
ALL_SEMS = ("c_pe", "c_act", "c_dve", "c_pool", "q_sp", "q_pool", "q_act")


class _Op:
    __slots__ = ("eng", "fn", "reads", "writes", "dma", "idx", "needs_inc",
                 "sem", "cnt", "waits", "vis", "tag", "deps", "bar_waits")

    def __init__(self, eng, fn, reads, writes, dma, idx, vis, tag):
        self.eng = eng
        self.fn = fn
        self.reads = tuple(reads)
        self.writes = tuple(writes)
        self.dma = dma
        self.idx = idx
        self.needs_inc = dma
        self.sem = None
        self.cnt = None
        self.waits = None
        self.vis = vis
        self.tag = tag
        self.deps = ()
        self.bar_waits = ()


PARANOID = [False]


class Prog:

    def __init__(self):
        self.ops = []
        self.barriers = []
        self._writer = {}
        self._readers = {}

    def op(self, eng, fn, reads=(), writes=(), dma=False, vis=True, tag=""):
        o = _Op(eng, fn, reads, writes, dma, len(self.ops), vis, tag)
        deps = set()
        for k in o.reads:
            w = self._writer.get(k)
            if w is not None:
                deps.add(w)
        for k in o.writes:
            w = self._writer.get(k)
            if w is not None:
                deps.add(w)
            for r in self._readers.get(k, ()):
                deps.add(r)
        deps.discard(o.idx)
        if PARANOID[0] and self.ops:
            deps.add(len(self.ops) - 1)
        real = []
        for d in deps:
            dop = self.ops[d]
            if ((not dop.dma) and dop.eng == eng == "pe" and not o.dma
                    and not PARANOID[0]):
                continue
            real.append(d)
        o.deps = tuple(real)
        for d in real:
            self.ops[d].needs_inc = True
        for k in o.reads:
            self._readers.setdefault(k, []).append(o.idx)
        for k in o.writes:
            self._writer[k] = o.idx
            self._readers[k] = []
        self.ops.append(o)
        return o

    def barrier(self, keep_prefix="W:"):
        self.barriers.append(len(self.ops))
        self._writer = {k: v for k, v in self._writer.items()
                        if k.startswith(keep_prefix)}
        self._readers = {k: v for k, v in self._readers.items()
                         if k.startswith(keep_prefix)}

    def finalize(self, nc, sems):
        counts = {name: 0 for name in sems}
        for o in self.ops:
            if o.dma:
                o.sem = ENG_QUEUE_SEM[o.eng]
                counts[o.sem] += 16
                o.cnt = counts[o.sem]
            elif o.needs_inc:
                o.sem = ENG_COMPUTE_SEM[o.eng]
                counts[o.sem] += 1
                o.cnt = counts[o.sem]

        bar_targets = []
        vis_counts = {name: 0 for name in sems}
        bi = 0
        for o in self.ops:
            while bi < len(self.barriers) and self.barriers[bi] == o.idx:
                bar_targets.append(dict(vis_counts))
                bi += 1
            if o.sem is not None and o.vis:
                vis_counts[o.sem] = o.cnt
        while bi < len(self.barriers):
            bar_targets.append(dict(vis_counts))
            bi += 1

        eng_clock = {e: {} for e in ("pe", "act", "dve", "pool", "sp")}
        post = [None] * len(self.ops)
        bar_done = [set() for _ in self.barriers]
        for o in self.ops:
            clock = eng_clock[o.eng]
            bw = []
            for b in range(len(self.barriers)):
                if self.barriers[b] <= o.idx and o.eng not in bar_done[b]:
                    bar_done[b].add(o.eng)
                    for s, c in bar_targets[b].items():
                        if c > clock.get(s, 0):
                            bw.append((s, c))
                            clock[s] = c
            o.bar_waits = tuple(bw)
            waits = []
            for d in sorted(o.deps):
                dop = self.ops[d]
                s, c = dop.sem, dop.cnt
                assert s is not None, (o.tag, dop.tag)
                if c > clock.get(s, 0):
                    waits.append((s, c))
                    clock[s] = c
                    for s2, c2 in post[d].items():
                        if c2 > clock.get(s2, 0):
                            clock[s2] = c2
            o.waits = tuple(waits)
            post[o.idx] = dict(clock)

        by_eng = {e: [] for e in ("pe", "act", "dve", "pool", "sp")}
        for o in self.ops:
            by_eng[o.eng].append(o)

        with nc.Block() as block:
            for eng_name, section in (("sp", block.sync), ("pool", block.gpsimd),
                                      ("pe", block.tensor), ("act", block.scalar),
                                      ("dve", block.vector)):
                def emit(eng, _eng=eng_name):
                    for o in by_eng[_eng]:
                        for s, c in o.bar_waits:
                            eng.wait_ge(sems[s], c)
                        for s, c in o.waits:
                            eng.wait_ge(sems[s], c)
                        inst = o.fn(eng)
                        if o.sem is not None:
                            assert inst is not None, o.tag
                            inst.then_inc(sems[o.sem], 16 if o.dma else 1)
                    for s in ALL_SEMS:
                        if counts[s] > 0:
                            eng.wait_ge(sems[s], counts[s])
                section(emit)
        return counts


class Salloc:

    _uid = [0]

    def __init__(self, nc, base=None, limit=None):
        self.nc = nc
        self.cur = nc.sbuf_base if base is None else base
        self.limit = nc.sbuf_top if limit is None else limit

    def alloc(self, name, shape, dtype, align=64):
        per_part = int(np.prod(shape[1:])) * mybir.dt.size(dtype)
        self.cur = (self.cur + align - 1) // align * align
        Salloc._uid[0] += 1
        t = self.nc.alloc_sbuf_tensor_at(f"{name}_{Salloc._uid[0]}", list(shape),
                                         dtype, offset=self.cur)
        self.cur += per_part
        assert self.cur <= self.limit, (name, shape, self.cur, self.limit)
        return t


def ceil_div(a, b):
    return (a + b - 1) // b


class MK:
    def __init__(self, debug_dump=()):
        self.nc = bass.Bass("TRN2", target_bir_lowering=False, debug=False)
        self.p = Prog()
        self.debug_dump = set(debug_dump)
        self._key = [0]
        self._sem_guards = []

    def K(self, pfx):
        self._key[0] += 1
        return f"{pfx}#{self._key[0]}"

    def declare_io(self):
        nc = self.nc

        def ein(name, shape, dtype=F32):
            return nc.dram_tensor(name, list(shape), dtype,
                                  kind="ExternalInput").ap()

        self.x_in = ein("x", [T, D])
        self.w_attn0 = ein("L0_attn_w_attn", [D, 3 * D])
        self.w_proj0 = ein("L0_attn_w_proj", [D, D])
        self.ws1 = ein("L0_ffn_Ws1", [D, 4096])
        self.bs1 = ein("L0_ffn_bs1", [4096])
        self.ws2 = ein("L0_ffn_Ws2", [2048, D])
        self.bs2 = ein("L0_ffn_bs2", [D])
        self.we1 = ein("L0_ffn_We1", [12, D, 6144])
        self.be1 = ein("L0_ffn_be1", [12, 6144])
        self.we2 = ein("L0_ffn_We2", [12, 3072, D])
        self.be2 = ein("L0_ffn_be2", [12, D])
        self.wr0 = ein("L0_ffn_Wr", [D, 12])
        self.br0 = ein("L0_ffn_br", [12])
        self.wq1 = ein("L1_attn_Wq", [NH, D, DH])
        self.wk1 = ein("L1_attn_Wk", [NH, D, DH])
        self.wv1 = ein("L1_attn_Wv", [NH, D, DH])
        self.wo1 = ein("L1_attn_Wo", [D, D])
        self.gamma1 = ein("L1_attn_gamma", [D])
        self.beta1 = ein("L1_attn_beta", [D])
        self.shup = ein("L1_ffn_shared_up", [1, D, 4096])
        self.shdn = ein("L1_ffn_shared_down", [1, 4096, D])
        self.roup = ein("L1_ffn_routed_up", [8, D, 4096])
        self.rodn = ein("L1_ffn_routed_down", [8, 4096, D])
        self.wr1 = ein("L1_ffn_router_w", [D, 8])

        self.out = nc.dram_tensor("out", [T, D], F32, kind="ExternalOutput").ap()

        def scratch(name, shape, dtype):
            kind = "ExternalOutput" if name in self.debug_dump else "Internal"
            return nc.dram_tensor(name, list(shape), dtype, kind=kind).ap()

        self.xres = scratch("xres", [T, D], F32)
        self.xln = scratch("xln", [T, D], BF16)
        self.heads = scratch("heads", [D, T], BF16)
        self.watt_b = [scratch("watt0b", [D, 3 * D], BF16),
                       scratch("watt1b", [D, 3 * D], BF16)]
        self.wproj_b = [scratch("wproj0b", [D, D], BF16),
                        scratch("wproj1b", [D, D], BF16)]
        self.ws1_b = scratch("ws1b", [D, 4096], BF16)
        self.ws2_b = scratch("ws2b", [2048, D], BF16)
        self.we1_b = scratch("we1b", [12, D, 6144], BF16)
        self.we2_b = scratch("we2b", [12, 3072, D], BF16)
        self.shup_b = scratch("shupb", [D, 4096], BF16)
        self.shdn_b = scratch("shdnb", [4096, D], BF16)
        self.roup_b = scratch("roupb", [8, D, 4096], BF16)
        self.rodn_b = scratch("rodnb", [8, 4096, D], BF16)
        self.wr_b = [scratch("wr0b", [D, 12], BF16),
                     scratch("wr1b", [D, 8], BF16)]
        self.br0_b = scratch("br0b", [1, 12], BF16)
        self.bs2_b = scratch("bs2b", [1, D], BF16)
        self.be2_b = scratch("be2b", [12, D], BF16)

        idf = np.eye(128, dtype=np.float32)
        tri = np.where(np.arange(128)[:, None] >= np.arange(128)[None, :],
                       np.float32(0.0), np.float32(NEG))
        self.idf_c = nc.inline_tensor(idf, name="ident_f32")
        self.maskA_c = nc.inline_tensor(tri, name="maskA_t")
        self.maskB_c = nc.inline_tensor(np.ascontiguousarray(tri.T), name="maskB_t")
        self.ones_bf_c = nc.inline_tensor(
            np.ones((1, 512), dtype=ml_dtypes.bfloat16), name="ones_bf_t")
        self.ones_f_c = nc.inline_tensor(
            np.ones((1, 512), dtype=np.float32), name="ones_f_t")
        cv = np.tile(np.array([[1e-5, 1e-3, 0.0, 0.0]], dtype=np.float32),
                     (128, 1))
        self.cvals_c = nc.inline_tensor(cv, name="cvals_t")

        sal = Salloc(nc)
        self.identf = sal.alloc("identf", [128, 128], F32)
        self.maskA = sal.alloc("maskA", [128, 128], F32)
        self.maskB = sal.alloc("maskB", [128, 128], F32)
        self.ones_bf = sal.alloc("ones_bf", [1, 512], BF16)
        self.ones_f = sal.alloc("ones_f", [1, 512], F32)
        self.cvals = sal.alloc("cvals", [128, 4], F32)
        self.bias_s1 = sal.alloc("bias_s1", [128, 32], F32)
        self.bias_s2 = sal.alloc("bias_s2", [128, 8], F32)
        self.bias_e1 = sal.alloc("bias_e1", [128, 12, 48], F32)
        self.gam_bc = sal.alloc("gam_bc", [128, D], F32)
        self.bet_bc = sal.alloc("bet_bc", [128, D], F32)
        self.br_row = sal.alloc("br_row", [1, 12], BF16)
        self.bs2_row = sal.alloc("bs2_row", [1, D], BF16)
        self.be2_sb = sal.alloc("be2_sb", [12, D], BF16)
        self.wr_sb = [sal.alloc("wr0_sb", [128, 8, 12], BF16),
                      sal.alloc("wr1_sb", [128, 8, 8], BF16)]
        self.SCRATCH_BASE = sal.cur

        self.psw = [nc.alloc_psum_tensor(f"psw{i}", [128, 1024], F32)
                    for i in range(4)]
        self.psb = [self.psw[i // 2][:, (i % 2) * 512:(i % 2) * 512 + 512]
                    for i in range(8)]
        self.PS = [f"ps{i}" for i in range(8)]

        self.sems = {}
        for name in ALL_SEMS:
            g = nc.semaphore(name)
            self.sems[name] = g.__enter__()
            self._sem_guards.append(g)

    def dma(self, eng, dst, src, reads, writes, vis=True, tag=""):
        self.p.op(eng, lambda e, d=dst, s=src: e.dma_start(out=d, in_=s),
                  reads=reads, writes=writes, dma=True, vis=vis, tag=tag)

    def dma_t(self, dst, src, reads, writes, tag=""):
        self.p.op("sp", lambda e, d=dst, s=src: e.dma_start(out=d, in_=s,
                                                            transpose=True),
                  reads=reads, writes=writes, dma=True, tag=tag)

    def prepass(self):
        def cast(dst_ap, src_ap, key, tag):
            n = int(np.prod(src_ap.shape))
            s = src_ap.flatten() if len(src_ap.shape) > 1 else src_ap
            d = dst_ap.flatten() if len(dst_ap.shape) > 1 else dst_ap
            CH = 1 << 23
            nch = ceil_div(n, CH)
            for i in range(nch):
                lo, hi = i * CH, min(n, (i + 1) * CH)
                self.dma("pool", d[lo:hi], s[lo:hi], (),
                         (key,) if i == nch - 1 else (), vis=False, tag=tag)

        cast(self.watt_b[0], self.w_attn0, "W:watt0", "pp_watt0")
        cast(self.wproj_b[0], self.w_proj0, "W:wproj0", "pp_wproj0")
        cast(self.ws1_b, self.ws1, "W:ws1", "pp_ws1")
        cast(self.ws2_b, self.ws2, "W:ws2", "pp_ws2")
        cast(self.wr_b[0], self.wr0, "W:wr0", "pp_wr0")
        cast(self.br0_b, self.br0.unsqueeze(0), "W:br0", "pp_br0")
        cast(self.be2_b, self.be2, "W:be2", "pp_be2")
        cast(self.bs2_b, self.bs2.unsqueeze(0), "W:bs2", "pp_bs2")
        for e in range(12):
            cast(self.we1_b[e], self.we1[e], f"W:we1_{e}", f"pp_we1_{e}")
            cast(self.we2_b[e], self.we2[e], f"W:we2_{e}", f"pp_we2_{e}")
        for i, w in enumerate((self.wq1, self.wk1, self.wv1)):
            dst = self.watt_b[1][:, i * D:(i + 1) * D]
            dstv = dst.rearrange("d (h j) -> h d j", j=DH)
            for h in range(NH):
                self.dma("pool", dstv[h], w[h], (),
                         (f"W:watt1_{i}",) if h == NH - 1 else (), vis=False,
                         tag=f"pp_watt1_{i}_{h}")
        cast(self.wproj_b[1], self.wo1, "W:wproj1", "pp_wproj1")
        cast(self.wr_b[1], self.wr1, "W:wr1", "pp_wr1")
        cast(self.shup_b, self.shup[0], "W:shup", "pp_shup")
        cast(self.shdn_b, self.shdn[0], "W:shdn", "pp_shdn")
        for e in range(8):
            cast(self.roup_b[e], self.roup[e], f"W:roup_{e}", f"pp_roup_{e}")
            cast(self.rodn_b[e], self.rodn[e], f"W:rodn_{e}", f"pp_rodn_{e}")

    def consts(self):
        p, psb, PS = self.p, self.psb, self.PS
        self.dma("sp", self.identf[:], self.idf_c.ap(), (), ("identf",), tag="c_id")
        self.dma("sp", self.maskA[:], self.maskA_c.ap(), (), ("maskA",), tag="c_mA")
        self.dma("sp", self.maskB[:], self.maskB_c.ap(), (), ("maskB",), tag="c_mB")
        self.dma("sp", self.ones_bf[:], self.ones_bf_c.ap(), (), ("ones_bf",), tag="c_1b")
        self.dma("sp", self.ones_f[:], self.ones_f_c.ap(), (), ("ones_f",), tag="c_1f")
        self.dma("sp", self.cvals[:], self.cvals_c.ap(), (), ("cvals",), tag="c_cv")
        self.dma("sp", self.br_row[:], self.br0_b, ("W:br0",), ("br_row",), tag="c_br")
        self.dma("sp", self.bs2_row[:], self.bs2_b, ("W:bs2",), ("bs2_row",), tag="c_bs2r")
        self.dma("sp", self.be2_sb[:], self.be2_b, ("W:be2",), ("be2_sb",), tag="c_be2")
        for l in range(2):
            self.dma("sp", self.wr_sb[l][:],
                     self.wr_b[l].rearrange("(c p) e -> p c e", p=128),
                     (f"W:wr{l}",), (f"wr_sb{l}",), tag=f"c_wr{l}")

        sal = Salloc(self.nc, self.SCRATCH_BASE)
        bstage = sal.alloc("bstage", [64, 128], F32)

        def bias_prep(src_flat, n_rows, dst, dst_key, tag):
            assert n_rows <= 64
            self.dma("sp", bstage[0:n_rows, :],
                     src_flat.rearrange("(a b) -> a b", b=128),
                     ("c_bstage",), ("c_bstage",), tag=tag + "_ld")
            p.op("pe", lambda e, n=n_rows: e.transpose(
                psb[6][0:128, 0:n], bstage[0:n, :], self.identf[0:n, 0:n]),
                 reads=("c_bstage", "identf"), writes=(PS[6],), tag=tag + "_tr")
            p.op("dve", lambda e, d=dst, n=n_rows: e.tensor_copy(
                d, psb[6][0:128, 0:n]),
                 reads=(PS[6],), writes=(dst_key,), tag=tag + "_ev")

        bias_prep(self.bs1, 32, self.bias_s1[:], "bias_s1", "c_bs1")
        for e in range(12):
            bias_prep(self.be1[e], 48, self.bias_e1[:, e, :], f"bias_e1_{e}",
                      f"c_be1_{e}")

        gb = sal.alloc("gb_stage", [1, 2 * D], F32)
        self.dma("sp", gb[:, 0:D], self.gamma1.unsqueeze(0), (), ("gbs_g",), tag="c_g")
        self.dma("sp", gb[:, D:2 * D], self.beta1.unsqueeze(0), (), ("gbs_b",), tag="c_b")
        for i, (dst, key, rkey) in enumerate(((self.gam_bc, "gam_bc", "gbs_g"),
                                              (self.bet_bc, "bet_bc", "gbs_b"))):
            for c in range(2):
                sl = slice(i * D + c * 512, i * D + (c + 1) * 512)
                b = 4 + c
                p.op("pe", lambda e, sl=sl, b=b: e.matmul(
                    psb[b][:, 0:512], self.ones_f[0:1, 0:128], gb[:, sl],
                    start=True, stop=True),
                     reads=(rkey, "ones_f"), writes=(PS[b],), tag=f"c_gb{i}{c}")
                p.op("dve", lambda e, dst=dst, c=c, b=b: e.tensor_copy(
                    dst[:, c * 512:(c + 1) * 512], psb[b][:, 0:512]),
                     reads=(PS[b],), writes=(f"{key}{c}",), tag=f"c_gbe{i}{c}")

    def ln_tile_ops(self, tag, t, rx, rx_key, stats, xout, xout_key, eps_col,
                    extra_reads=()):
        p, K = self.p, self.K
        sq = self._ln_sq
        s1, s2 = stats[:, 0:1], stats[:, 1:2]
        mean, rr = stats[:, 2:3], stats[:, 3:4]
        negmr, tmp = stats[:, 4:5], stats[:, 5:6]
        ks = K(f"{tag}_stats{t}")
        p.op("act", lambda e: e.activation(out=sq[:], in_=rx[:], func=AF.Square,
                                           accum_out=s2),
             reads=(rx_key,) + tuple(extra_reads), writes=(ks + "s2", "_lnsq"),
             tag=f"{tag}_sq{t}")
        p.op("dve", lambda e: e.reduce_sum(s1, rx[:], axis=AX.X),
             reads=(rx_key,), writes=(ks + "s1",), tag=f"{tag}_s1{t}")
        p.op("dve", lambda e: e.tensor_scalar(out=mean, in0=s1, scalar1=1.0 / D,
                                              scalar2=None, op0=ALU.mult),
             reads=(ks + "s1",), writes=(ks + "mean",), tag=f"{tag}_mn{t}")
        p.op("dve", lambda e: e.tensor_mul(tmp, mean, mean),
             reads=(ks + "mean",), writes=(ks + "m2",), tag=f"{tag}_m2{t}")
        p.op("dve", lambda e: e.scalar_tensor_tensor(
            out=tmp, in0=s2, scalar=1.0 / D, in1=tmp,
            op0=ALU.mult, op1=ALU.subtract),
             reads=(ks + "s2", ks + "m2"), writes=(ks + "var",), tag=f"{tag}_va{t}")
        p.op("act", lambda e: e.activation(out=rr, in_=tmp, func=AF.Sqrt,
                                           bias=self.cvals[:, eps_col:eps_col + 1]),
             reads=(ks + "var", "cvals"), writes=(ks + "sd",), tag=f"{tag}_sd{t}")
        p.op("dve", lambda e: e.reciprocal(rr, rr),
             reads=(ks + "sd",), writes=(ks + "r",), tag=f"{tag}_r{t}")
        p.op("dve", lambda e: e.scalar_tensor_tensor(
            out=negmr, in0=mean, scalar=-1.0, in1=rr, op0=ALU.mult, op1=ALU.mult),
             reads=(ks + "mean", ks + "r"), writes=(ks + "nmr",), tag=f"{tag}_nm{t}")
        p.op("act", lambda e: e.activation(out=xout[:], in_=rx[:], func=AF.Identity,
                                           bias=negmr, scale=rr),
             reads=(rx_key, ks + "nmr", ks + "r"), writes=(xout_key,),
             tag=f"{tag}_ap{t}")

    def ln1(self):
        p = self.p
        sal = Salloc(self.nc, self.SCRATCH_BASE)
        rx = [sal.alloc("ln1_rx", [128, D], F32) for _ in range(3)]
        xo = [sal.alloc("ln1_xo", [128, D], BF16) for _ in range(3)]
        self._ln_sq = sal.alloc("ln1_sq", [128, D], BF16)
        stats = sal.alloc("ln1_stats", [128, 8 * 32], F32)
        for t in range(32):
            b, ob = rx[t % 3], xo[t % 3]
            kb, ko = f"ln1_rx{t % 3}", f"ln1_xo{t % 3}"
            self.dma("sp", b[:], self.x_in[t * 128:(t + 1) * 128, :],
                     (kb,), (kb,), tag=f"ln1_ld{t}")
            self.ln_tile_ops("ln1", t, b, kb, stats[:, t * 8:(t + 1) * 8],
                             ob, ko, 0)
            self.dma("sp", self.xln[t * 128:(t + 1) * 128, :], ob[:],
                     (ko,), (ko,), tag=f"ln1_st{t}")

    def attn(self, l):
        p, psb, PS, K = self.p, self.psb, self.PS, self.K
        nc = self.nc
        sal = Salloc(nc, self.SCRATCH_BASE)
        watt2 = [sal.alloc(f"at_watt{i}", [128, 8, 384], BF16) for i in range(2)]
        xlnf = sal.alloc("at_xlnf", [128, 8, T], BF16)
        qb = [sal.alloc(f"at_qb{i}", [66, T], BF16) for i in range(2)]
        kb = [sal.alloc(f"at_kb{i}", [66, T], BF16) for i in range(2)]
        vsb = sal.alloc("at_vsb", [128, 32, 2, 65], BF16)
        att = [[sal.alloc(f"at_att{i}_{a}", [128, 1024], BF16) for a in range(3)]
               for i in range(2)]
        hsb = [sal.alloc(f"at_hsb{s}", [64, 1024], BF16) for s in range(4)]
        qsq = [sal.alloc(f"at_qsq{i}", [64, T], BF16) for i in range(2)]
        kmx = sal.alloc("at_kmx", [65, 16], F32)
        stmp = sal.alloc("at_stmp", [65, 4, 1024], F32)
        rbc = [sal.alloc(f"at_rbc{s}", [64, 1024], F32) for s in range(4)]
        oc_b = sal.alloc("at_ocb", [65, 64], BF16)
        oc_f = sal.alloc("at_ocf", [65, 64], F32)

        wb = self.watt_b[l]
        wkeys = (("W:watt0",) if l == 0 else
                 ("W:watt1_0", "W:watt1_1", "W:watt1_2"))

        for kc in range(8):
            self.dma_t(xlnf[:, kc, :], self.xln[:, kc * 128:(kc + 1) * 128],
                       (), (f"at_xlnf{kc}",), tag=f"at_xld{kc}")
        XLNF_ALL = tuple(f"at_xlnf{kc}" for kc in range(8))
        for i in range(2):
            p.op("pool", lambda e, i=i: e.memset(kb[i][64:66, :], 0.0),
                 reads=(), writes=(f"at_kones{i}",), tag=f"at_ko{i}")
            p.op("pool", lambda e, i=i: e.memset(kb[i][64:65, :], -0.51),
                 reads=(f"at_kones{i}",), writes=(f"at_kones{i}",),
                 tag=f"at_ko2{i}")
            p.op("pool", lambda e, i=i: e.memset(qb[i][64:66, :], 0.0),
                 reads=(), writes=(f"at_qz{i}",), tag=f"at_qzm{i}")
        p.op("pool", lambda e: e.memset(vsb[:, :, :, 64:65], 1.0),
             reads=(), writes=("at_vones",), tag="at_vo")
        p.op("pool", lambda e: e.memset(oc_b[:], 1.0),
             reads=(), writes=("at_ocb",), tag="at_ocb")
        p.op("pool", lambda e: e.memset(oc_f[:], 1.0),
             reads=(), writes=("at_ocf",), tag="at_ocf")

        for g in range(8):
            wt = watt2[g % 2]
            WATT_ALL = tuple(f"at_w{g % 2}_{kc}" for kc in range(8))
            for kc in range(8):
                srcv = wb[kc * 128:(kc + 1) * 128, :].rearrange(
                    "p (s g c) -> p s g c", s=3, c=128)[:, :, g, :]
                self.dma("sp", wt[:, kc, :].rearrange("p (s c) -> p s c", c=128),
                         srcv, wkeys, (f"at_w{g % 2}_{kc}",),
                         tag=f"at_wld{g}_{kc}")
            for i in range(2):
                h = 2 * g + i
                for x, colbase, dstb, dstk in ((0, i * DH, qb[i], f"at_q{i}"),
                                               (1, 128 + i * DH, kb[i], f"at_k{i}")):
                    for tt in range(8):
                        b = 4 + (tt % 4)
                        def mmfn(e, cb=colbase, tt=tt, b=b, wt=wt):
                            inst = None
                            for kc in range(8):
                                inst = e.matmul(
                                    psb[b][0:64, 0:512],
                                    wt[:, kc, cb:cb + DH],
                                    xlnf[:, kc, tt * 512:(tt + 1) * 512],
                                    start=(kc == 0), stop=(kc == 7))
                            return inst
                        p.op("pe", mmfn, reads=WATT_ALL + XLNF_ALL,
                             writes=(PS[b],), tag=f"at_qk{g}{i}{x}{tt}")
                        p.op("dve", lambda e, dstb=dstb, tt=tt, b=b: e.tensor_copy(
                            dstb[0:64, tt * 512:(tt + 1) * 512],
                            psb[b][0:64, 0:512]),
                             reads=(PS[b],), writes=(f"{dstk}_{tt}",),
                             tag=f"at_qke{g}{i}{x}{tt}")
            for i in range(2):
                h = 2 * g + i
                QROWS = tuple(f"at_q{i}_{tt}" for tt in range(8))
                KROWS = tuple(f"at_k{i}_{tt}" for tt in range(8))
                sq = qsq[i]
                p.op("pool", lambda e, i=i, sq=sq: e.tensor_mul(
                    sq[0:64, :], kb[i][0:64, :], kb[i][0:64, :]),
                     reads=KROWS, writes=(f"at_sq{i}",), tag=f"at_ksq{h}")
                for c in range(8):
                    bb = 6 + (c % 2)
                    p.op("pe", lambda e, sq=sq, c=c, bb=bb: e.matmul(
                        psb[bb][64:65, 0:512], oc_b[0:64, 0:1],
                        sq[0:64, c * 512:(c + 1) * 512],
                        start=True, stop=True),
                         reads=(f"at_sq{i}", "at_ocb"), writes=(PS[bb],),
                         tag=f"at_kn{h}_{c}")
                    p.op("dve", lambda e, c=c, bb=bb: e.reduce_max(
                        kmx[64:65, c:c + 1], psb[bb][64:65, 0:512], axis=AX.X),
                         reads=(PS[bb],), writes=(f"at_kmx{c}",),
                         tag=f"at_knm{h}_{c}")
                p.op("dve", lambda e: e.reduce_max(
                    kmx[64:65, 8:9], kmx[64:65, 0:8], axis=AX.X),
                     reads=tuple(f"at_kmx{c}" for c in range(8)),
                     writes=("at_kmxf",), tag=f"at_kmf{h}")
                p.op("pool", lambda e, i=i, sq=sq: e.tensor_mul(
                    sq[0:64, :], qb[i][0:64, :], qb[i][0:64, :]),
                     reads=QROWS, writes=(f"at_sq{i}",), tag=f"at_qsq{h}")
                for c in range(8):
                    bb = 6 + (c % 2)
                    p.op("pe", lambda e, sq=sq, c=c, bb=bb: e.matmul(
                        psb[bb][64:65, 0:512], oc_b[0:64, 0:1],
                        sq[0:64, c * 512:(c + 1) * 512],
                        start=True, stop=True),
                         reads=(f"at_sq{i}", "at_ocb"), writes=(PS[bb],),
                         tag=f"at_qn{h}_{c}")
                    p.op("dve", lambda e, i=i, c=c, bb=bb: e.tensor_scalar(
                        out=qb[i][64:65, c * 512:(c + 1) * 512],
                        in0=psb[bb][64:65, 0:512], scalar1=kmx[64:65, 8:9],
                        scalar2=None, op0=ALU.add),
                         reads=(PS[bb], "at_kmxf", f"at_qz{i}"),
                         writes=(f"at_qrow{i}",), tag=f"at_qnr{h}_{c}")
            for ts in range(32):
                b = ts % 4
                def vfn(e, ts=ts, b=b, wt=wt):
                    inst = None
                    for kc in range(8):
                        inst = e.matmul(
                            psb[b][:, 0:128],
                            xlnf[:, kc, ts * 128:(ts + 1) * 128],
                            wt[:, kc, 256:384],
                            start=(kc == 0), stop=(kc == 7))
                    return inst
                p.op("pe", vfn, reads=WATT_ALL + XLNF_ALL, writes=(PS[b],),
                     tag=f"at_v{g}{ts}")
                p.op("dve", lambda e, ts=ts, b=b: e.tensor_copy(
                    vsb[:, ts, :, 0:64],
                    psb[b][:, 0:128].rearrange("p (h c) -> p h c", c=64)),
                     reads=(PS[b],), writes=(f"at_v{ts}",), tag=f"at_ve{g}{ts}")
            VSB_ALL = tuple(f"at_v{ts}" for ts in range(32))

            QR = [tuple(f"at_q{i}_{tt}" for tt in range(8)) for i in range(2)]
            KR = [tuple(f"at_k{i}_{tt}" for tt in range(8)) for i in range(2)]
            scp = [self.psw[0], self.psw[1]]
            SCK2 = [(PS[0], PS[1]), (PS[2], PS[3])]
            avp = [self.psw[2], self.psw[3]]
            AVK2 = [(PS[4], PS[5]), (PS[6], PS[7])]

            def emit_evac(i, pr):
                h = 2 * g + i
                av = avp[i]
                s = i * 2 + (pr % 2)
                p.op("dve", lambda e, av=av, s=s: e.reciprocal(
                    stmp[64:65, s, :], av[64:65, 0:1024]),
                     reads=AVK2[i], writes=(f"at_rcp{s}",), tag=f"at_rc{h}_{pr}")
                p.op("pe", lambda e, i=i, s=s: (
                    e.matmul(psb[2 * i][0:64, 0:512], oc_f[64:65, 0:64],
                             stmp[64:65, s, 0:512], start=True, stop=True),
                    e.matmul(psb[2 * i + 1][0:64, 0:512], oc_f[64:65, 0:64],
                             stmp[64:65, s, 512:1024], start=True, stop=True))[-1],
                     reads=(f"at_rcp{s}", "at_ocf"),
                     writes=SCK2[i], tag=f"at_rb{h}_{pr}")
                for half in range(2):
                    p.op("dve", lambda e, i=i, s=s, half=half: e.tensor_copy(
                        rbc[s][0:64, half * 512:(half + 1) * 512],
                        psb[2 * i + half][0:64, 0:512]),
                         reads=(SCK2[i][half],), writes=(f"at_rbc{s}_{half}",),
                         tag=f"at_rbe{h}_{pr}_{half}")
                p.op("dve", lambda e, av=av, s=s: e.tensor_tensor(
                    out=hsb[s][:, 0:1024], in0=av[0:64, 0:1024],
                    in1=rbc[s][0:64, 0:1024], op=ALU.mult),
                     reads=AVK2[i] + (f"at_rbc{s}_0", f"at_rbc{s}_1"),
                     writes=(f"at_hsb{s}",), tag=f"at_he{h}_{pr}")
                self.dma("sp",
                         self.heads[h * 64:(h + 1) * 64,
                                    pr * 1024:(pr + 1) * 1024],
                         hsb[s][:, 0:1024], (f"at_hsb{s}",),
                         (f"at_hst{h}_{pr}",), tag=f"at_hs{h}_{pr}")

            def emit_av(i, kt, pr):
                h = 2 * g + i
                wide = kt <= 8 * pr + 3
                a = kt % 3
                av = avp[i]
                def avfn(e, i=i, kt=kt, pr=pr, a=a, wide=wide, av=av):
                    inst = None
                    if wide:
                        inst = e.matmul(
                            av[0:65, 0:512], vsb[:, kt, i, 0:65],
                            att[i][a][:, 0:512],
                            start=(kt == 0), stop=(kt == 8 * pr + 3))
                    inst = e.matmul(
                        av[0:65, 512:1024], vsb[:, kt, i, 0:65],
                        att[i][a][:, 512:1024],
                        start=(kt == 0), stop=(kt == 8 * pr + 7))
                    return inst
                p.op("pe", avfn,
                     reads=(f"at_att{i}_{a}", "at_vones") + VSB_ALL,
                     writes=AVK2[i], tag=f"at_av{h}_{pr}_{kt}")

            pending = []
            for pr in range(4):
                qce, qco = 2 * pr, 2 * pr + 1
                nkt = 8 * pr + 8
                for kt in range(nkt):
                    for i in range(2):
                        h = 2 * g + i
                        wide = kt <= 8 * pr + 3
                        sc = scp[i]
                        SCK = SCK2[i]
                        a = kt % 3
                        atb = att[i][a]
                        atk = f"at_att{i}_{a}"
                        def scfn(e, i=i, kt=kt, qce=qce, wide=wide, sc=sc):
                            inst = None
                            if wide:
                                inst = e.matmul(
                                    sc[:, 0:512],
                                    kb[i][0:66, kt * 128:(kt + 1) * 128],
                                    qb[i][0:66, qce * 512:(qce + 1) * 512],
                                    start=True, stop=True)
                            inst = e.matmul(
                                sc[:, 512:1024],
                                kb[i][0:66, kt * 128:(kt + 1) * 128],
                                qb[i][0:66, (qce + 1) * 512:(qce + 2) * 512],
                                start=True, stop=True)
                            return inst
                        p.op("pe", scfn,
                             reads=KR[i] + QR[i] + (f"at_qrow{i}", f"at_kones{i}"),
                             writes=SCK, tag=f"at_pb{h}_{pr}_{kt}")
                        dge = (kt // 4 == qce)
                        dgo = (kt // 4 == qco)
                        if dge or dgo:
                            j0 = (0 if dge else 512) + (kt % 4) * 128
                            zlo = 0 if dge else 512
                            p.op("dve", lambda e, sc=sc, j0=j0: e.tensor_add(
                                sc[:, j0:j0 + 128], sc[:, j0:j0 + 128],
                                self.maskB[:]),
                                 reads=SCK + ("maskB",), writes=SCK,
                                 tag=f"at_pbm{h}_{pr}_{kt}")
                            if j0 > zlo:
                                p.op("dve", lambda e, atb=atb, zlo=zlo, j0=j0:
                                     e.memset(atb[:, zlo:j0], 0.0),
                                     reads=(), writes=(atk,),
                                     tag=f"at_pbz{h}_{pr}_{kt}")
                            p.op("act", lambda e, atb=atb, sc=sc, j0=j0: e.activation(
                                out=atb[:, j0:1024], in_=sc[:, j0:1024],
                                func=AF.Exp, scale=0.125),
                                 reads=SCK, writes=(atk,),
                                 tag=f"at_pbe{h}_{pr}_{kt}")
                        else:
                            lo = 0 if wide else 512
                            p.op("act", lambda e, atb=atb, sc=sc, lo=lo: e.activation(
                                out=atb[:, lo:1024], in_=sc[:, lo:1024],
                                func=AF.Exp, scale=0.125),
                                 reads=SCK, writes=(atk,),
                                 tag=f"at_pbe{h}_{pr}_{kt}")
                    if kt == 0 and pending:
                        for fn in pending:
                            fn()
                        pending = []
                    if kt >= 1:
                        for i in range(2):
                            emit_av(i, kt - 1, pr)
                for i in range(2):
                    emit_av(i, nkt - 1, pr)
                for i in range(2):
                    pending.append(lambda i=i, pr=pr: emit_evac(i, pr))
            for fn in pending:
                fn()

    def proj(self, l):
        p, psb, PS = self.p, self.psb, self.PS
        nc = self.nc
        sal = Salloc(nc, self.SCRATCH_BASE)
        wp = sal.alloc("pj_wp", [128, 8, D], BF16)
        hb = [sal.alloc(f"pj_hb{i}", [128, 8, 128], BF16) for i in range(2)]
        rx = [sal.alloc(f"pj_rx{i}", [128, D], F32) for i in range(2)]
        ub = [sal.alloc(f"pj_ub{i}", [128, D], BF16) for i in range(2)]
        yb = [sal.alloc(f"pj_yb{i}", [128, D], F32) for i in range(2)]
        y2 = [sal.alloc(f"pj_y2{i}", [128, D], F32) for i in range(2)]
        rnew = [sal.alloc(f"pj_rn{i}", [128, D], F32) for i in range(2)]
        xo = [sal.alloc(f"pj_xo{i}", [128, D], BF16) for i in range(2)]
        self._ln_sq = sal.alloc("pj_sq", [128, D], BF16)
        stats = sal.alloc("pj_stats", [128, 8 * 32], F32)
        stats2 = sal.alloc("pj_stats2", [128, 8 * 32], F32)

        for kc in range(8):
            self.dma("sp", wp[:, kc, :],
                     self.wproj_b[l][kc * 128:(kc + 1) * 128, :],
                     (f"W:wproj{l}",), (f"pj_wp{kc}",), tag=f"pj_wld{kc}")
        WP_ALL = tuple(f"pj_wp{kc}" for kc in range(8))
        resid_src = self.x_in if l == 0 else self.xres

        for ts in range(32):
            hbb, khb = hb[ts % 2], f"pj_hb{ts % 2}"
            self.dma("sp", hbb[:],
                     self.heads.rearrange("(c p) t -> p c t", p=128)[
                         :, :, ts * 128:(ts + 1) * 128],
                     (), (khb,), tag=f"pj_hld{ts}")
            rxb, krx = rx[ts % 2], f"pj_rx{ts % 2}"
            self.dma("sp", rxb[:], resid_src[ts * 128:(ts + 1) * 128, :],
                     (), (krx,), tag=f"pj_rld{ts}")

            def mmfn(e, hbb=hbb):
                inst = None
                for kc in range(8):
                    for half in range(2):
                        inst = e.matmul(
                            psb[half][:, 0:512], hbb[:, kc, :],
                            wp[:, kc, half * 512:(half + 1) * 512],
                            start=(kc == 0), stop=(kc == 7))
                return inst
            p.op("pe", mmfn, reads=(khb,) + WP_ALL, writes=(PS[0], PS[1]),
                 tag=f"pj_mm{ts}")

            rnb, krn = rnew[ts % 2], f"pj_rn{ts % 2}"
            if l == 0:
                for half in range(2):
                    p.op("dve", lambda e, half=half, rxb=rxb, rnb=rnb: e.tensor_add(
                        rnb[:, half * 512:(half + 1) * 512],
                        psb[half][:, 0:512],
                        rxb[:, half * 512:(half + 1) * 512]),
                         reads=(PS[half], krx), writes=(krn + f"h{half}",),
                         tag=f"pj_add{ts}_{half}")
                rn_r = (krn + "h0", krn + "h1")
            else:
                ubb, kub = ub[ts % 2], f"pj_ub{ts % 2}"
                self.dma("sp", ubb[:], self.xln[ts * 128:(ts + 1) * 128, :],
                         (), (kub,), tag=f"pj_uld{ts}")
                ybb, kyb = yb[ts % 2], f"pj_yb{ts % 2}"
                y2b, ky2 = y2[ts % 2], f"pj_y2{ts % 2}"
                for half in range(2):
                    p.op("dve", lambda e, half=half, ubb=ubb, ybb=ybb: e.tensor_add(
                        ybb[:, half * 512:(half + 1) * 512],
                        psb[half][:, 0:512],
                        ubb[:, half * 512:(half + 1) * 512]),
                         reads=(PS[half], kub, kyb), writes=(kyb,),
                         tag=f"pj_yadd{ts}_{half}")
                self.ln_tile_ops("pjI", ts, ybb, kyb,
                                 stats[:, ts * 8:ts * 8 + 8], y2b, ky2, 1)
                p.op("dve", lambda e, y2b=y2b: e.tensor_mul(
                    y2b[:], y2b[:], self.gam_bc[:]),
                     reads=(ky2, "gam_bc0", "gam_bc1"),
                     writes=(ky2,), tag=f"pj_gm{ts}")
                p.op("dve", lambda e, y2b=y2b: e.tensor_add(
                    y2b[:], y2b[:], self.bet_bc[:]),
                     reads=(ky2, "bet_bc0", "bet_bc1"),
                     writes=(ky2,), tag=f"pj_bt{ts}")
                p.op("dve", lambda e, rxb=rxb, rnb=rnb, y2b=y2b: e.tensor_add(
                    rnb[:], y2b[:], rxb[:]),
                     reads=(ky2, krx), writes=(krn + "h0",),
                     tag=f"pj_add{ts}")
                rn_r = (krn + "h0",)
            self.dma("sp", self.xres[ts * 128:(ts + 1) * 128, :], rnb[:],
                     rn_r, (f"pj_rst{ts}",), tag=f"pj_rs{ts}")
            xob, kxo = xo[ts % 2], f"pj_xo{ts % 2}"
            self.ln_tile_ops("pjL", ts, rnb, rn_r[0],
                             stats2[:, ts * 8:ts * 8 + 8], xob, kxo, 0,
                             extra_reads=rn_r[1:])
            self.dma("sp", self.xln[ts * 128:(ts + 1) * 128, :], xob[:],
                     (kxo,), (f"pj_xst{ts}",), tag=f"pj_xs{ts}")

    def ffn(self, l):
        p, psb, PS, K = self.p, self.psb, self.PS, self.K
        nc = self.nc
        E = 12 if l == 0 else 8
        HA = 3072 if l == 0 else 4096
        NJ = HA // 128
        sal = Salloc(nc, self.SCRATCH_BASE)
        xlnf = sal.alloc("ff_xlnf", [128, 8, 1024], BF16)
        act = sal.alloc("ff_act", [128, 32 if l else 24, 1024], BF16)
        yac = sal.alloc("ff_yac", [128, 8, D], F32)
        wup = [sal.alloc(f"ff_wup{i}", [128, 8, 1536 if l == 0 else 768], BF16)
               for i in range(2)]
        wdn = [sal.alloc(f"ff_wdn{i}", [128, 4, D], BF16) for i in range(2)]
        gsb = sal.alloc("ff_gsb", [128, 2, 512], BF16)
        lgb = sal.alloc("ff_lgb", [128, 8, E], F32)
        gts = sal.alloc("ff_gts", [128, 8, E], F32)
        gsc = sal.alloc("ff_gsc", [128, 8 * 8], F32)
        gfm = sal.alloc("ff_gfm", [12, 1024], BF16) if l == 0 else None
        rx = [sal.alloc(f"ff_rx{i}", [128, D], F32) for i in range(2)]
        ub = [sal.alloc(f"ff_ub{i}", [128, D], BF16) for i in range(2)] \
            if l == 1 else [None, None]
        rn = [sal.alloc(f"ff_rn{i}", [128, D], F32) for i in range(2)]
        xo = [sal.alloc(f"ff_xo{i}", [128, D], BF16) for i in range(2)] \
            if l == 0 else [None, None]
        if l == 0:
            self._ln_sq = sal.alloc("ff_sq", [128, D], BF16)
            stats = sal.alloc("ff_stats", [128, 8 * 32], F32)
        else:
            stats = None

        if l == 0:
            jobs = [("d", self.ws1_b, self.ws2_b, 4096, 2048, None, "s1")]
            jobs += [(f"e{e}", self.we1_b[e], self.we2_b[e], 6144, 3072, e,
                      f"e1_{e}") for e in range(12)]
            wk_up = {"d": ("W:ws1",)}
            wk_dn = {"d": ("W:ws2",)}
            for e in range(12):
                wk_up[f"e{e}"] = (f"W:we1_{e}",)
                wk_dn[f"e{e}"] = (f"W:we2_{e}",)
            geglu = True
        else:
            jobs = [("d", self.shup_b, self.shdn_b, 4096, 4096, None, None)]
            jobs += [(f"e{e}", self.roup_b[e], self.rodn_b[e], 4096, 4096, e,
                      None) for e in range(8)]
            wk_up = {"d": ("W:shup",)}
            wk_dn = {"d": ("W:shdn",)}
            for e in range(8):
                wk_up[f"e{e}"] = (f"W:roup_{e}",)
                wk_dn[f"e{e}"] = (f"W:rodn_{e}",)
            geglu = False

        for tt in range(4):
            t0 = tt * 1024
            for kc in range(8):
                self.dma_t(xlnf[:, kc, :],
                           self.xln[t0:t0 + 1024, kc * 128:(kc + 1) * 128],
                           (f"ff_xlnf{kc}",), (f"ff_xlnf{kc}",),
                           tag=f"ff_xld{tt}_{kc}")
            XLNF = tuple(f"ff_xlnf{kc}" for kc in range(8))

            for ts in range(8):
                def lgfn(e, ts=ts, l=l):
                    inst = None
                    for kc in range(8):
                        inst = e.matmul(
                            psb[6][:, 0:E],
                            xlnf[:, kc, ts * 128:(ts + 1) * 128],
                            self.wr_sb[l][:, kc, :],
                            start=(kc == 0), stop=(kc == 7 and l != 0))
                    if l == 0:
                        inst = e.matmul(psb[6][:, 0:E], self.ones_bf[0:1, 0:128],
                                        self.br_row[:], start=False, stop=True)
                    return inst
                p.op("pe", lgfn, reads=XLNF + (f"wr_sb{l}", "ones_bf", "br_row"),
                     writes=(PS[6],), tag=f"ff_lg{tt}_{ts}")
                p.op("act", lambda e, ts=ts: e.activation(
                    out=lgb[:, ts, :], in_=psb[6][:, 0:E], func=AF.Identity,
                    scale=(1.0 / 1.5) if l == 0 else 1.0),
                     reads=(PS[6],), writes=(f"ff_lg{ts}",), tag=f"ff_lge{tt}_{ts}")
            LG = tuple(f"ff_lg{ts}" for ts in range(8))
            v1 = gsc[:, 0:8]
            v2 = gsc[:, 8:16]
            w1 = gsc[:, 16:24]
            w2 = gsc[:, 24:32]
            ssum = gsc[:, 32:40]
            if l == 0:
                p.op("dve", lambda e: e.tensor_reduce(
                    v1, lgb[:], axis=AX.X, op=ALU.max),
                     reads=LG, writes=("ff_v1",), tag=f"ff_r1{tt}")
                p.op("dve", lambda e: e.tensor_tensor(
                    out=gts[:], in0=lgb[:],
                    in1=v1.unsqueeze(2).broadcast_to([128, 8, E]),
                    op=ALU.is_equal),
                     reads=LG + ("ff_v1",), writes=("ff_m1",), tag=f"ff_m1{tt}")
                p.op("dve", lambda e: e.scalar_tensor_tensor(
                    out=lgb[:], in0=gts[:], scalar=NEG, in1=lgb[:],
                    op0=ALU.mult, op1=ALU.add),
                     reads=("ff_m1",) + LG, writes=("ff_msk",), tag=f"ff_mk{tt}")
                p.op("dve", lambda e: e.tensor_reduce(
                    v2, lgb[:], axis=AX.X, op=ALU.max),
                     reads=("ff_msk",), writes=("ff_v2",), tag=f"ff_r2{tt}")
                p.op("dve", lambda e: e.tensor_sub(w1, v1, v2),
                     reads=("ff_v1", "ff_v2"), writes=("ff_d12",), tag=f"ff_d{tt}")
                p.op("act", lambda e: e.activation(out=w1, in_=w1, func=AF.Sigmoid),
                     reads=("ff_d12",), writes=("ff_w1",), tag=f"ff_s1{tt}")
                p.op("dve", lambda e: e.tensor_scalar(
                    out=w2, in0=w1, scalar1=-1.0, scalar2=1.0,
                    op0=ALU.mult, op1=ALU.add),
                     reads=("ff_w1",), writes=("ff_w2",), tag=f"ff_s2{tt}")
                p.op("dve", lambda e: e.tensor_tensor(
                    out=lgb[:], in0=lgb[:],
                    in1=v2.unsqueeze(2).broadcast_to([128, 8, E]),
                    op=ALU.is_equal),
                     reads=("ff_msk", "ff_v2"), writes=("ff_m2",), tag=f"ff_m2{tt}")
                p.op("dve", lambda e: e.tensor_mul(
                    gts[:], gts[:],
                    w1.unsqueeze(2).broadcast_to([128, 8, E])),
                     reads=("ff_m1", "ff_w1"), writes=("ff_g1",), tag=f"ff_g1{tt}")
                p.op("dve", lambda e: e.tensor_mul(
                    lgb[:], lgb[:],
                    w2.unsqueeze(2).broadcast_to([128, 8, E])),
                     reads=("ff_m2", "ff_w2"), writes=("ff_g2",), tag=f"ff_g2{tt}")
                p.op("dve", lambda e: e.tensor_add(gts[:], gts[:], lgb[:]),
                     reads=("ff_g1", "ff_g2"), writes=("ff_gts",), tag=f"ff_gs{tt}")
                for ts in range(8):
                    p.op("pe", lambda e, ts=ts: e.transpose(
                        psb[7][0:E, 0:128], gts[:, ts, :], self.identf[:]),
                         reads=("ff_gts", "identf"), writes=(PS[7],),
                         tag=f"ff_gt{tt}_{ts}")
                    p.op("dve", lambda e, ts=ts: e.tensor_copy(
                        gfm[:, ts * 128:(ts + 1) * 128], psb[7][0:E, 0:128]),
                         reads=(PS[7],), writes=("ff_gfm",), tag=f"ff_gf{tt}_{ts}")
            else:
                p.op("dve", lambda e: e.tensor_reduce(
                    v1, lgb[:], axis=AX.X, op=ALU.max),
                     reads=LG, writes=("ff_v1",), tag=f"ff_r1{tt}")
                p.op("dve", lambda e: e.tensor_tensor(
                    out=lgb[:], in0=lgb[:],
                    in1=v1.unsqueeze(2).broadcast_to([128, 8, E]),
                    op=ALU.subtract),
                     reads=LG + ("ff_v1",), writes=("ff_sh",), tag=f"ff_sh{tt}")
                p.op("act", lambda e: e.activation(out=lgb[:], in_=lgb[:],
                                                   func=AF.Exp),
                     reads=("ff_sh",), writes=("ff_ex",), tag=f"ff_ex{tt}")
                p.op("dve", lambda e: e.tensor_reduce(
                    ssum, lgb[:], axis=AX.X, op=ALU.add),
                     reads=("ff_ex",), writes=("ff_ss",), tag=f"ff_ss{tt}")
                p.op("dve", lambda e: e.reciprocal(ssum, ssum),
                     reads=("ff_ss",), writes=("ff_rs",), tag=f"ff_rs{tt}")
                p.op("dve", lambda e: e.tensor_mul(
                    lgb[:], lgb[:], ssum.unsqueeze(2).broadcast_to([128, 8, E])),
                     reads=("ff_ex", "ff_rs"), writes=("ff_pr",), tag=f"ff_pr{tt}")
                p.op("dve", lambda e: e.tensor_reduce(
                    v1, lgb[:], axis=AX.X, op=ALU.max),
                     reads=("ff_pr",), writes=("ff_p1",), tag=f"ff_p1{tt}")
                p.op("dve", lambda e: e.tensor_tensor(
                    out=gts[:], in0=lgb[:],
                    in1=v1.unsqueeze(2).broadcast_to([128, 8, E]),
                    op=ALU.is_equal),
                     reads=("ff_pr", "ff_p1"), writes=("ff_m1",), tag=f"ff_m1{tt}")
                p.op("dve", lambda e: e.scalar_tensor_tensor(
                    out=lgb[:], in0=gts[:], scalar=-2.0, in1=lgb[:],
                    op0=ALU.mult, op1=ALU.add),
                     reads=("ff_m1", "ff_pr"), writes=("ff_msk",), tag=f"ff_mk{tt}")
                p.op("dve", lambda e: e.tensor_reduce(
                    v2, lgb[:], axis=AX.X, op=ALU.max),
                     reads=("ff_msk",), writes=("ff_v2",), tag=f"ff_r2{tt}")
                p.op("dve", lambda e: e.tensor_tensor(
                    out=lgb[:], in0=lgb[:],
                    in1=v2.unsqueeze(2).broadcast_to([128, 8, E]),
                    op=ALU.is_equal),
                     reads=("ff_msk", "ff_v2"), writes=("ff_m2",), tag=f"ff_m2{tt}")
                p.op("dve", lambda e: e.tensor_mul(
                    gts[:], gts[:], v1.unsqueeze(2).broadcast_to([128, 8, E])),
                     reads=("ff_m1", "ff_p1"), writes=("ff_g1",), tag=f"ff_g1{tt}")
                p.op("dve", lambda e: e.tensor_mul(
                    lgb[:], lgb[:], v2.unsqueeze(2).broadcast_to([128, 8, E])),
                     reads=("ff_m2", "ff_v2"), writes=("ff_g2",), tag=f"ff_g2{tt}")
                p.op("dve", lambda e: e.tensor_add(gts[:], gts[:], lgb[:]),
                     reads=("ff_g1", "ff_g2"), writes=("ff_gts",), tag=f"ff_gs{tt}")

            p.op("dve", lambda e: e.memset(yac[:], 0.0),
                 reads=(), writes=("ff_yac",), tag=f"ff_yz{tt}")

            for (name, wu, wd, HB, HAe, gate_e, bias) in jobs:
                nj = HAe // 128
                if geglu:
                    nband = ceil_div(nj, 6)
                    for b in range(nband):
                        j0 = b * 6
                        j1 = min(nj, j0 + 6)
                        wbuf, wkb = wup[b % 2], f"ff_wup{b % 2}"
                        for kc in range(8):
                            srcv = wu[kc * 128:(kc + 1) * 128, :].rearrange(
                                "p (s c) -> p s c", s=2)[
                                :, :, j0 * 128:j1 * 128]
                            self.dma("sp",
                                     wbuf[:, kc, 0:2 * (j1 - j0) * 128].rearrange(
                                         "p (s c) -> p s c", s=2),
                                     srcv, wk_up[name] + (wkb,), (wkb,),
                                     tag=f"ff_wul{tt}{name}{b}_{kc}")
                        for j in range(j0, j1):
                            jj = j - j0
                            bs = (j % 2) * 4
                            for ab in range(2):
                                def upfn(e, jj=jj, ab=ab, bs=bs, wbuf=wbuf,
                                         nw=(j1 - j0)):
                                    inst = None
                                    for kc in range(8):
                                        lhs = wbuf[:, kc,
                                                   ab * nw * 128 + jj * 128:
                                                   ab * nw * 128 + (jj + 1) * 128]
                                        for tc in range(2):
                                            inst = e.matmul(
                                                psb[bs + 2 * ab + tc][:, 0:512],
                                                lhs,
                                                xlnf[:, kc,
                                                     tc * 512:(tc + 1) * 512],
                                                start=(kc == 0), stop=(kc == 7))
                                    return inst
                                p.op("pe", upfn,
                                     reads=(wkb,) + XLNF,
                                     writes=(PS[bs + 2 * ab], PS[bs + 2 * ab + 1]),
                                     tag=f"ff_up{tt}{name}_{j}_{ab}")
                            if bias == "s1":
                                bcol_a = self.bias_s1[:, j:j + 1]
                                bcol_g = self.bias_s1[:, 16 + j:17 + j]
                                brd = ("bias_s1",)
                            else:
                                ei = int(bias[3:])
                                bcol_a = self.bias_e1[:, ei, j:j + 1]
                                bcol_g = self.bias_e1[:, ei, 24 + j:25 + j]
                                brd = (f"bias_{bias}",)
                            for tc in range(2):
                                p.op("act", lambda e, bs=bs, tc=tc, bg=bcol_g: e.activation(
                                    out=gsb[:, tc, :], in_=psb[bs + 2 + tc][:, 0:512],
                                    func=AF.Gelu, bias=bg),
                                     reads=(PS[bs + 2 + tc],) + brd,
                                     writes=(f"ff_gsb{tc}",),
                                     tag=f"ff_ge{tt}{name}_{j}_{tc}")
                                p.op("dve", lambda e, bs=bs, tc=tc, j=j, ba=bcol_a: e.scalar_tensor_tensor(
                                    out=act[:, j, tc * 512:(tc + 1) * 512],
                                    in0=psb[bs + tc][:, 0:512], scalar=ba,
                                    in1=gsb[:, tc, :], op0=ALU.add, op1=ALU.mult),
                                     reads=(PS[bs + tc], f"ff_gsb{tc}") + brd,
                                     writes=(f"ff_act{j}",),
                                     tag=f"ff_ae{tt}{name}_{j}_{tc}")
                else:
                    nband = ceil_div(nj, 6)
                    for b in range(nband):
                        j0 = b * 6
                        j1 = min(nj, j0 + 6)
                        wbuf, wkb = wup[b % 2], f"ff_wup{b % 2}"
                        for kc in range(8):
                            self.dma("sp",
                                     wbuf[:, kc, 0:(j1 - j0) * 128],
                                     wu[kc * 128:(kc + 1) * 128,
                                        j0 * 128:j1 * 128],
                                     wk_up[name] + (wkb,), (wkb,),
                                     tag=f"ff_wul{tt}{name}{b}_{kc}")
                        for j in range(j0, j1):
                            jj = j - j0
                            bs = (j % 4) * 2
                            def upfn(e, jj=jj, bs=bs, wbuf=wbuf):
                                inst = None
                                for kc in range(8):
                                    lhs = wbuf[:, kc, jj * 128:(jj + 1) * 128]
                                    for tc in range(2):
                                        inst = e.matmul(
                                            psb[bs + tc][:, 0:512], lhs,
                                            xlnf[:, kc, tc * 512:(tc + 1) * 512],
                                            start=(kc == 0), stop=(kc == 7))
                                return inst
                            p.op("pe", upfn, reads=(wkb,) + XLNF,
                                 writes=(PS[bs], PS[bs + 1]),
                                 tag=f"ff_up{tt}{name}_{j}")
                            for tc in range(2):
                                p.op("act", lambda e, bs=bs, tc=tc, j=j: e.activation(
                                    out=act[:, j, tc * 512:(tc + 1) * 512],
                                    in_=psb[bs + tc][:, 0:512], func=AF.Gelu),
                                     reads=(PS[bs + tc],),
                                     writes=(f"ff_act{j}",),
                                     tag=f"ff_ae{tt}{name}_{j}_{tc}")
                ACT_ALL = tuple(f"ff_act{j}" for j in range(nj))

                for half in range(2):
                    for kc in range(nj):
                        if kc % 4 == 0:
                            wb_, wkd = wdn[(kc // 4) % 2], f"ff_wdn{(kc // 4) % 2}"
                            self.dma("sp",
                                     wb_[:, 0:4, :],
                                     wd[kc * 128:(kc + 4) * 128, :].rearrange(
                                         "(k p) d -> p k d", p=128),
                                     wk_dn[name] + (wkd,), (wkd,),
                                     tag=f"ff_wdl{tt}{name}{half}_{kc}")
                        wb_, wkd = wdn[(kc // 4) % 2], f"ff_wdn{(kc // 4) % 2}"
                        first = (kc == 0)
                        last = (kc == nj - 1)
                        for tsub in range(4):
                            tg = half * 4 + tsub
                            def dnfn(e, kc=kc, tsub=tsub, tg=tg, wb_=wb_,
                                     first=first, last=last, name=name):
                                inst = None
                                if first and name == "d" and l == 0:
                                    inst = e.matmul(
                                        psb[2 * tsub][:, 0:512],
                                        self.ones_bf[0:1, 0:128],
                                        self.bs2_row[:, 0:512],
                                        start=True, stop=False)
                                    inst = e.matmul(
                                        psb[2 * tsub + 1][:, 0:512],
                                        self.ones_bf[0:1, 0:128],
                                        self.bs2_row[:, 512:1024],
                                        start=True, stop=False)
                                    inst = e.matmul(
                                        psb[2 * tsub][:, 0:512],
                                        gfm[:, tg * 128:(tg + 1) * 128],
                                        self.be2_sb[:, 0:512],
                                        start=False, stop=False)
                                    inst = e.matmul(
                                        psb[2 * tsub + 1][:, 0:512],
                                        gfm[:, tg * 128:(tg + 1) * 128],
                                        self.be2_sb[:, 512:1024],
                                        start=False, stop=False)
                                for dh in range(2):
                                    inst = e.matmul(
                                        psb[2 * tsub + dh][:, 0:512],
                                        act[:, kc, tg * 128:(tg + 1) * 128],
                                        wb_[:, kc % 4, dh * 512:(dh + 1) * 512],
                                        start=(first and not (name == "d" and l == 0)),
                                        stop=last)
                                return inst
                            extra = ("ones_bf", "bs2_row", "be2_sb", "ff_gfm") \
                                if (first and name == "d" and l == 0) else ()
                            p.op("pe", dnfn,
                                 reads=(wkd, f"ff_act{kc}") + extra,
                                 writes=(PS[2 * tsub], PS[2 * tsub + 1]),
                                 tag=f"ff_dn{tt}{name}_{half}_{kc}_{tsub}")
                    for tsub in range(4):
                        tg = half * 4 + tsub
                        for dh in range(2):
                            bk = 2 * tsub + dh
                            dst = yac[:, tg, dh * 512:(dh + 1) * 512]
                            if gate_e is None:
                                p.op("dve", lambda e, dst=dst, bk=bk: e.tensor_tensor(
                                    out=dst, in0=dst, in1=psb[bk][:, 0:512],
                                    op=ALU.add),
                                     reads=(PS[bk], "ff_yac"), writes=("ff_yac",),
                                     tag=f"ff_ev{tt}{name}_{tg}_{dh}")
                            else:
                                gcol = gts[:, tg, gate_e:gate_e + 1]
                                p.op("dve", lambda e, dst=dst, bk=bk, gcol=gcol: e.scalar_tensor_tensor(
                                    out=dst, in0=psb[bk][:, 0:512], scalar=gcol,
                                    in1=dst, op0=ALU.mult, op1=ALU.add),
                                     reads=(PS[bk], "ff_yac", "ff_gts"),
                                     writes=("ff_yac",),
                                     tag=f"ff_ev{tt}{name}_{tg}_{dh}")

            for ts in range(8):
                t1 = t0 + ts * 128
                rxb, krx = rx[ts % 2], f"ff_rx{ts % 2}"
                self.dma("sp", rxb[:], self.xres[t1:t1 + 128, :],
                         (krx,), (krx,), tag=f"ff_rld{tt}_{ts}")
                rnb, krn = rn[ts % 2], f"ff_rn{ts % 2}"
                if l == 0:
                    p.op("dve", lambda e, ts=ts, rxb=rxb, rnb=rnb: e.tensor_add(
                        rnb[:], yac[:, ts, :], rxb[:]),
                         reads=("ff_yac", krx, krn), writes=(krn,),
                         tag=f"ff_mg{tt}_{ts}")
                else:
                    ubb, kub = ub[ts % 2], f"ff_ub{ts % 2}"
                    self.dma("sp", ubb[:], self.xln[t1:t1 + 128, :],
                             (kub,), (kub,), tag=f"ff_uld{tt}_{ts}")
                    p.op("dve", lambda e, ts=ts, rxb=rxb, rnb=rnb: e.tensor_add(
                        rnb[:], yac[:, ts, :], rxb[:]),
                         reads=("ff_yac", krx, krn), writes=(krn,),
                         tag=f"ff_mg{tt}_{ts}")
                    p.op("dve", lambda e, rnb=rnb, ubb=ubb: e.tensor_add(
                        rnb[:], rnb[:], ubb[:]),
                         reads=(krn, kub), writes=(krn,),
                         tag=f"ff_mgu{tt}_{ts}")
                if l == 0:
                    self.dma("sp", self.xres[t1:t1 + 128, :], rnb[:],
                             (krn,), (f"ff_rst{tt}_{ts}",), tag=f"ff_rs{tt}_{ts}")
                    xob, kxo = xo[ts % 2], f"ff_xo{ts % 2}"
                    self.ln_tile_ops("ffL", tt * 8 + ts, rnb, krn,
                                     stats[:, (tt * 8 + ts) * 8:(tt * 8 + ts) * 8 + 8],
                                     xob, kxo, 0)
                    self.dma("sp", self.xln[t1:t1 + 128, :], xob[:],
                             (kxo,), (f"ff_xst{tt}_{ts}",), tag=f"ff_xs{tt}_{ts}")
                else:
                    self.dma("sp", self.out[t1:t1 + 128, :], rnb[:],
                             (krn,), (f"ff_ost{tt}_{ts}",), tag=f"ff_os{tt}_{ts}")

    def build(self, phases):
        self.declare_io()
        if "prepass" in phases:
            self.prepass()
        self.consts()
        self.p.barrier()
        if "ln1" in phases:
            self.ln1()
            self.p.barrier()
        if "attn0" in phases:
            self.attn(0)
            self.p.barrier()
        if "proj0" in phases:
            self.proj(0)
            self.p.barrier()
        if "ffn0" in phases:
            self.ffn(0)
            self.p.barrier()
        if "attn1" in phases:
            self.attn(1)
            self.p.barrier()
        if "proj1" in phases:
            self.proj(1)
            self.p.barrier()
        if "ffn1" in phases:
            self.ffn(1)
            self.p.barrier()
        self.p.finalize(self.nc, self.sems)
        return self.nc


ALL_PHASES = ("prepass", "ln1", "attn0", "proj0", "ffn0", "attn1", "proj1",
              "ffn1")


_LAST_MK = []


def build_nc(phases=ALL_PHASES, debug_dump=()):
    mk = MK(debug_dump=debug_dump)
    _LAST_MK.clear()
    _LAST_MK.append(mk)
    mk.build(phases)
    return mk.nc


def kernel(**inputs):
    nc = build_nc()
    xs = np.ascontiguousarray(inputs["L0_attn_x"])
    base = {k: np.ascontiguousarray(v) for k, v in inputs.items()
            if k != "L0_attn_x"}
    in_maps = []
    for c in range(8):
        m = dict(base)
        m["x"] = np.ascontiguousarray(xs[c])
        in_maps.append(m)
    res = run_bass_kernel_spmd(nc, in_maps, core_ids=list(range(8)))
    return np.stack([res.results[c]["out"] for c in range(8)], axis=0)
